# version 15
# baseline (speedup 1.0000x reference)
import math
import contextlib
import numpy as np
import ml_dtypes
import concourse.bass as bass
import concourse.mybir as mybir
from concourse.bass_utils import run_bass_kernel_spmd

F32 = mybir.dt.float32
BF16 = mybir.dt.bfloat16
AF = mybir.ActivationFunctionType
ALU = mybir.AluOpType
AX = mybir.AxisListType

N_CORES = 8


class Cfg:
    def __init__(self, S=2, T=2048, D=2048, H=8192, NHEADS=16):
        self.S = S
        self.T = T
        self.D = D
        self.H = H
        self.NHEADS = NHEADS
        self.DH = D // NHEADS
        assert self.DH == 128
        self.NK = D // 128
        self.NH = H // 128
        self.A = H // 128
        self.TOKS = S * T
        self.NT = self.TOKS // 512
        self.NTT = self.TOKS // 128
        self.NTC = T // 128
        self.NQT = T // 512
        self.QW = min(512, D)
        self.NQ = D // self.QW
        self.SCALE = 1.0 / self.DH


def _hadamard(n):
    h = np.array([[1.0]], dtype=np.float64)
    while h.shape[0] < n:
        h = np.block([[h, h], [h, -h]])
    return h


class RotGuard:

    def __init__(self, n):
        self.n = n
        self.req = [None] * n

    def acquire(self, eng, i):
        r = self.req[i % self.n]
        if r is not None:
            eng.wait_ge(r[0], r[1])

    def release(self, i, sem, val):
        self.req[i % self.n] = (sem, val)


STORE_SEMS = (["stx0", "stx1", "stp0", "stp1", "stz0", "stz1"]
              + ["sth%d" % i for i in range(4)] + ["stq%d" % i for i in range(4)])


class Builder:
    def __init__(self, cfg: Cfg):
        self.cfg = cfg
        nc = bass.Bass("TRN2", target_bir_lowering=False)
        self.nc = nc
        c = cfg
        self.x0 = nc.dram_tensor("x0", [c.TOKS, c.D], F32, kind="ExternalInput")
        self.W1T = nc.dram_tensor("W1T", [c.D, c.H], F32, kind="ExternalInput")
        self.W3T = nc.dram_tensor("W3T", [c.D, c.H], F32, kind="ExternalInput")
        self.W2T = nc.dram_tensor("W2T", [c.H, c.D], F32, kind="ExternalInput")
        self.Wqkv = nc.dram_tensor("Wqkv", [c.D, 3 * c.D], F32, kind="ExternalInput")
        self.Wout = nc.dram_tensor("Wout", [c.D, c.D], F32, kind="ExternalInput")
        self.bqkv = nc.dram_tensor("bqkv", [3 * c.D], F32, kind="ExternalInput")
        self.bout = nc.dram_tensor("bout", [c.D], F32, kind="ExternalInput")
        self.out = nc.dram_tensor("out", [c.TOKS, c.D], F32, kind="ExternalOutput")
        self.x1 = nc.dram_tensor("x1", [c.TOKS, c.D], F32, kind="Internal")
        self.x2 = nc.dram_tensor("x2", [c.TOKS, c.D], F32, kind="Internal")
        self.hidA = nc.dram_tensor("hidA", [c.NH, 128, c.TOKS], BF16, kind="Internal")
        self.hidB = nc.dram_tensor("hidB", [c.NH, 128, c.TOKS], BF16, kind="Internal")
        self.z_s = nc.dram_tensor("z_s", [128, c.A, c.D], BF16, kind="Internal")
        self.W2fT = nc.dram_tensor("W2fT", [c.H, c.D], BF16, kind="Internal")
        self.Q_s = nc.dram_tensor("Q_s", [c.S, c.NHEADS, 128, c.T], BF16, kind="Internal")
        self.K_s = nc.dram_tensor("K_s", [c.S, c.NHEADS, 128, c.T], BF16, kind="Internal")
        self.V_s = nc.dram_tensor("V_s", [c.S, c.NHEADS, 128, c.T], BF16, kind="Internal")
        h128 = _hadamard(128).astype(ml_dtypes.bfloat16)
        hA = _hadamard(c.A).astype(ml_dtypes.bfloat16)
        self.cH128 = nc.inline_tensor(np.ascontiguousarray(h128), "cH128")
        self.cHA = nc.inline_tensor(np.ascontiguousarray(hA), "cHA")
        self.cId = nc.inline_tensor(np.eye(128, dtype=ml_dtypes.bfloat16), "cId")
        self.cIdF = nc.inline_tensor(np.eye(128, dtype=np.float32), "cIdF")
        masks = np.stack([(np.arange(512)[None, :] >= (np.arange(128)[:, None] + 128 * j))
                          for j in range(4)]).astype(ml_dtypes.bfloat16)
        self.cMask = nc.inline_tensor(np.ascontiguousarray(masks), "cMask")
        self.sem = {}
        self.cnt = {}
        for name in (["mm", "mmt", "act", "dve", "fv", "fa", "mq",
                      "xl0", "xl1", "wl0", "wl1", "w2q", "hl0", "hl1",
                      "atl0", "atl1", "wo", "c0", "zl0", "zl1", "bb"]
                     + ["w2p%d" % i for i in range(8)]
                     + ["wo%d" % i for i in range(4)] + STORE_SEMS):
            self.sem[name] = nc.alloc_semaphore("s_" + name)
            self.cnt[name] = 0

    def nm(self, base):
        self._uid = getattr(self, "_uid", 0) + 1
        return "%s_%d" % (base, self._uid)

    def inc(self, ins, name, k=1):
        ins.then_inc(self.sem[name], k)
        self.cnt[name] += k
        return self.cnt[name]

    def wge(self, eng, name, val):
        if val > 0:
            eng.wait_ge(self.sem[name], val)

    def dma(self, eng, out_ap, in_ap, name):
        ins = eng.dma_start(out_ap, in_ap)
        return self.inc(ins, name, 16)

    def quiesce(self, *engines):
        self.quiesce_at(dict(self.cnt), *engines)

    def quiesce_at(self, snap, *engines):
        for eng in engines:
            for name in snap:
                self.wge(eng, name, snap[name])

    def setup_consts(self, ctx):
        nc, c = self.nc, self.cfg
        self.sb_h128 = ctx.enter_context(nc.sbuf_tensor(self.nm("sbH128"), [128, 128], BF16))
        self.sb_hA = ctx.enter_context(nc.sbuf_tensor(self.nm("sbHA"), [c.A, c.A], BF16))
        self.sb_id = ctx.enter_context(nc.sbuf_tensor(self.nm("sbId"), [128, 128], BF16))
        self.sb_idf = ctx.enter_context(nc.sbuf_tensor(self.nm("sbIdF"), [128, 128], F32))
        self.sb_bq = ctx.enter_context(nc.sbuf_tensor(self.nm("sbBq"), [128, 3 * c.NK], F32))
        self.sb_eps = ctx.enter_context(nc.sbuf_tensor(self.nm("sbEps"), [128, 1], F32))
        nc.vector.memset(self.sb_eps.ap(), 1e-5)
        self.dma(nc.sync, self.sb_h128.ap(), self.cH128.ap(), "c0")
        self.dma(nc.sync, self.sb_hA.ap(), self.cHA.ap(), "c0")
        self.dma(nc.sync, self.sb_id.ap(), self.cId.ap(), "c0")
        self.dma(nc.sync, self.sb_idf.ap(), self.cIdF.ap(), "c0")
        nb = 3 * c.NK
        with (nc.sbuf_tensor(self.nm("bq_t"), [nb, 128], F32) as bq_t,
              nc.psum_tensor(self.nm("bq_p"), [128, nb], F32) as bq_p):
            self.dma(nc.sync, bq_t.ap(),
                     self.bqkv.ap().rearrange("(c p) -> c p", p=128), "c0")
            self.wge(nc.tensor, "c0", self.cnt["c0"])
            ins = nc.tensor.transpose(bq_p.ap(), bq_t.ap(), self.sb_idf.ap()[0:nb, 0:nb])
            vmm = self.inc(ins, "mm")
            self.wge(nc.vector, "mm", vmm)
            ins = nc.vector.tensor_copy(self.sb_bq.ap(), bq_p.ap())
            self.inc(ins, "dve")
        self.bq_ready = self.cnt["dve"]

    def w2f_prepass(self):
        nc, c = self.nc, self.cfg
        nd = max(1, c.D // 512)
        dw = min(512, c.D)
        with (nc.sbuf_tensor(self.nm("p0_wf32"), [128, 3, c.D], F32) as wf32,
              nc.sbuf_tensor(self.nm("p0_w2c"), [128, 3, c.D], BF16) as w2c,
              nc.sbuf_tensor(self.nm("p0_zc"), [128, 2, c.D], BF16) as zc,
              nc.sbuf_tensor(self.nm("p0_zr"), [c.A, 2, 2, c.D], BF16) as zr,
              nc.sbuf_tensor(self.nm("p0_wf"), [c.A, 2, 2, c.D], BF16) as wf,
              nc.sbuf_tensor(self.nm("p0_xs"), [128, 2, c.D], F32) as xs0,
              nc.sbuf_tensor(self.nm("p0_xq"), [128, c.D], F32) as xq0,
              nc.psum_tensor(self.nm("p0_ps0"), [128, nd * dw], F32) as ps0,
              nc.psum_tensor(self.nm("p0_ps1"), [128, nd * dw], F32) as ps1):
            ps = [ps0, ps1]
            self.quiesce(nc.gpsimd, nc.sync, nc.scalar)
            self.wge(nc.tensor, "c0", self.cnt["c0"])
            self.wge(nc.tensor, "dve", self.cnt["dve"])
            gW = RotGuard(3)
            gC = RotGuard(3)
            gZ = RotGuard(2)
            gP = RotGuard(2)
            gX0 = RotGuard(2)
            x0ld = {}

            def pre_ln_load(tt):
                par = tt % 2
                gX0.acquire(nc.scalar, tt)
                x0ld[tt] = self.dma(nc.scalar, xs0.ap()[:, par],
                                    self.x0.ap()[tt * 128:(tt + 1) * 128, :],
                                    "xl%d" % par)

            def pre_ln_stats(tt):
                par = tt % 2
                self.wge(nc.vector, "xl%d" % par, x0ld[tt])
                ins = nc.vector.reduce_sum(self.ln_acc.ap()[:, tt, 0:1],
                                           xs0.ap()[:, par], AX.X)
                self.inc(ins, "fv")
                ins = nc.vector.scalar_tensor_tensor(
                    xq0.ap(), xs0.ap()[:, par], 1.0, xs0.ap()[:, par],
                    ALU.mult, ALU.mult,
                    accum_out=self.ln_acc.ap()[:, tt, 4:5])
                vfv = self.inc(ins, "fv")
                gX0.release(tt, self.sem["fv"], vfv)
                self.stats_ready[tt] = (self.cnt["dve"], vfv)

            for a in range(c.A):
                par = a % 2
                p4 = a % 3
                if a % 2 == 0 and a // 2 < c.NTT:
                    pre_ln_load(a // 2)
                elif a // 2 < c.NTT:
                    pre_ln_stats(a // 2)
                gW.acquire(nc.sync, a)
                vld = self.dma(nc.sync, wf32.ap()[:, p4],
                               self.W2T.ap()[a * 128:(a + 1) * 128, :], "wl%d" % par)
                self.wge(nc.vector, "wl%d" % par, vld)
                gC.acquire(nc.vector, a)
                ins = nc.vector.tensor_copy(w2c.ap()[:, p4], wf32.ap()[:, p4])
                vcast = self.inc(ins, "dve")
                gW.release(a, self.sem["dve"], vcast)
                self.wge(nc.tensor, "dve", vcast)
                gP.acquire(nc.tensor, a)
                for n in range(nd):
                    ins = nc.tensor.matmul(ps[a % 2].ap()[:, n * dw:(n + 1) * dw],
                                           self.sb_h128.ap(),
                                           w2c.ap()[:, p4, n * dw:(n + 1) * dw],
                                           start=True, stop=True)
                vmm = self.inc(ins, "mm")
                gC.release(a, self.sem["mm"], vmm)
                self.wge(nc.scalar, "mm", vmm)
                gZ.acquire(nc.scalar, a)
                ins = nc.scalar.copy(zc.ap()[:, par], ps[a % 2].ap())
                vact = self.inc(ins, "act")
                gP.release(a, self.sem["act"], vact)
                self.wge(nc.gpsimd, "act", vact)
                vst = self.dma(nc.gpsimd, self.z_s.ap()[:, a, :], zc.ap()[:, par],
                               "stp%d" % par)
                gZ.release(a, self.sem["stp%d" % par], vst)
            scl = float(1.0 / math.sqrt(c.H))
            stA0, stA1 = self.cnt["stp0"], self.cnt["stp1"]
            gWf = [RotGuard(2), RotGuard(2)]
            gP2 = [RotGuard(2), RotGuard(2)]
            w2f_view = self.W2fT.ap().rearrange("(a p) d -> a p d", p=128)
            actA = self.cnt["act"]
            gZr = RotGuard(2)
            for bp in range(64):
                par = bp % 2
                p4 = bp % 2
                if bp == 0:
                    self.wge(nc.sync, "stp0", stA0)
                    self.wge(nc.sync, "stp1", stA1)
                    self.wge(nc.tensor, "act", actA)
                gZr.acquire(nc.sync, bp)
                vld = self.dma(nc.sync, zr.ap()[:, p4],
                               self.z_s.ap()[2 * bp:2 * bp + 2, :, :]
                               .rearrange("b a d -> a b d"), "zl%d" % par)
                self.wge(nc.tensor, "zl%d" % par, vld)
                vmmj = [0, 0]
                for j in range(2):
                    gP2[j].acquire(nc.tensor, bp)
                    for n in range(nd):
                        ins = nc.tensor.matmul(
                            ps[j].ap()[0:c.A, n * dw:(n + 1) * dw], self.sb_hA.ap(),
                            zr.ap()[:, p4, j, n * dw:(n + 1) * dw],
                            start=True, stop=True)
                    vmmj[j] = self.inc(ins, "mm")
                gZr.release(bp, self.sem["mm"], vmmj[1])
                self.wge(nc.vector, "mm", vmmj[0])
                gWf[0].acquire(nc.vector, bp)
                ins = nc.vector.tensor_scalar_mul(wf.ap()[:, par, 0],
                                                  ps[0].ap()[0:c.A, :], scl)
                vd0 = self.inc(ins, "dve")
                gP2[0].release(bp, self.sem["dve"], vd0)
                self.wge(nc.scalar, "mm", vmmj[1])
                gWf[1].acquire(nc.scalar, bp)
                ins = nc.scalar.activation(wf.ap()[:, par, 1], ps[1].ap()[0:c.A, :],
                                           AF.Copy, scale=scl)
                va1 = self.inc(ins, "act")
                gP2[1].release(bp, self.sem["act"], va1)
                self.wge(nc.gpsimd, "dve", vd0)
                self.wge(nc.gpsimd, "act", va1)
                vst = self.dma(nc.gpsimd, w2f_view[:, 2 * bp:2 * bp + 2, :],
                               wf.ap()[:, par], "stz%d" % par)
                gWf[0].release(bp, self.sem["stz%d" % par], vst)
                gWf[1].release(bp, self.sem["stz%d" % par], vst)

    def ln_state(self, ctx, x_in, tok0, tfm, W, pre=False):
        nc, c = self.nc, self.cfg
        b = self

        class LNState:

            def __init__(s):
                s.xs = ctx.enter_context(nc.sbuf_tensor(b.nm("ln_xs"), [128, 2, c.D], F32))
                s.xc = ctx.enter_context(nc.sbuf_tensor(b.nm("ln_xc"), [128, 2, c.D], F32))
                s.xq = ctx.enter_context(nc.sbuf_tensor(b.nm("ln_xq"), [128, c.D], F32))
                s.ttm = ctx.enter_context(nc.sbuf_tensor(b.nm("ln_tt"), [128, 2, c.D], BF16))
                s.st8 = ctx.enter_context(nc.sbuf_tensor(b.nm("ln_st"), [128, 2, 8], F32))
                s.tp = [ctx.enter_context(nc.psum_tensor(b.nm("ln_tp%d" % i), [128, 512], BF16))
                        for i in range(2)]
                s.gX = RotGuard(2)
                s.gT = RotGuard(2)
                s.gP = RotGuard(2)
                s.ttm_done = {}
                s.trans_act = {}
                s.last_stats = -1

            def stats_pre(s, tt):
                par = tt % 2
                gt = tok0 // 128 + tt
                s.gX.acquire(nc.sync, tt)
                r0 = tok0 + tt * 128
                vld = b.dma(nc.sync, s.xs.ap()[:, par], x_in.ap()[r0:r0 + 128, :],
                            "xl%d" % par)
                rdy = b.stats_ready[gt]
                b.wge(nc.vector, "dve", rdy[0])
                b.wge(nc.vector, "fv", rdy[1])
                ins = nc.vector.reduce_sum(s.st8.ap()[:, par, 0:1],
                                           b.ln_acc.ap()[:, gt, 0:4], AX.X)
                b.inc(ins, "fv")
                ins = nc.vector.reduce_sum(s.st8.ap()[:, par, 5:6],
                                           b.ln_acc.ap()[:, gt, 4:8], AX.X)
                b.inc(ins, "fv")
                b.wge(nc.vector, "fv", b.cnt["fv"])
                ins = nc.vector.tensor_scalar_mul(s.st8.ap()[:, par, 1:2],
                                                  s.st8.ap()[:, par, 0:1], 1.0 / c.D)
                b.inc(ins, "fv")
                b.wge(nc.vector, "fv", b.cnt["fv"])
                ins = nc.vector.tensor_mul(s.st8.ap()[:, par, 6:7],
                                           s.st8.ap()[:, par, 0:1],
                                           s.st8.ap()[:, par, 1:2])
                b.inc(ins, "fv")
                b.wge(nc.vector, "fv", b.cnt["fv"])
                ins = nc.vector.tensor_sub(s.st8.ap()[:, par, 2:3],
                                           s.st8.ap()[:, par, 5:6],
                                           s.st8.ap()[:, par, 6:7])
                vfv = b.inc(ins, "fv")
                b.wge(nc.scalar, "fv", vfv)
                ins = nc.scalar.activation(s.st8.ap()[:, par, 3:4],
                                           s.st8.ap()[:, par, 2:3], AF.Ln,
                                           bias=b.sb_eps.ap(), scale=1.0 / c.D)
                b.inc(ins, "fa")
                b.wge(nc.scalar, "fa", b.cnt["fa"])
                ins = nc.scalar.activation(s.st8.ap()[:, par, 4:5],
                                           s.st8.ap()[:, par, 3:4], AF.Exp,
                                           scale=-0.5)
                vact = b.inc(ins, "act")
                b.wge(nc.vector, "act", vact)
                b.wge(nc.vector, "xl%d" % par, vld)
                s.gT.acquire(nc.vector, tt)
                ins = nc.vector.tensor_scalar(s.ttm.ap()[:, par], s.xs.ap()[:, par],
                                              s.st8.ap()[:, par, 1:2],
                                              s.st8.ap()[:, par, 4:5],
                                              ALU.subtract, ALU.mult)
                vdve = b.inc(ins, "dve")
                s.gX.release(tt, b.sem["dve"], vdve)
                s.ttm_done[tt] = vdve

            def stats(s, tt):
                if pre:
                    return s.stats_pre(tt)
                par = tt % 2
                s.gX.acquire(nc.sync, tt)
                r0 = tok0 + tt * 128
                vld = b.dma(nc.sync, s.xs.ap()[:, par], x_in.ap()[r0:r0 + 128, :],
                            "xl%d" % par)
                b.wge(nc.vector, "xl%d" % par, vld)
                ins = nc.vector.reduce_sum(s.st8.ap()[:, par, 0:1], s.xs.ap()[:, par],
                                           AX.X)
                b.inc(ins, "fv")
                b.wge(nc.vector, "fv", b.cnt["fv"])
                ins = nc.vector.tensor_scalar_mul(s.st8.ap()[:, par, 1:2],
                                                  s.st8.ap()[:, par, 0:1], 1.0 / c.D)
                b.inc(ins, "fv")
                b.wge(nc.vector, "fv", b.cnt["fv"])
                ins = nc.vector.tensor_scalar(s.xc.ap()[:, par], s.xs.ap()[:, par],
                                              s.st8.ap()[:, par, 1:2], None,
                                              ALU.subtract)
                b.inc(ins, "fv")
                b.wge(nc.vector, "fv", b.cnt["fv"])
                ins = nc.vector.scalar_tensor_tensor(s.xq.ap(), s.xc.ap()[:, par], 1.0,
                                                     s.xc.ap()[:, par],
                                                     ALU.mult, ALU.mult,
                                                     accum_out=s.st8.ap()[:, par, 2:3])
                vdve = b.inc(ins, "dve")
                s.gX.release(tt, b.sem["dve"], vdve)
                b.wge(nc.scalar, "dve", vdve)
                ins = nc.scalar.activation(s.st8.ap()[:, par, 3:4],
                                           s.st8.ap()[:, par, 2:3], AF.Ln,
                                           bias=b.sb_eps.ap(), scale=1.0 / c.D)
                b.inc(ins, "fa")
                b.wge(nc.scalar, "fa", b.cnt["fa"])
                ins = nc.scalar.activation(s.st8.ap()[:, par, 4:5],
                                           s.st8.ap()[:, par, 3:4], AF.Exp,
                                           scale=-0.5)
                vact = b.inc(ins, "act")
                b.wge(nc.vector, "act", vact)
                s.gT.acquire(nc.vector, tt)
                ins = nc.vector.tensor_scalar(s.ttm.ap()[:, par], s.xc.ap()[:, par],
                                              s.st8.ap()[:, par, 4:5], None,
                                              ALU.mult)
                vdve = b.inc(ins, "dve")
                s.ttm_done[tt] = vdve

            def trans(s, tt):
                par = tt % 2
                b.wge(nc.tensor, "dve", s.ttm_done[tt])
                tfm3 = tfm.ap().rearrange("p (k w) -> p k w", w=W)
                for jb in range(c.NK // 4):
                    ib = tt * (c.NK // 4) + jb
                    s.gP.acquire(nc.tensor, ib)
                    for j in range(4):
                        kc = jb * 4 + j
                        ins = nc.tensor.transpose(
                            s.tp[ib % 2].ap()[:, j * 128:(j + 1) * 128],
                            s.ttm.ap()[:, par, kc * 128:(kc + 1) * 128],
                            b.sb_id.ap())
                    vmm = b.inc(ins, "mmt")
                    b.wge(nc.scalar, "mmt", vmm)
                    ins = nc.scalar.copy(
                        tfm3[:, jb * 4:(jb + 1) * 4, tt * 128:tt * 128 + 128],
                        s.tp[ib % 2].ap().rearrange("p (k w) -> p k w", w=128))
                    va = b.inc(ins, "act")
                    s.gP.release(ib, b.sem["act"], va)
                s.gT.release(tt, b.sem["mmt"], b.cnt["mmt"])
                s.trans_act[tt] = b.cnt["act"]

            def tile(s, tt):
                s.stats(tt)
                if tt > 0:
                    s.trans(tt - 1)
                s.last_stats = tt

            def flush(s):
                s.trans(s.last_stats)
        return LNState()

    def ffn_up(self, x_in, hid, pre=False):
        nc, c = self.nc, self.cfg
        NT = c.NT
        ln_per_t = 512 // 128
        snap = dict(self.cnt)
        with contextlib.ExitStack() as ctx:
            tfm = ctx.enter_context(nc.sbuf_tensor(self.nm("f_tfm"), [128, c.NK * c.TOKS], BF16))
            w1c = ctx.enter_context(nc.sbuf_tensor(self.nm("f_w1c"), [128, 2, c.NK, 128], BF16))
            w3c = ctx.enter_context(nc.sbuf_tensor(self.nm("f_w3c"), [128, 2, c.NK, 128], BF16))
            gsb = ctx.enter_context(nc.sbuf_tensor(self.nm("f_gsb"), [128, 4, 512], F32))
            hsb = ctx.enter_context(nc.sbuf_tensor(self.nm("f_hsb"), [128, 4, 512], BF16))
            self.quiesce_at(snap, nc.sync, nc.gpsimd, nc.tensor)
            lns = self.ln_state(ctx, x_in, 0, tfm, c.TOKS, pre=pre)
            pg = [ctx.enter_context(nc.psum_tensor(self.nm("f_pg%d" % i), [128, 512], F32))
                  for i in range(3)]
            pl = [ctx.enter_context(nc.psum_tensor(self.nm("f_pl%d" % i), [128, 512], F32))
                  for i in range(3)]
            b_mm = self.cnt["mm"]
            b_wl = (self.cnt["wl0"], self.cnt["wl1"])
            gW = RotGuard(2)
            gPG = RotGuard(3)
            gPL = RotGuard(3)
            gGS = RotGuard(4)
            gH = RotGuard(4)
            for ch in range(c.NH):
                bpar = ch % 2
                gW.acquire(nc.gpsimd, ch)
                for wsb, wdr in ((w1c, self.W1T), (w3c, self.W3T)):
                    self.dma(nc.gpsimd, wsb.ap()[:, bpar],
                             wdr.ap()[:, ch * 128:(ch + 1) * 128]
                             .rearrange("(kc p) m -> p kc m", p=128), "wl%d" % bpar)
                gW.release(ch, self.sem["mm"], b_mm + (ch + 1) * 2 * NT)
            act_after_t = {}

            def emit_tiles(tw):
                for tt in range(tw * ln_per_t, (tw + 1) * ln_per_t):
                    lns.tile(tt)

            emit_tiles(0)
            if NT > 1:
                emit_tiles(1)
            for ch in range(c.NH):
                bpar = ch % 2
                self.wge(nc.tensor, "wl%d" % bpar, b_wl[bpar] + 32 * (ch // 2 + 1))
                for t in range(NT):
                    gi = ch * NT + t
                    par = gi % 4
                    pp = gi % 3
                    if ch == 0:
                        self.wge(nc.tensor, "act",
                                 lns.trans_act[(t + 1) * ln_per_t - 1])
                    gPG.acquire(nc.tensor, gi)
                    for k in range(c.NK):
                        ins = nc.tensor.matmul(
                            pg[pp].ap(), w1c.ap()[:, bpar, k],
                            tfm.ap()[:, k * c.TOKS + t * 512: k * c.TOKS + t * 512 + 512],
                            start=(k == 0), stop=(k == c.NK - 1))
                    vmm_g = self.inc(ins, "mm")
                    gPL.acquire(nc.tensor, gi)
                    for k in range(c.NK):
                        ins = nc.tensor.matmul(
                            pl[pp].ap(), w3c.ap()[:, bpar, k],
                            tfm.ap()[:, k * c.TOKS + t * 512: k * c.TOKS + t * 512 + 512],
                            start=(k == 0), stop=(k == c.NK - 1))
                    vmm_l = self.inc(ins, "mm")
                    self.wge(nc.scalar, "mm", vmm_g)
                    gGS.acquire(nc.scalar, gi)
                    ins = nc.scalar.activation(gsb.ap()[:, par], pg[pp].ap(), AF.Silu)
                    vact = self.inc(ins, "act")
                    gPG.release(gi, self.sem["act"], vact)
                    self.wge(nc.vector, "mm", vmm_l)
                    self.wge(nc.vector, "act", vact)
                    gH.acquire(nc.vector, gi)
                    ins = nc.vector.tensor_tensor(hsb.ap()[:, par], pl[pp].ap(),
                                                  gsb.ap()[:, par], ALU.mult)
                    vdve = self.inc(ins, "dve")
                    gPL.release(gi, self.sem["dve"], vdve)
                    gGS.release(gi, self.sem["dve"], vdve)
                    self.wge(nc.sync, "dve", vdve)
                    vst = self.dma(nc.sync, hid.ap()[ch, :, t * 512:(t + 1) * 512],
                                   hsb.ap()[:, par], "sth%d" % par)
                    gH.release(gi, self.sem["sth%d" % par], vst)
                    if ch == 0 and t + 2 < NT:
                        emit_tiles(t + 2)
                        if t + 2 == NT - 1:
                            lns.flush()

    def ffn_down(self, x_in, x_out, hid, collect=False):
        nc, c = self.nc, self.cfg
        NQ, QW, NT = c.NQ, c.QW, c.NT
        kgrp = max(1, c.NH // 8)
        nwq = (c.NH + kgrp - 1) // kgrp
        nspd = (c.NH + kgrp - 1) // kgrp
        NG = NQ * NT * 4
        snap = dict(self.cnt)
        spans = []
        li = 0
        prev_tw = None
        for q in range(NQ):
            order = range(NT) if q % 2 == 0 else range(NT - 1, -1, -1)
            for j, tw in enumerate(order):
                if j == 0 and tw == prev_tw and q > 0:
                    spans.append((q, tw, None))
                else:
                    spans.append((q, tw, li))
                    li += 1
                prev_tw = tw
        nloads = li
        groups = []
        for si, (q, tw, _l) in enumerate(spans):
            for ts in range(4):
                groups.append((q, tw, ts, si))
        span_buf = {}
        bufp = {}
        nxt = 0
        for si, (q, tw, l) in enumerate(spans):
            if l is None:
                span_buf[si] = span_buf[si - 1]
            else:
                span_buf[si] = l % 2
        def mm_after_span(si):
            return (si + 1) * 4
        with contextlib.ExitStack() as ctx:
            w2q = ctx.enter_context(nc.sbuf_tensor(self.nm("d_w2q"), [128, c.NH, QW], BF16))
            hsp = ctx.enter_context(nc.sbuf_tensor(self.nm("d_hsp"), [128, 2, c.NH, 512], BF16))
            xisb = ctx.enter_context(nc.sbuf_tensor(self.nm("d_xi"), [128, 2, QW], F32))
            xosb = ctx.enter_context(nc.sbuf_tensor(self.nm("d_xo"), [128, 2, QW], F32))
            sqs = (ctx.enter_context(nc.sbuf_tensor(self.nm("d_sq"), [128, QW], F32))
                   if collect else None)
            pd = [ctx.enter_context(nc.psum_tensor(self.nm("d_pd%d" % i), [128, QW], F32))
                  for i in range(4)]
            b_mm = self.cnt["mm"]
            b_mq = self.cnt["mq"]
            b_dve = self.cnt["dve"]
            b_w2p = [self.cnt["w2p%d" % i] for i in range(8)]
            b_hl = (self.cnt["hl0"], self.cnt["hl1"])
            b_xl = (self.cnt["xl0"], self.cnt["xl1"])
            b_stx = (self.cnt["stx0"], self.cnt["stx1"])
            sth_tot = [self.cnt["sth%d" % i] for i in range(4)]
            self.quiesce_at(snap, nc.sync)
            for i in range(4):
                self.wge(nc.sync, "sth%d" % i, sth_tot[i])
            g2 = 0
            for si, (q, tw, l) in enumerate(spans):
                if tw == (0 if q % 2 == 0 else NT - 1) or (si > 0 and spans[si - 1][0] != q):
                    pass
                if si == 0 or spans[si - 1][0] != q:
                    for pi, k0 in enumerate(range(0, c.NH, kgrp)):
                        if q >= 1:
                            if pi < nwq - 1:
                                self.wge(nc.sync, "mq",
                                         b_mq + (q - 1) * (nwq - 1) + pi + 1)
                            else:
                                self.wge(nc.sync, "mm", b_mm + q * NT * 4)
                        self.dma(nc.sync, w2q.ap()[:, k0:k0 + kgrp, :],
                                 self.W2fT.ap()[:, q * QW:(q + 1) * QW]
                                 .rearrange("(kc p) m -> p kc m", p=128)[:, k0:k0 + kgrp, :],
                                 "w2p%d" % (pi % 8))
                if l is not None:
                    par = l % 2
                    if l >= 2:
                        prev_si = [s for s, sp in enumerate(spans)
                                   if sp[2] == l - 2][0]
                        last_use = prev_si
                        if prev_si + 1 < len(spans) and spans[prev_si + 1][2] is None:
                            last_use = prev_si + 1
                        self.wge(nc.sync, "mm", b_mm + mm_after_span(last_use))
                    for k0 in range(0, c.NH, kgrp):
                        self.dma(nc.sync, hsp.ap()[:, par, k0:k0 + kgrp, :],
                                 hid.ap()[k0:k0 + kgrp, :, tw * 512:(tw + 1) * 512]
                                 .rearrange("kc p m -> p kc m"), "hl%d" % par)
                for ts in range(4):
                    p2 = g2 % 2
                    if g2 >= 2:
                        self.wge(nc.sync, "dve", b_dve + g2 - 1)
                    r0 = tw * 512 + ts * 128
                    self.dma(nc.sync, xisb.ap()[:, p2],
                             x_in.ap()[r0:r0 + 128, q * QW:(q + 1) * QW],
                             "xl%d" % p2)
                    g2 += 1
            self.quiesce_at(snap, nc.tensor)
            g2 = 0
            hl_seen = [0, 0]
            for si, (q, tw, l) in enumerate(spans):
                first_of_q = (si == 0 or spans[si - 1][0] != q)
                if l is not None:
                    par = l % 2
                    hl_seen[par] += nspd * 16
                    self.wge(nc.tensor, "hl%d" % par, b_hl[par] + hl_seen[par])
                else:
                    par = span_buf[si]
                last_of_q = (si + 1 == len(spans) or spans[si + 1][0] != q)
                for ts in range(4):
                    p4 = g2 % 4
                    if g2 >= 4:
                        self.wge(nc.tensor, "dve", b_dve + g2 - 3)
                    for kc in range(c.NH):
                        if first_of_q and ts == 0 and kc % kgrp == 0:
                            pi = kc // kgrp
                            self.wge(nc.tensor, "w2p%d" % (pi % 8),
                                     b_w2p[pi % 8] + (q + 1) * 16)
                        ins = nc.tensor.matmul(
                            pd[p4].ap(),
                            hsp.ap()[:, par, kc, ts * 128:(ts + 1) * 128],
                            w2q.ap()[:, kc, :],
                            start=(kc == 0), stop=(kc == c.NH - 1))
                        if (last_of_q and ts == 3 and kc % kgrp == kgrp - 1
                                and kc < c.NH - 1):
                            self.inc(ins, "mq")
                    self.inc(ins, "mm")
                    g2 += 1
            def emit_sq(g2s):
                q, tw, ts, _si = groups[g2s]
                gtile = tw * 4 + ts
                p2s = g2s % 2
                ins = nc.vector.scalar_tensor_tensor(
                    sqs.ap(), xosb.ap()[:, p2s], 1.0, xosb.ap()[:, p2s],
                    ALU.mult, ALU.mult,
                    accum_out=self.ln_acc.ap()[:, gtile, 4 + q:5 + q])
                self.inc(ins, "fv")
                self.stats_ready[gtile] = (self.cnt["dve"], self.cnt["fv"])

            for g2 in range(NG):
                p2 = g2 % 2
                q, tw, ts, _si = groups[g2]
                gtile = tw * 4 + ts
                self.wge(nc.vector, "mm", b_mm + g2 + 1)
                self.wge(nc.vector, "xl%d" % p2, b_xl[p2] + (g2 // 2 + 1) * 16)
                if g2 >= 2:
                    self.wge(nc.vector, "stx%d" % p2, b_stx[p2] + 16 * (g2 // 2))
                ins = nc.vector.scalar_tensor_tensor(
                    xosb.ap()[:, p2], pd[g2 % 4].ap(), 0.5, xisb.ap()[:, p2],
                    ALU.mult, ALU.add,
                    accum_out=(self.ln_acc.ap()[:, gtile, q:q + 1]
                               if collect else None))
                self.inc(ins, "dve")
                if collect and g2 >= 1:
                    emit_sq(g2 - 1)
            if collect:
                emit_sq(NG - 1)
            for g2 in range(NG):
                p2 = g2 % 2
                q, tw, ts, _si = groups[g2]
                r0 = tw * 512 + ts * 128
                self.wge(nc.scalar, "dve", b_dve + g2 + 1)
                self.dma(nc.scalar, x_out.ap()[r0:r0 + 128, q * QW:(q + 1) * QW],
                         xosb.ap()[:, p2], "stx%d" % p2)

    def qkv_pass(self, s):
        nc, c = self.nc, self.cfg
        TQ = c.T // 512
        NM = 3 * c.NK
        ln_per_t = 512 // 128
        dsts = [self.Q_s, self.K_s, self.V_s]
        snap = dict(self.cnt)
        with contextlib.ExitStack() as ctx:
            tfm = ctx.enter_context(nc.sbuf_tensor(self.nm("q_tfm"), [128, c.NK * c.T], BF16))
            wqc = ctx.enter_context(nc.sbuf_tensor(self.nm("q_wq"), [128, 2, c.NK, 128], BF16))
            qsb = ctx.enter_context(nc.sbuf_tensor(self.nm("q_qs"), [128, 4, 512], BF16))
            self.quiesce_at(snap, nc.sync, nc.gpsimd, nc.tensor)
            lns = self.ln_state(ctx, self.x1, s * c.T, tfm, c.T, pre=True)
            pq = [ctx.enter_context(nc.psum_tensor(self.nm("q_pq%d" % i), [128, 512], F32))
                  for i in range(3)]
            b_mm = self.cnt["mm"]
            b_wl = (self.cnt["wl0"], self.cnt["wl1"])
            gW = RotGuard(2)
            gP = RotGuard(3)
            gQ = RotGuard(4)
            for m in range(NM):
                bpar = m % 2
                gW.acquire(nc.gpsimd, m)
                self.dma(nc.gpsimd, wqc.ap()[:, bpar],
                         self.Wqkv.ap()[:, m * 128:(m + 1) * 128]
                         .rearrange("(kc p) m -> p kc m", p=128), "wl%d" % bpar)
                gW.release(m, self.sem["mm"], b_mm + (m + 1) * TQ)
            def emit_tiles(tw):
                for tt in range(tw * ln_per_t, (tw + 1) * ln_per_t):
                    lns.tile(tt)

            emit_tiles(0)
            if TQ > 1:
                emit_tiles(1)
            for m in range(NM):
                bpar = m % 2
                kind, h = divmod(m, c.NK)
                tot = (NM + 1) // 2 if bpar == 0 else NM // 2
                self.wge(nc.tensor, "wl%d" % bpar, b_wl[bpar] + 16 * (m // 2 + 1))
                for tq in range(TQ):
                    gi = m * TQ + tq
                    par = gi % 4
                    pp = gi % 3
                    if m == 0:
                        self.wge(nc.tensor, "act",
                                 lns.trans_act[(tq + 1) * ln_per_t - 1])
                    gP.acquire(nc.tensor, gi)
                    for k in range(c.NK):
                        ins = nc.tensor.matmul(
                            pq[pp].ap(), wqc.ap()[:, bpar, k],
                            tfm.ap()[:, k * c.T + tq * 512: k * c.T + tq * 512 + 512],
                            start=(k == 0), stop=(k == c.NK - 1))
                    vmm = self.inc(ins, "mm")
                    self.wge(nc.scalar, "mm", vmm)
                    if gi == 0:
                        self.wge(nc.scalar, "dve", self.bq_ready)
                    gQ.acquire(nc.scalar, gi)
                    ins = nc.scalar.add(qsb.ap()[:, par], pq[pp].ap(),
                                        self.sb_bq.ap()[:, m:m + 1])
                    vact = self.inc(ins, "act")
                    gP.release(gi, self.sem["act"], vact)
                    self.wge(nc.sync, "act", vact)
                    vst = self.dma(nc.sync,
                                   dsts[kind].ap()[s, h % c.NHEADS, :, tq * 512:(tq + 1) * 512],
                                   qsb.ap()[:, par], "stq%d" % par)
                    gQ.release(gi, self.sem["stq%d" % par], vst)
                    if m == 0 and tq + 2 < TQ:
                        emit_tiles(tq + 2)
                        if tq + 2 == TQ - 1:
                            lns.flush()

    def attn_pass(self, s, o_tm, masks, o_guard):
        nc, c = self.nc, self.cfg
        NTC, NQT = c.NTC, c.NQT
        with contextlib.ExitStack() as ctx:
            khb = ctx.enter_context(nc.sbuf_tensor(self.nm("a_kh"), [128, 2, c.T], BF16))
            vhb = ctx.enter_context(nc.sbuf_tensor(self.nm("a_vh"), [128, 2, c.T], BF16))
            qhb = ctx.enter_context(nc.sbuf_tensor(self.nm("a_qh"), [128, 2, c.T], BF16))
            vtm = ctx.enter_context(nc.sbuf_tensor(self.nm("a_vtm"), [128, 2, NTC, 129], BF16))
            pt = ctx.enter_context(nc.sbuf_tensor(self.nm("a_pt"), [128, 2, NTC, 512], BF16))
            rden = ctx.enter_context(nc.sbuf_tensor(self.nm("a_rd"), [128, 3], F32))
            sp = [ctx.enter_context(nc.psum_tensor(self.nm("a_sp%d" % i), [128, 1024], F32))
                  for i in range(2)]
            op = [ctx.enter_context(nc.psum_tensor(self.nm("a_op%d" % i), [128, 129], F32))
                  for i in range(3)]
            vp = [ctx.enter_context(nc.psum_tensor(self.nm("a_vp%d" % i), [128, 512], BF16))
                  for i in range(1)]
            self.quiesce(nc.sync, nc.tensor, nc.vector)
            for par in range(2):
                ins = nc.vector.memset(vtm.ap()[:, par, :, 128:129], 1.0)
                self.inc(ins, "dve")
            ones_dve = self.cnt["dve"]
            stq_tot = [self.cnt["stq%d" % i] for i in range(4)]
            gL = RotGuard(2)
            gVP = RotGuard(1)
            gSP = RotGuard(2)
            gOP = RotGuard(3)
            sc_n = 0
            st = {"iop": 0, "first": True}
            pending = []

            def emit_attv():
                if not pending:
                    return
                hh, hpar, ptp, qt, qs, exp_w, msk_w, vtm_w = pending.pop(0)
                qg = qt * 4 + qs
                iop = st["iop"]
                st["iop"] += 1
                self.wge(nc.tensor, "act", exp_w)
                self.wge(nc.tensor, "dve", max(msk_w, vtm_w))
                gOP.acquire(nc.tensor, iop)
                for kc in range(qg + 1):
                    ins = nc.tensor.matmul(
                        op[iop % 3].ap(),
                        pt.ap()[:, ptp, kc, qs * 128:(qs + 1) * 128],
                        vtm.ap()[:, hpar, kc, :],
                        start=(kc == 0), stop=(kc == qg))
                vmm = self.inc(ins, "mm")
                self.wge(nc.vector, "mm", vmm)
                ins = nc.vector.reciprocal(rden.ap()[:, iop % 3:iop % 3 + 1],
                                           op[iop % 3].ap()[:, 128:129])
                self.inc(ins, "fv")
                self.wge(nc.vector, "fv", self.cnt["fv"])
                if st["first"] and o_guard is not None:
                    self.wge(nc.vector, "mm", o_guard)
                st["first"] = False
                ins = nc.vector.tensor_scalar(
                    o_tm.ap()[:, qg * c.D + hh * 128: qg * c.D + hh * 128 + 128],
                    op[iop % 3].ap()[:, 0:128],
                    rden.ap()[:, iop % 3:iop % 3 + 1], None, ALU.mult)
                vd = self.inc(ins, "dve")
                gOP.release(iop, self.sem["dve"], vd)

            for h in range(c.NHEADS):
                par = h % 2
                if h == 0:
                    for i in range(4):
                        self.wge(nc.sync, "stq%d" % i, stq_tot[i])
                gL.acquire(nc.sync, h)
                for buf, src in ((khb, self.K_s), (vhb, self.V_s), (qhb, self.Q_s)):
                    self.dma(nc.sync, buf.ap()[:, par], src.ap()[s, h, :, :],
                             "atl%d" % par)
                self.wge(nc.tensor, "atl%d" % par, self.cnt["atl%d" % par])
                self.wge(nc.tensor, "dve", ones_dve)
                for jb in range(NTC // 4):
                    ib = h * (NTC // 4) + jb
                    gVP.acquire(nc.tensor, ib)
                    for j in range(4):
                        kc = jb * 4 + j
                        ins = nc.tensor.transpose(
                            vp[0].ap()[:, j * 128:(j + 1) * 128],
                            vhb.ap()[:, par, kc * 128:(kc + 1) * 128],
                            self.sb_id.ap())
                    vmm = self.inc(ins, "mm")
                    self.wge(nc.vector, "mm", vmm)
                    ins = nc.vector.tensor_copy(
                        vtm.ap()[:, par, jb * 4:(jb + 1) * 4, 0:128],
                        vp[0].ap().rearrange("p (k w) -> p k w", w=128))
                    vd = self.inc(ins, "dve")
                    gVP.release(ib, self.sem["dve"], vd)
                    emit_attv()
                vtm_ready = self.cnt["dve"]
                for qt in range(NQT):
                    ptpar = (h * NQT + qt) % 2
                    kmax = 4 * qt + 4
                    exp_last = 0
                    msk_last = 0
                    for kc0 in range(0, kmax, 2):
                        gSP.acquire(nc.tensor, sc_n)
                        for kh in range(2):
                            kc = kc0 + kh
                            ins = nc.tensor.matmul(
                                sp[sc_n % 2].ap()[:, kh * 512:(kh + 1) * 512],
                                khb.ap()[:, par, kc * 128:(kc + 1) * 128],
                                qhb.ap()[:, par, qt * 512:(qt + 1) * 512],
                                start=True, stop=True)
                        vmm = self.inc(ins, "mm")
                        self.wge(nc.scalar, "mm", vmm)
                        ins = nc.scalar.activation(
                            pt.ap()[:, ptpar, kc0:kc0 + 2, :],
                            sp[sc_n % 2].ap(), AF.Exp, scale=c.SCALE)
                        exp_last = self.inc(ins, "act")
                        gSP.release(sc_n, self.sem["act"], exp_last)
                        for kh in range(2):
                            kc = kc0 + kh
                            if kc >= 4 * qt:
                                self.wge(nc.vector, "act", exp_last)
                                ins = nc.vector.tensor_mul(pt.ap()[:, ptpar, kc, :],
                                                           pt.ap()[:, ptpar, kc, :],
                                                           masks.ap()[:, kc - 4 * qt, :])
                                msk_last = self.inc(ins, "dve")
                        sc_n += 1
                        emit_attv()
                    for qs in range(4):
                        pending.append((h, par, ptpar, qt, qs,
                                        exp_last, msk_last, vtm_ready))
                gL.release(h, self.sem["mm"], self.cnt["mm"])
            while pending:
                emit_attv()

    def outproj_pass(self, s, o_tm, bout_bc):
        nc, c = self.nc, self.cfg
        NTC = c.NTC
        NN = c.D // c.QW
        with contextlib.ExitStack() as ctx:
            woc = ctx.enter_context(nc.sbuf_tensor(self.nm("o_wo"), [128, c.NK, c.D], BF16))
            ofm = ctx.enter_context(nc.sbuf_tensor(self.nm("o_ofm"), [128, 2, c.NK, 128], BF16))
            xisb = ctx.enter_context(nc.sbuf_tensor(self.nm("o_xi"), [128, 2, c.D], F32))
            xosb = ctx.enter_context(nc.sbuf_tensor(self.nm("o_xo"), [128, 2, c.D], F32))
            sqs = ctx.enter_context(nc.sbuf_tensor(self.nm("o_sq"), [128, c.QW], F32))
            tp = [ctx.enter_context(nc.psum_tensor(self.nm("o_tp%d" % i), [128, 512], BF16))
                  for i in range(2)]
            pd = [ctx.enter_context(nc.psum_tensor(self.nm("o_pd%d" % i), [128, c.QW], F32))
                  for i in range(2)]
            self.quiesce(nc.gpsimd, nc.sync, nc.tensor, nc.scalar)
            kg = max(1, c.NK // 4)
            b_wo = [self.cnt["wo%d" % i] for i in range(4)]
            for pi, k0 in enumerate(range(0, c.NK, kg)):
                self.dma(nc.gpsimd, woc.ap()[:, k0:k0 + kg, :],
                         self.Wout.ap()
                         .rearrange("(kc p) n -> p kc n", p=128)[:, k0:k0 + kg, :],
                         "wo%d" % (pi % 4))
            gTP = RotGuard(2)
            gPD = RotGuard(2)
            gXI = RotGuard(2)
            gXO = RotGuard(2)
            wo_seen = [False] * 4
            for tc in range(NTC):
                par = tc % 2
                gXI.acquire(nc.sync, tc)
                r0 = s * c.T + tc * 128
                vxi = self.dma(nc.sync, xisb.ap()[:, par], self.x1.ap()[r0:r0 + 128, :],
                               "xl%d" % par)
                for jb in range(c.NK // 4):
                    ib = tc * (c.NK // 4) + jb
                    gTP.acquire(nc.tensor, ib)
                    for j in range(4):
                        kc = jb * 4 + j
                        ins = nc.tensor.transpose(
                            tp[ib % 2].ap()[:, j * 128:(j + 1) * 128],
                            o_tm.ap()[:, tc * c.D + kc * 128: tc * c.D + kc * 128 + 128],
                            self.sb_id.ap())
                    vmm = self.inc(ins, "mm")
                    self.wge(nc.vector, "mm", vmm)
                    ins = nc.vector.tensor_copy(
                        ofm.ap()[:, par, jb * 4:(jb + 1) * 4, :],
                        tp[ib % 2].ap().rearrange("p (k w) -> p k w", w=128))
                    vd = self.inc(ins, "dve")
                    gTP.release(ib, self.sem["dve"], vd)
                self.wge(nc.tensor, "dve", self.cnt["dve"])
                self.wge(nc.vector, "xl%d" % par, vxi)
                gXO.acquire(nc.vector, tc)
                ins = nc.vector.tensor_add(xosb.ap()[:, par], xisb.ap()[:, par],
                                           bout_bc.ap())
                self.inc(ins, "fv")
                self.wge(nc.vector, "fv", self.cnt["fv"])
                for n in range(NN):
                    ip = tc * NN + n
                    gPD.acquire(nc.tensor, ip)
                    for kc in range(c.NK):
                        pi = min(3, kc // kg)
                        if tc == 0 and not wo_seen[pi]:
                            self.wge(nc.tensor, "wo%d" % pi, b_wo[pi] + 16)
                            wo_seen[pi] = True
                        ins = nc.tensor.matmul(
                            pd[ip % 2].ap(), ofm.ap()[:, par, kc],
                            woc.ap()[:, kc, n * c.QW:(n + 1) * c.QW],
                            start=(kc == 0), stop=(kc == c.NK - 1))
                    vmm = self.inc(ins, "mm")
                    self.wge(nc.vector, "mm", vmm)
                    gt = s * (c.T // 128) + tc
                    ins = nc.vector.scalar_tensor_tensor(
                        xosb.ap()[:, par, n * c.QW:(n + 1) * c.QW],
                        pd[ip % 2].ap(), 1.0,
                        xosb.ap()[:, par, n * c.QW:(n + 1) * c.QW], ALU.mult, ALU.add,
                        accum_out=self.ln_acc.ap()[:, gt, n:n + 1])
                    vd = self.inc(ins, "dve")
                    gPD.release(ip, self.sem["dve"], vd)
                    ins = nc.vector.scalar_tensor_tensor(
                        sqs.ap(),
                        xosb.ap()[:, par, n * c.QW:(n + 1) * c.QW], 1.0,
                        xosb.ap()[:, par, n * c.QW:(n + 1) * c.QW],
                        ALU.mult, ALU.mult,
                        accum_out=self.ln_acc.ap()[:, gt, 4 + n:5 + n])
                    self.inc(ins, "fv")
                    self.stats_ready[gt] = (self.cnt["dve"], self.cnt["fv"])
                gXI.release(tc, self.sem["dve"], self.cnt["dve"])
                self.wge(nc.scalar, "dve", self.cnt["dve"])
                self.dma(nc.scalar, self.x2.ap()[r0:r0 + 128, :],
                         xosb.ap()[:, par], "stx%d" % par)
                gXO.release(tc, self.sem["stx%d" % par], self.cnt["stx%d" % par])

    def emit(self):
        nc, c = self.nc, self.cfg
        with contextlib.ExitStack() as topctx:
            self.setup_consts(topctx)
            self.ln_acc = topctx.enter_context(
                nc.sbuf_tensor(self.nm("lnacc"), [128, c.NTT, 8], F32))
            ins = nc.vector.memset(self.ln_acc.ap(), 0.0)
            self.inc(ins, "dve")
            self.stats_ready = {}
            self.w2f_prepass()
            self.ffn_up(self.x0, self.hidA, pre=True)
            self.ffn_down(self.x0, self.x1, self.hidA, collect=True)
            with contextlib.ExitStack() as actx:
                o_tm = actx.enter_context(
                    nc.sbuf_tensor(self.nm("at_otm"), [128, c.NTC * c.D], BF16))
                masks = actx.enter_context(
                    nc.sbuf_tensor(self.nm("at_msk"), [128, 4, 512], BF16))
                bout_bc = actx.enter_context(
                    nc.sbuf_tensor(self.nm("at_bo"), [128, c.D], F32))
                self.quiesce(nc.sync)
                for j in range(4):
                    self.dma(nc.sync, masks.ap()[:, j, :], self.cMask.ap()[j], "bb")
                self.dma(nc.sync, bout_bc.ap(),
                         bass.AP(self.bout.ap().tensor, 0, [[0, 128], [1, c.D]]), "bb")
                bb_done = self.cnt["bb"]
                self.wge(nc.vector, "bb", bb_done)
                self.wge(nc.tensor, "bb", bb_done)
                o_guard = None
                for s in range(c.S):
                    self.qkv_pass(s)
                    self.attn_pass(s, o_tm, masks, o_guard)
                    self.outproj_pass(s, o_tm, bout_bc)
                    o_guard = self.cnt["mm"]
            self.ffn_up(self.x2, self.hidB, pre=True)
            self.ffn_down(self.x2, self.out, self.hidB)
            for name in STORE_SEMS:
                self.wge(nc.sync, name, self.cnt[name])
        return self.nc


def build(cfg: Cfg):
    return Builder(cfg).emit()


def shard_inputs(cfg, L0_attn_x, L0_attn_W_qkv, L0_attn_b_qkv, L0_attn_W_out,
                 L0_attn_b_out, L0_ffn_W1, L0_ffn_W2, L0_ffn_W3, n_cores=N_CORES):
    x = np.ascontiguousarray(np.asarray(L0_attn_x, dtype=np.float32))
    w = {
        "W1T": np.ascontiguousarray(np.asarray(L0_ffn_W1, dtype=np.float32).T),
        "W3T": np.ascontiguousarray(np.asarray(L0_ffn_W3, dtype=np.float32).T),
        "W2T": np.ascontiguousarray(np.asarray(L0_ffn_W2, dtype=np.float32).T),
        "Wqkv": np.ascontiguousarray(np.asarray(L0_attn_W_qkv, dtype=np.float32)),
        "Wout": np.ascontiguousarray(np.asarray(L0_attn_W_out, dtype=np.float32)),
        "bqkv": np.ascontiguousarray(np.asarray(L0_attn_b_qkv, dtype=np.float32)),
        "bout": np.ascontiguousarray(np.asarray(L0_attn_b_out, dtype=np.float32)),
    }
    in_maps = []
    for core in range(n_cores):
        xs = np.ascontiguousarray(
            x[cfg.S * core: cfg.S * (core + 1)].reshape(cfg.TOKS, cfg.D))
        m = {"x0": xs}
        m.update(w)
        in_maps.append(m)
    return in_maps


def kernel(L0_attn_x, L0_attn_W_qkv, L0_attn_b_qkv, L0_attn_W_out, L0_attn_b_out,
           L0_ffn_W1, L0_ffn_W2, L0_ffn_W3):
    cfg = Cfg(S=2, T=2048, D=2048, H=8192, NHEADS=16)
    nc = build(cfg)
    in_maps = shard_inputs(cfg, L0_attn_x, L0_attn_W_qkv, L0_attn_b_qkv,
                           L0_attn_W_out, L0_attn_b_out, L0_ffn_W1, L0_ffn_W2,
                           L0_ffn_W3)
    res = run_bass_kernel_spmd(nc, in_maps, core_ids=list(range(N_CORES)))
    out = np.concatenate([np.asarray(r["out"]).reshape(cfg.S, cfg.T, cfg.D)
                          for r in res.results], axis=0)
    return out


# version 17
# speedup vs baseline: 1.0391x; 1.0391x over previous
import math
import contextlib
import numpy as np
import ml_dtypes
import concourse.bass as bass
import concourse.mybir as mybir
from concourse.bass_utils import run_bass_kernel_spmd

F32 = mybir.dt.float32
F32R = mybir.dt.float32r
BF16 = mybir.dt.bfloat16
AF = mybir.ActivationFunctionType
ALU = mybir.AluOpType
AX = mybir.AxisListType

N_CORES = 8


class Cfg:
    def __init__(self, S=2, T=2048, D=2048, H=8192, NHEADS=16):
        self.S = S
        self.T = T
        self.D = D
        self.H = H
        self.NHEADS = NHEADS
        self.DH = D // NHEADS
        assert self.DH == 128
        self.NK = D // 128
        self.NH = H // 128
        self.A = H // 128
        self.TOKS = S * T
        self.NT = self.TOKS // 512
        self.NTT = self.TOKS // 128
        self.NTC = T // 128
        self.NQT = T // 512
        self.QW = min(512, D)
        self.NQ = D // self.QW
        self.SCALE = 1.0 / self.DH


def _hadamard(n):
    h = np.array([[1.0]], dtype=np.float64)
    while h.shape[0] < n:
        h = np.block([[h, h], [h, -h]])
    return h


class RotGuard:

    def __init__(self, n):
        self.n = n
        self.req = [None] * n

    def acquire(self, eng, i):
        r = self.req[i % self.n]
        if r is not None:
            eng.wait_ge(r[0], r[1])

    def release(self, i, sem, val):
        self.req[i % self.n] = (sem, val)


STORE_SEMS = (["stx0", "stx1", "stp0", "stp1", "stz0", "stz1"]
              + ["sth%d" % i for i in range(4)] + ["stq%d" % i for i in range(4)])


class Builder:
    def __init__(self, cfg: Cfg):
        self.cfg = cfg
        nc = bass.Bass("TRN2", target_bir_lowering=False)
        self.nc = nc
        c = cfg
        self.x0 = nc.dram_tensor("x0", [c.TOKS, c.D], F32, kind="ExternalInput")
        self.W1T = nc.dram_tensor("W1T", [c.D, c.H], F32, kind="ExternalInput")
        self.W3T = nc.dram_tensor("W3T", [c.D, c.H], F32, kind="ExternalInput")
        self.W2T = nc.dram_tensor("W2T", [c.H, c.D], F32R, kind="ExternalInput")
        self.H128f = nc.dram_tensor("H128f", [128, 128], F32R, kind="ExternalInput")
        self.Wqkv = nc.dram_tensor("Wqkv", [c.D, 3 * c.D], F32, kind="ExternalInput")
        self.Wout = nc.dram_tensor("Wout", [c.D, c.D], F32, kind="ExternalInput")
        self.bqkv = nc.dram_tensor("bqkv", [3 * c.D], F32, kind="ExternalInput")
        self.bout = nc.dram_tensor("bout", [c.D], F32, kind="ExternalInput")
        self.out = nc.dram_tensor("out", [c.TOKS, c.D], F32, kind="ExternalOutput")
        self.x1 = nc.dram_tensor("x1", [c.TOKS, c.D], F32, kind="Internal")
        self.x2 = nc.dram_tensor("x2", [c.TOKS, c.D], F32, kind="Internal")
        self.hidA = nc.dram_tensor("hidA", [c.NH, 128, c.TOKS], BF16, kind="Internal")
        self.hidB = nc.dram_tensor("hidB", [c.NH, 128, c.TOKS], BF16, kind="Internal")
        self.z_s = nc.dram_tensor("z_s", [128, c.A, c.D], BF16, kind="Internal")
        self.W2fT = nc.dram_tensor("W2fT", [c.H, c.D], BF16, kind="Internal")
        self.Q_s = nc.dram_tensor("Q_s", [c.S, c.NHEADS, 128, c.T], BF16, kind="Internal")
        self.K_s = nc.dram_tensor("K_s", [c.S, c.NHEADS, 128, c.T], BF16, kind="Internal")
        self.V_s = nc.dram_tensor("V_s", [c.S, c.NHEADS, 128, c.T], BF16, kind="Internal")
        h128 = _hadamard(128).astype(ml_dtypes.bfloat16)
        hA = _hadamard(c.A).astype(ml_dtypes.bfloat16)
        self.cH128 = nc.inline_tensor(np.ascontiguousarray(h128), "cH128")
        self.cHA = nc.inline_tensor(np.ascontiguousarray(hA), "cHA")
        self.cId = nc.inline_tensor(np.eye(128, dtype=ml_dtypes.bfloat16), "cId")
        self.cIdF = nc.inline_tensor(np.eye(128, dtype=np.float32), "cIdF")
        masks = np.stack([(np.arange(512)[None, :] >= (np.arange(128)[:, None] + 128 * j))
                          for j in range(4)]).astype(ml_dtypes.bfloat16)
        self.cMask = nc.inline_tensor(np.ascontiguousarray(masks), "cMask")
        self.sem = {}
        self.cnt = {}
        for name in (["mm", "mmt", "act", "dve", "fv", "fa", "mq",
                      "xl0", "xl1", "wl0", "wl1", "w2q", "hl0", "hl1",
                      "atl0", "atl1", "wo", "c0", "zl0", "zl1", "bb"]
                     + ["w2p%d" % i for i in range(8)]
                     + ["wo%d" % i for i in range(4)] + STORE_SEMS):
            self.sem[name] = nc.alloc_semaphore("s_" + name)
            self.cnt[name] = 0

    def nm(self, base):
        self._uid = getattr(self, "_uid", 0) + 1
        return "%s_%d" % (base, self._uid)

    def inc(self, ins, name, k=1):
        ins.then_inc(self.sem[name], k)
        self.cnt[name] += k
        return self.cnt[name]

    def wge(self, eng, name, val):
        if val > 0:
            eng.wait_ge(self.sem[name], val)

    def dma(self, eng, out_ap, in_ap, name):
        ins = eng.dma_start(out_ap, in_ap)
        return self.inc(ins, name, 16)

    def quiesce(self, *engines):
        self.quiesce_at(dict(self.cnt), *engines)

    def quiesce_at(self, snap, *engines):
        for eng in engines:
            for name in snap:
                self.wge(eng, name, snap[name])

    def setup_consts(self, ctx):
        nc, c = self.nc, self.cfg
        self.sb_h128r = ctx.enter_context(nc.sbuf_tensor(self.nm("sbH128r"), [128, 128], F32R))
        self.sb_hA = ctx.enter_context(nc.sbuf_tensor(self.nm("sbHA"), [c.A, c.A], BF16))
        self.sb_id = ctx.enter_context(nc.sbuf_tensor(self.nm("sbId"), [128, 128], BF16))
        self.sb_idf = ctx.enter_context(nc.sbuf_tensor(self.nm("sbIdF"), [128, 128], F32))
        self.sb_bq = ctx.enter_context(nc.sbuf_tensor(self.nm("sbBq"), [128, 3 * c.NK], F32))
        self.sb_eps = ctx.enter_context(nc.sbuf_tensor(self.nm("sbEps"), [128, 1], F32))
        nc.vector.memset(self.sb_eps.ap(), 1e-5)
        self.dma(nc.sync, self.sb_h128r.ap(), self.H128f.ap(), "c0")
        self.dma(nc.sync, self.sb_hA.ap(), self.cHA.ap(), "c0")
        self.dma(nc.sync, self.sb_id.ap(), self.cId.ap(), "c0")
        self.dma(nc.sync, self.sb_idf.ap(), self.cIdF.ap(), "c0")
        nb = 3 * c.NK
        with (nc.sbuf_tensor(self.nm("bq_t"), [nb, 128], F32) as bq_t,
              nc.psum_tensor(self.nm("bq_p"), [128, nb], F32) as bq_p):
            self.dma(nc.sync, bq_t.ap(),
                     self.bqkv.ap().rearrange("(c p) -> c p", p=128), "c0")
            self.wge(nc.tensor, "c0", self.cnt["c0"])
            ins = nc.tensor.transpose(bq_p.ap(), bq_t.ap(), self.sb_idf.ap()[0:nb, 0:nb])
            vmm = self.inc(ins, "mm")
            self.wge(nc.vector, "mm", vmm)
            ins = nc.vector.tensor_copy(self.sb_bq.ap(), bq_p.ap())
            self.inc(ins, "dve")
        self.bq_ready = self.cnt["dve"]

    def w2f_prepass(self):
        nc, c = self.nc, self.cfg
        nd = max(1, c.D // 512)
        dw = min(512, c.D)
        with (nc.sbuf_tensor(self.nm("p0_wf32"), [128, 3, c.D], F32R) as wf32,
              nc.sbuf_tensor(self.nm("p0_zc"), [128, 4, c.D], BF16) as zc,
              nc.sbuf_tensor(self.nm("p0_zr"), [c.A, 2, 2, c.D], BF16) as zr,
              nc.sbuf_tensor(self.nm("p0_wf"), [c.A, 2, 2, c.D], BF16) as wf,
              nc.sbuf_tensor(self.nm("p0_xs"), [128, 4, c.D], F32) as xs0,
              nc.sbuf_tensor(self.nm("p0_xq"), [128, c.D], F32) as xq0,
              nc.psum_tensor(self.nm("p0_ps0"), [128, nd * dw], F32) as ps0,
              nc.psum_tensor(self.nm("p0_ps1"), [128, nd * dw], F32) as ps1):
            ps = [ps0, ps1]
            self.quiesce(nc.gpsimd, nc.sync, nc.scalar)
            self.wge(nc.tensor, "c0", self.cnt["c0"])
            self.wge(nc.tensor, "dve", self.cnt["dve"])
            gW = RotGuard(3)
            gZ = RotGuard(4)
            gP = RotGuard(2)
            gX0 = RotGuard(4)
            x0ld = {}

            def pre_ln_load(tt):
                par = tt % 2
                gX0.acquire(nc.sync, tt)
                x0ld[tt] = self.dma(nc.sync, xs0.ap()[:, tt % 4],
                                    self.x0.ap()[tt * 128:(tt + 1) * 128, :],
                                    "xl%d" % par)

            def pre_ln_stats(tt):
                par = tt % 2
                self.wge(nc.vector, "xl%d" % par, x0ld[tt])
                ins = nc.vector.reduce_sum(self.ln_acc.ap()[:, tt, 0:1],
                                           xs0.ap()[:, tt % 4], AX.X)
                self.inc(ins, "fv")
                ins = nc.vector.scalar_tensor_tensor(
                    xq0.ap(), xs0.ap()[:, tt % 4], 1.0, xs0.ap()[:, tt % 4],
                    ALU.mult, ALU.mult,
                    accum_out=self.ln_acc.ap()[:, tt, 4:5])
                vfv = self.inc(ins, "fv")
                gX0.release(tt, self.sem["fv"], vfv)
                self.stats_ready[tt] = (self.cnt["dve"], vfv)

            for a in range(c.A):
                par = a % 2
                p4 = a % 3
                if a % 2 == 0 and a // 2 < c.NTT:
                    pre_ln_load(a // 2)
                elif a // 2 < c.NTT:
                    pre_ln_stats(a // 2)
                gW.acquire(nc.sync, a)
                vld = self.dma(nc.sync, wf32.ap()[:, p4],
                               self.W2T.ap()[a * 128:(a + 1) * 128, :], "wl%d" % par)
                self.wge(nc.tensor, "wl%d" % par, vld)
                gP.acquire(nc.tensor, a)
                for n in range(nd):
                    ins = nc.tensor.matmul(ps[a % 2].ap()[:, n * dw:(n + 1) * dw],
                                           self.sb_h128r.ap(),
                                           wf32.ap()[:, p4, n * dw:(n + 1) * dw],
                                           start=True, stop=True)
                vmm = self.inc(ins, "mm")
                gW.release(a, self.sem["mm"], vmm)
                self.wge(nc.scalar, "mm", vmm)
                gZ.acquire(nc.scalar, a)
                ins = nc.scalar.copy(zc.ap()[:, a % 4], ps[a % 2].ap())
                vact = self.inc(ins, "act")
                gP.release(a, self.sem["act"], vact)
                self.wge(nc.gpsimd, "act", vact)
                vst = self.dma(nc.gpsimd, self.z_s.ap()[:, a, :], zc.ap()[:, a % 4],
                               "stp%d" % par)
                gZ.release(a, self.sem["stp%d" % par], vst)
            scl = float(1.0 / math.sqrt(c.H))
            stA0, stA1 = self.cnt["stp0"], self.cnt["stp1"]
            gWf = [RotGuard(2), RotGuard(2)]
            gP2 = [RotGuard(2), RotGuard(2)]
            w2f_view = self.W2fT.ap().rearrange("(a p) d -> a p d", p=128)
            actA = self.cnt["act"]
            gZr = RotGuard(2)
            for bp in range(64):
                par = bp % 2
                p4 = bp % 2
                if bp == 0:
                    self.wge(nc.sync, "stp0", stA0)
                    self.wge(nc.sync, "stp1", stA1)
                    self.wge(nc.tensor, "act", actA)
                gZr.acquire(nc.sync, bp)
                vld = self.dma(nc.sync, zr.ap()[:, p4],
                               self.z_s.ap()[2 * bp:2 * bp + 2, :, :]
                               .rearrange("b a d -> a b d"), "zl%d" % par)
                self.wge(nc.tensor, "zl%d" % par, vld)
                vmmj = [0, 0]
                for j in range(2):
                    gP2[j].acquire(nc.tensor, bp)
                    for n in range(nd):
                        ins = nc.tensor.matmul(
                            ps[j].ap()[0:c.A, n * dw:(n + 1) * dw], self.sb_hA.ap(),
                            zr.ap()[:, p4, j, n * dw:(n + 1) * dw],
                            start=True, stop=True)
                    vmmj[j] = self.inc(ins, "mm")
                gZr.release(bp, self.sem["mm"], vmmj[1])
                self.wge(nc.vector, "mm", vmmj[0])
                gWf[0].acquire(nc.vector, bp)
                ins = nc.vector.tensor_scalar_mul(wf.ap()[:, par, 0],
                                                  ps[0].ap()[0:c.A, :], scl)
                vd0 = self.inc(ins, "dve")
                gP2[0].release(bp, self.sem["dve"], vd0)
                self.wge(nc.scalar, "mm", vmmj[1])
                gWf[1].acquire(nc.scalar, bp)
                ins = nc.scalar.activation(wf.ap()[:, par, 1], ps[1].ap()[0:c.A, :],
                                           AF.Copy, scale=scl)
                va1 = self.inc(ins, "act")
                gP2[1].release(bp, self.sem["act"], va1)
                self.wge(nc.gpsimd, "dve", vd0)
                self.wge(nc.gpsimd, "act", va1)
                vst = self.dma(nc.gpsimd, w2f_view[:, 2 * bp:2 * bp + 2, :],
                               wf.ap()[:, par], "stz%d" % par)
                gWf[0].release(bp, self.sem["stz%d" % par], vst)
                gWf[1].release(bp, self.sem["stz%d" % par], vst)

    def ln_state(self, ctx, x_in, tok0, tfm, W, pre=False):
        nc, c = self.nc, self.cfg
        b = self

        class LNState:

            def __init__(s):
                s.xs = ctx.enter_context(nc.sbuf_tensor(b.nm("ln_xs"), [128, 2, c.D], F32))
                s.xc = ctx.enter_context(nc.sbuf_tensor(b.nm("ln_xc"), [128, 2, c.D], F32))
                s.xq = ctx.enter_context(nc.sbuf_tensor(b.nm("ln_xq"), [128, c.D], F32))
                s.ttm = ctx.enter_context(nc.sbuf_tensor(b.nm("ln_tt"), [128, 2, c.D], BF16))
                s.st8 = ctx.enter_context(nc.sbuf_tensor(b.nm("ln_st"), [128, 2, 8], F32))
                s.tp = [ctx.enter_context(nc.psum_tensor(b.nm("ln_tp%d" % i), [128, 512], BF16))
                        for i in range(2)]
                s.gX = RotGuard(2)
                s.gT = RotGuard(2)
                s.gP = RotGuard(2)
                s.ttm_done = {}
                s.trans_act = {}
                s.last_stats = -1

            def stats_pre(s, tt):
                par = tt % 2
                gt = tok0 // 128 + tt
                s.gX.acquire(nc.sync, tt)
                r0 = tok0 + tt * 128
                vld = b.dma(nc.sync, s.xs.ap()[:, par], x_in.ap()[r0:r0 + 128, :],
                            "xl%d" % par)
                rdy = b.stats_ready[gt]
                b.wge(nc.vector, "dve", rdy[0])
                b.wge(nc.vector, "fv", rdy[1])
                ins = nc.vector.reduce_sum(s.st8.ap()[:, par, 0:1],
                                           b.ln_acc.ap()[:, gt, 0:4], AX.X)
                b.inc(ins, "fv")
                ins = nc.vector.reduce_sum(s.st8.ap()[:, par, 5:6],
                                           b.ln_acc.ap()[:, gt, 4:8], AX.X)
                b.inc(ins, "fv")
                b.wge(nc.vector, "fv", b.cnt["fv"])
                ins = nc.vector.tensor_scalar_mul(s.st8.ap()[:, par, 1:2],
                                                  s.st8.ap()[:, par, 0:1], 1.0 / c.D)
                b.inc(ins, "fv")
                b.wge(nc.vector, "fv", b.cnt["fv"])
                ins = nc.vector.tensor_mul(s.st8.ap()[:, par, 6:7],
                                           s.st8.ap()[:, par, 0:1],
                                           s.st8.ap()[:, par, 1:2])
                b.inc(ins, "fv")
                b.wge(nc.vector, "fv", b.cnt["fv"])
                ins = nc.vector.tensor_sub(s.st8.ap()[:, par, 2:3],
                                           s.st8.ap()[:, par, 5:6],
                                           s.st8.ap()[:, par, 6:7])
                vfv = b.inc(ins, "fv")
                b.wge(nc.scalar, "fv", vfv)
                ins = nc.scalar.activation(s.st8.ap()[:, par, 3:4],
                                           s.st8.ap()[:, par, 2:3], AF.Ln,
                                           bias=b.sb_eps.ap(), scale=1.0 / c.D)
                b.inc(ins, "fa")
                b.wge(nc.scalar, "fa", b.cnt["fa"])
                ins = nc.scalar.activation(s.st8.ap()[:, par, 4:5],
                                           s.st8.ap()[:, par, 3:4], AF.Exp,
                                           scale=-0.5)
                vact = b.inc(ins, "act")
                b.wge(nc.vector, "act", vact)
                b.wge(nc.vector, "xl%d" % par, vld)
                s.gT.acquire(nc.vector, tt)
                ins = nc.vector.tensor_scalar(s.ttm.ap()[:, par], s.xs.ap()[:, par],
                                              s.st8.ap()[:, par, 1:2],
                                              s.st8.ap()[:, par, 4:5],
                                              ALU.subtract, ALU.mult)
                vdve = b.inc(ins, "dve")
                s.gX.release(tt, b.sem["dve"], vdve)
                s.ttm_done[tt] = vdve

            def stats(s, tt):
                if pre:
                    return s.stats_pre(tt)
                par = tt % 2
                s.gX.acquire(nc.sync, tt)
                r0 = tok0 + tt * 128
                vld = b.dma(nc.sync, s.xs.ap()[:, par], x_in.ap()[r0:r0 + 128, :],
                            "xl%d" % par)
                b.wge(nc.vector, "xl%d" % par, vld)
                ins = nc.vector.reduce_sum(s.st8.ap()[:, par, 0:1], s.xs.ap()[:, par],
                                           AX.X)
                b.inc(ins, "fv")
                b.wge(nc.vector, "fv", b.cnt["fv"])
                ins = nc.vector.tensor_scalar_mul(s.st8.ap()[:, par, 1:2],
                                                  s.st8.ap()[:, par, 0:1], 1.0 / c.D)
                b.inc(ins, "fv")
                b.wge(nc.vector, "fv", b.cnt["fv"])
                ins = nc.vector.tensor_scalar(s.xc.ap()[:, par], s.xs.ap()[:, par],
                                              s.st8.ap()[:, par, 1:2], None,
                                              ALU.subtract)
                b.inc(ins, "fv")
                b.wge(nc.vector, "fv", b.cnt["fv"])
                ins = nc.vector.scalar_tensor_tensor(s.xq.ap(), s.xc.ap()[:, par], 1.0,
                                                     s.xc.ap()[:, par],
                                                     ALU.mult, ALU.mult,
                                                     accum_out=s.st8.ap()[:, par, 2:3])
                vdve = b.inc(ins, "dve")
                s.gX.release(tt, b.sem["dve"], vdve)
                b.wge(nc.scalar, "dve", vdve)
                ins = nc.scalar.activation(s.st8.ap()[:, par, 3:4],
                                           s.st8.ap()[:, par, 2:3], AF.Ln,
                                           bias=b.sb_eps.ap(), scale=1.0 / c.D)
                b.inc(ins, "fa")
                b.wge(nc.scalar, "fa", b.cnt["fa"])
                ins = nc.scalar.activation(s.st8.ap()[:, par, 4:5],
                                           s.st8.ap()[:, par, 3:4], AF.Exp,
                                           scale=-0.5)
                vact = b.inc(ins, "act")
                b.wge(nc.vector, "act", vact)
                s.gT.acquire(nc.vector, tt)
                ins = nc.vector.tensor_scalar(s.ttm.ap()[:, par], s.xc.ap()[:, par],
                                              s.st8.ap()[:, par, 4:5], None,
                                              ALU.mult)
                vdve = b.inc(ins, "dve")
                s.ttm_done[tt] = vdve

            def trans(s, tt):
                par = tt % 2
                b.wge(nc.tensor, "dve", s.ttm_done[tt])
                tfm3 = tfm.ap().rearrange("p (k w) -> p k w", w=W)
                for jb in range(c.NK // 4):
                    ib = tt * (c.NK // 4) + jb
                    s.gP.acquire(nc.tensor, ib)
                    for j in range(4):
                        kc = jb * 4 + j
                        ins = nc.tensor.transpose(
                            s.tp[ib % 2].ap()[:, j * 128:(j + 1) * 128],
                            s.ttm.ap()[:, par, kc * 128:(kc + 1) * 128],
                            b.sb_id.ap())
                    vmm = b.inc(ins, "mmt")
                    b.wge(nc.scalar, "mmt", vmm)
                    ins = nc.scalar.copy(
                        tfm3[:, jb * 4:(jb + 1) * 4, tt * 128:tt * 128 + 128],
                        s.tp[ib % 2].ap().rearrange("p (k w) -> p k w", w=128))
                    va = b.inc(ins, "act")
                    s.gP.release(ib, b.sem["act"], va)
                s.gT.release(tt, b.sem["mmt"], b.cnt["mmt"])
                s.trans_act[tt] = b.cnt["act"]

            def tile(s, tt):
                s.stats(tt)
                if tt > 0:
                    s.trans(tt - 1)
                s.last_stats = tt

            def flush(s):
                s.trans(s.last_stats)
        return LNState()

    def ffn_up(self, x_in, hid, pre=False):
        nc, c = self.nc, self.cfg
        NT = c.NT
        ln_per_t = 512 // 128
        snap = dict(self.cnt)
        with contextlib.ExitStack() as ctx:
            tfm = ctx.enter_context(nc.sbuf_tensor(self.nm("f_tfm"), [128, c.NK * c.TOKS], BF16))
            w1c = ctx.enter_context(nc.sbuf_tensor(self.nm("f_w1c"), [128, 2, c.NK, 128], BF16))
            w3c = ctx.enter_context(nc.sbuf_tensor(self.nm("f_w3c"), [128, 2, c.NK, 128], BF16))
            gsb = ctx.enter_context(nc.sbuf_tensor(self.nm("f_gsb"), [128, 4, 512], F32))
            hsb = ctx.enter_context(nc.sbuf_tensor(self.nm("f_hsb"), [128, 4, 512], BF16))
            self.quiesce_at(snap, nc.sync, nc.gpsimd, nc.tensor)
            lns = self.ln_state(ctx, x_in, 0, tfm, c.TOKS, pre=pre)
            pg = [ctx.enter_context(nc.psum_tensor(self.nm("f_pg%d" % i), [128, 512], F32))
                  for i in range(3)]
            pl = [ctx.enter_context(nc.psum_tensor(self.nm("f_pl%d" % i), [128, 512], F32))
                  for i in range(3)]
            b_mm = self.cnt["mm"]
            b_wl = (self.cnt["wl0"], self.cnt["wl1"])
            gW = RotGuard(2)
            gPG = RotGuard(3)
            gPL = RotGuard(3)
            gGS = RotGuard(4)
            gH = RotGuard(4)
            for ch in range(c.NH):
                bpar = ch % 2
                gW.acquire(nc.gpsimd, ch)
                for wsb, wdr in ((w1c, self.W1T), (w3c, self.W3T)):
                    self.dma(nc.gpsimd, wsb.ap()[:, bpar],
                             wdr.ap()[:, ch * 128:(ch + 1) * 128]
                             .rearrange("(kc p) m -> p kc m", p=128), "wl%d" % bpar)
                gW.release(ch, self.sem["mm"], b_mm + (ch + 1) * 2 * NT)
            act_after_t = {}

            def emit_tiles(tw):
                for tt in range(tw * ln_per_t, (tw + 1) * ln_per_t):
                    lns.tile(tt)

            emit_tiles(0)
            if NT > 1:
                emit_tiles(1)
            for ch in range(c.NH):
                bpar = ch % 2
                self.wge(nc.tensor, "wl%d" % bpar, b_wl[bpar] + 32 * (ch // 2 + 1))
                for t in range(NT):
                    gi = ch * NT + t
                    par = gi % 4
                    pp = gi % 3
                    if ch == 0:
                        self.wge(nc.tensor, "act",
                                 lns.trans_act[(t + 1) * ln_per_t - 1])
                    gPG.acquire(nc.tensor, gi)
                    for k in range(c.NK):
                        ins = nc.tensor.matmul(
                            pg[pp].ap(), w1c.ap()[:, bpar, k],
                            tfm.ap()[:, k * c.TOKS + t * 512: k * c.TOKS + t * 512 + 512],
                            start=(k == 0), stop=(k == c.NK - 1))
                    vmm_g = self.inc(ins, "mm")
                    gPL.acquire(nc.tensor, gi)
                    for k in range(c.NK):
                        ins = nc.tensor.matmul(
                            pl[pp].ap(), w3c.ap()[:, bpar, k],
                            tfm.ap()[:, k * c.TOKS + t * 512: k * c.TOKS + t * 512 + 512],
                            start=(k == 0), stop=(k == c.NK - 1))
                    vmm_l = self.inc(ins, "mm")
                    self.wge(nc.scalar, "mm", vmm_g)
                    gGS.acquire(nc.scalar, gi)
                    ins = nc.scalar.activation(gsb.ap()[:, par], pg[pp].ap(), AF.Silu)
                    vact = self.inc(ins, "act")
                    gPG.release(gi, self.sem["act"], vact)
                    self.wge(nc.vector, "mm", vmm_l)
                    self.wge(nc.vector, "act", vact)
                    gH.acquire(nc.vector, gi)
                    ins = nc.vector.tensor_tensor(hsb.ap()[:, par], pl[pp].ap(),
                                                  gsb.ap()[:, par], ALU.mult)
                    vdve = self.inc(ins, "dve")
                    gPL.release(gi, self.sem["dve"], vdve)
                    gGS.release(gi, self.sem["dve"], vdve)
                    self.wge(nc.sync, "dve", vdve)
                    vst = self.dma(nc.sync, hid.ap()[ch, :, t * 512:(t + 1) * 512],
                                   hsb.ap()[:, par], "sth%d" % par)
                    gH.release(gi, self.sem["sth%d" % par], vst)
                    if ch == 0 and t + 2 < NT:
                        emit_tiles(t + 2)
                        if t + 2 == NT - 1:
                            lns.flush()

    def ffn_down(self, x_in, x_out, hid, collect=False):
        nc, c = self.nc, self.cfg
        NQ, QW, NT = c.NQ, c.QW, c.NT
        kgrp = max(1, c.NH // 8)
        nwq = (c.NH + kgrp - 1) // kgrp
        nspd = (c.NH + kgrp - 1) // kgrp
        NG = NQ * NT * 4
        snap = dict(self.cnt)
        spans = []
        li = 0
        prev_tw = None
        for q in range(NQ):
            order = range(NT) if q % 2 == 0 else range(NT - 1, -1, -1)
            for j, tw in enumerate(order):
                if j == 0 and tw == prev_tw and q > 0:
                    spans.append((q, tw, None))
                else:
                    spans.append((q, tw, li))
                    li += 1
                prev_tw = tw
        nloads = li
        groups = []
        for si, (q, tw, _l) in enumerate(spans):
            for ts in range(4):
                groups.append((q, tw, ts, si))
        span_buf = {}
        bufp = {}
        nxt = 0
        for si, (q, tw, l) in enumerate(spans):
            if l is None:
                span_buf[si] = span_buf[si - 1]
            else:
                span_buf[si] = l % 2
        def mm_after_span(si):
            return (si + 1) * 4
        with contextlib.ExitStack() as ctx:
            w2q = ctx.enter_context(nc.sbuf_tensor(self.nm("d_w2q"), [128, c.NH, QW], BF16))
            hsp = ctx.enter_context(nc.sbuf_tensor(self.nm("d_hsp"), [128, 2, c.NH, 512], BF16))
            xisb = ctx.enter_context(nc.sbuf_tensor(self.nm("d_xi"), [128, 2, QW], F32))
            xosb = ctx.enter_context(nc.sbuf_tensor(self.nm("d_xo"), [128, 2, QW], F32))
            sqs = (ctx.enter_context(nc.sbuf_tensor(self.nm("d_sq"), [128, QW], F32))
                   if collect else None)
            pd = [ctx.enter_context(nc.psum_tensor(self.nm("d_pd%d" % i), [128, QW], F32))
                  for i in range(4)]
            b_mm = self.cnt["mm"]
            b_mq = self.cnt["mq"]
            b_dve = self.cnt["dve"]
            b_w2p = [self.cnt["w2p%d" % i] for i in range(8)]
            b_hl = (self.cnt["hl0"], self.cnt["hl1"])
            b_xl = (self.cnt["xl0"], self.cnt["xl1"])
            b_stx = (self.cnt["stx0"], self.cnt["stx1"])
            sth_tot = [self.cnt["sth%d" % i] for i in range(4)]
            self.quiesce_at(snap, nc.sync)
            for i in range(4):
                self.wge(nc.sync, "sth%d" % i, sth_tot[i])
            g2 = 0
            for si, (q, tw, l) in enumerate(spans):
                if tw == (0 if q % 2 == 0 else NT - 1) or (si > 0 and spans[si - 1][0] != q):
                    pass
                if si == 0 or spans[si - 1][0] != q:
                    for pi, k0 in enumerate(range(0, c.NH, kgrp)):
                        if q >= 1:
                            if pi < nwq - 1:
                                self.wge(nc.sync, "mq",
                                         b_mq + (q - 1) * (nwq - 1) + pi + 1)
                            else:
                                self.wge(nc.sync, "mm", b_mm + q * NT * 4)
                        self.dma(nc.sync, w2q.ap()[:, k0:k0 + kgrp, :],
                                 self.W2fT.ap()[:, q * QW:(q + 1) * QW]
                                 .rearrange("(kc p) m -> p kc m", p=128)[:, k0:k0 + kgrp, :],
                                 "w2p%d" % (pi % 8))
                if l is not None:
                    par = l % 2
                    if l >= 2:
                        prev_si = [s for s, sp in enumerate(spans)
                                   if sp[2] == l - 2][0]
                        last_use = prev_si
                        if prev_si + 1 < len(spans) and spans[prev_si + 1][2] is None:
                            last_use = prev_si + 1
                        self.wge(nc.sync, "mm", b_mm + mm_after_span(last_use))
                    for k0 in range(0, c.NH, kgrp):
                        self.dma(nc.sync, hsp.ap()[:, par, k0:k0 + kgrp, :],
                                 hid.ap()[k0:k0 + kgrp, :, tw * 512:(tw + 1) * 512]
                                 .rearrange("kc p m -> p kc m"), "hl%d" % par)
                for ts in range(4):
                    p2 = g2 % 2
                    if g2 >= 2:
                        self.wge(nc.sync, "dve", b_dve + g2 - 1)
                    r0 = tw * 512 + ts * 128
                    self.dma(nc.sync, xisb.ap()[:, p2],
                             x_in.ap()[r0:r0 + 128, q * QW:(q + 1) * QW],
                             "xl%d" % p2)
                    g2 += 1
            self.quiesce_at(snap, nc.tensor)
            g2 = 0
            hl_seen = [0, 0]
            for si, (q, tw, l) in enumerate(spans):
                first_of_q = (si == 0 or spans[si - 1][0] != q)
                if l is not None:
                    par = l % 2
                    hl_seen[par] += nspd * 16
                    self.wge(nc.tensor, "hl%d" % par, b_hl[par] + hl_seen[par])
                else:
                    par = span_buf[si]
                last_of_q = (si + 1 == len(spans) or spans[si + 1][0] != q)
                for ts in range(4):
                    p4 = g2 % 4
                    if g2 >= 4:
                        self.wge(nc.tensor, "dve", b_dve + g2 - 3)
                    for kc in range(c.NH):
                        if first_of_q and ts == 0 and kc % kgrp == 0:
                            pi = kc // kgrp
                            self.wge(nc.tensor, "w2p%d" % (pi % 8),
                                     b_w2p[pi % 8] + (q + 1) * 16)
                        ins = nc.tensor.matmul(
                            pd[p4].ap(),
                            hsp.ap()[:, par, kc, ts * 128:(ts + 1) * 128],
                            w2q.ap()[:, kc, :],
                            start=(kc == 0), stop=(kc == c.NH - 1))
                        if (last_of_q and ts == 3 and kc % kgrp == kgrp - 1
                                and kc < c.NH - 1):
                            self.inc(ins, "mq")
                    self.inc(ins, "mm")
                    g2 += 1
            def emit_sq(g2s):
                q, tw, ts, _si = groups[g2s]
                gtile = tw * 4 + ts
                p2s = g2s % 2
                ins = nc.vector.scalar_tensor_tensor(
                    sqs.ap(), xosb.ap()[:, p2s], 1.0, xosb.ap()[:, p2s],
                    ALU.mult, ALU.mult,
                    accum_out=self.ln_acc.ap()[:, gtile, 4 + q:5 + q])
                self.inc(ins, "fv")
                self.stats_ready[gtile] = (self.cnt["dve"], self.cnt["fv"])

            for g2 in range(NG):
                p2 = g2 % 2
                q, tw, ts, _si = groups[g2]
                gtile = tw * 4 + ts
                self.wge(nc.vector, "mm", b_mm + g2 + 1)
                self.wge(nc.vector, "xl%d" % p2, b_xl[p2] + (g2 // 2 + 1) * 16)
                if g2 >= 2:
                    self.wge(nc.vector, "stx%d" % p2, b_stx[p2] + 16 * (g2 // 2))
                ins = nc.vector.scalar_tensor_tensor(
                    xosb.ap()[:, p2], pd[g2 % 4].ap(), 0.5, xisb.ap()[:, p2],
                    ALU.mult, ALU.add,
                    accum_out=(self.ln_acc.ap()[:, gtile, q:q + 1]
                               if collect else None))
                self.inc(ins, "dve")
                if collect and g2 >= 1:
                    emit_sq(g2 - 1)
            if collect:
                emit_sq(NG - 1)
            for g2 in range(NG):
                p2 = g2 % 2
                q, tw, ts, _si = groups[g2]
                r0 = tw * 512 + ts * 128
                self.wge(nc.scalar, "dve", b_dve + g2 + 1)
                self.dma(nc.scalar, x_out.ap()[r0:r0 + 128, q * QW:(q + 1) * QW],
                         xosb.ap()[:, p2], "stx%d" % p2)

    def qkv_pass(self, s):
        nc, c = self.nc, self.cfg
        TQ = c.T // 512
        NM = 3 * c.NK
        ln_per_t = 512 // 128
        dsts = [self.Q_s, self.K_s, self.V_s]
        snap = dict(self.cnt)
        with contextlib.ExitStack() as ctx:
            tfm = ctx.enter_context(nc.sbuf_tensor(self.nm("q_tfm"), [128, c.NK * c.T], BF16))
            wqc = ctx.enter_context(nc.sbuf_tensor(self.nm("q_wq"), [128, 2, c.NK, 128], BF16))
            qsb = ctx.enter_context(nc.sbuf_tensor(self.nm("q_qs"), [128, 4, 512], BF16))
            self.quiesce_at(snap, nc.sync, nc.gpsimd, nc.tensor)
            lns = self.ln_state(ctx, self.x1, s * c.T, tfm, c.T, pre=True)
            pq = [ctx.enter_context(nc.psum_tensor(self.nm("q_pq%d" % i), [128, 512], F32))
                  for i in range(3)]
            b_mm = self.cnt["mm"]
            b_wl = (self.cnt["wl0"], self.cnt["wl1"])
            gW = RotGuard(2)
            gP = RotGuard(3)
            gQ = RotGuard(4)
            for m in range(NM):
                bpar = m % 2
                gW.acquire(nc.gpsimd, m)
                self.dma(nc.gpsimd, wqc.ap()[:, bpar],
                         self.Wqkv.ap()[:, m * 128:(m + 1) * 128]
                         .rearrange("(kc p) m -> p kc m", p=128), "wl%d" % bpar)
                gW.release(m, self.sem["mm"], b_mm + (m + 1) * TQ)
            def emit_tiles(tw):
                for tt in range(tw * ln_per_t, (tw + 1) * ln_per_t):
                    lns.tile(tt)

            emit_tiles(0)
            if TQ > 1:
                emit_tiles(1)
            for m in range(NM):
                bpar = m % 2
                kind, h = divmod(m, c.NK)
                tot = (NM + 1) // 2 if bpar == 0 else NM // 2
                self.wge(nc.tensor, "wl%d" % bpar, b_wl[bpar] + 16 * (m // 2 + 1))
                for tq in range(TQ):
                    gi = m * TQ + tq
                    par = gi % 4
                    pp = gi % 3
                    if m == 0:
                        self.wge(nc.tensor, "act",
                                 lns.trans_act[(tq + 1) * ln_per_t - 1])
                    gP.acquire(nc.tensor, gi)
                    for k in range(c.NK):
                        ins = nc.tensor.matmul(
                            pq[pp].ap(), wqc.ap()[:, bpar, k],
                            tfm.ap()[:, k * c.T + tq * 512: k * c.T + tq * 512 + 512],
                            start=(k == 0), stop=(k == c.NK - 1))
                    vmm = self.inc(ins, "mm")
                    self.wge(nc.scalar, "mm", vmm)
                    if gi == 0:
                        self.wge(nc.scalar, "dve", self.bq_ready)
                    gQ.acquire(nc.scalar, gi)
                    ins = nc.scalar.add(qsb.ap()[:, par], pq[pp].ap(),
                                        self.sb_bq.ap()[:, m:m + 1])
                    vact = self.inc(ins, "act")
                    gP.release(gi, self.sem["act"], vact)
                    self.wge(nc.sync, "act", vact)
                    vst = self.dma(nc.sync,
                                   dsts[kind].ap()[s, h % c.NHEADS, :, tq * 512:(tq + 1) * 512],
                                   qsb.ap()[:, par], "stq%d" % par)
                    gQ.release(gi, self.sem["stq%d" % par], vst)
                    if m == 0 and tq + 2 < TQ:
                        emit_tiles(tq + 2)
                        if tq + 2 == TQ - 1:
                            lns.flush()

    def attn_pass(self, s, o_tm, masks, o_guard):
        nc, c = self.nc, self.cfg
        NTC, NQT = c.NTC, c.NQT
        with contextlib.ExitStack() as ctx:
            khb = ctx.enter_context(nc.sbuf_tensor(self.nm("a_kh"), [128, 2, c.T], BF16))
            vhb = ctx.enter_context(nc.sbuf_tensor(self.nm("a_vh"), [128, 2, c.T], BF16))
            qhb = ctx.enter_context(nc.sbuf_tensor(self.nm("a_qh"), [128, 2, c.T], BF16))
            vtm = ctx.enter_context(nc.sbuf_tensor(self.nm("a_vtm"), [128, 2, NTC, 129], BF16))
            pt = ctx.enter_context(nc.sbuf_tensor(self.nm("a_pt"), [128, 2, NTC, 512], BF16))
            rden = ctx.enter_context(nc.sbuf_tensor(self.nm("a_rd"), [128, 3], F32))
            sp = [ctx.enter_context(nc.psum_tensor(self.nm("a_sp%d" % i), [128, 1024], F32))
                  for i in range(2)]
            op = [ctx.enter_context(nc.psum_tensor(self.nm("a_op%d" % i), [128, 129], F32))
                  for i in range(3)]
            vp = [ctx.enter_context(nc.psum_tensor(self.nm("a_vp%d" % i), [128, 512], BF16))
                  for i in range(1)]
            self.quiesce(nc.sync, nc.tensor, nc.vector)
            for par in range(2):
                ins = nc.vector.memset(vtm.ap()[:, par, :, 128:129], 1.0)
                self.inc(ins, "dve")
            ones_dve = self.cnt["dve"]
            stq_tot = [self.cnt["stq%d" % i] for i in range(4)]
            gL = RotGuard(2)
            gVP = RotGuard(1)
            gSP = RotGuard(2)
            gOP = RotGuard(3)
            sc_n = 0
            st = {"iop": 0, "first": True}
            pending = []

            def emit_attv():
                if not pending:
                    return
                hh, hpar, ptp, qt, qs, exp_w, msk_w, vtm_w = pending.pop(0)
                qg = qt * 4 + qs
                iop = st["iop"]
                st["iop"] += 1
                self.wge(nc.tensor, "act", exp_w)
                self.wge(nc.tensor, "dve", max(msk_w, vtm_w))
                gOP.acquire(nc.tensor, iop)
                for kc in range(qg + 1):
                    ins = nc.tensor.matmul(
                        op[iop % 3].ap(),
                        pt.ap()[:, ptp, kc, qs * 128:(qs + 1) * 128],
                        vtm.ap()[:, hpar, kc, :],
                        start=(kc == 0), stop=(kc == qg))
                vmm = self.inc(ins, "mm")
                self.wge(nc.vector, "mm", vmm)
                ins = nc.vector.reciprocal(rden.ap()[:, iop % 3:iop % 3 + 1],
                                           op[iop % 3].ap()[:, 128:129])
                self.inc(ins, "fv")
                self.wge(nc.vector, "fv", self.cnt["fv"])
                if st["first"] and o_guard is not None:
                    self.wge(nc.vector, "mm", o_guard)
                st["first"] = False
                ins = nc.vector.tensor_scalar(
                    o_tm.ap()[:, qg * c.D + hh * 128: qg * c.D + hh * 128 + 128],
                    op[iop % 3].ap()[:, 0:128],
                    rden.ap()[:, iop % 3:iop % 3 + 1], None, ALU.mult)
                vd = self.inc(ins, "dve")
                gOP.release(iop, self.sem["dve"], vd)

            for h in range(c.NHEADS):
                par = h % 2
                if h == 0:
                    for i in range(4):
                        self.wge(nc.sync, "stq%d" % i, stq_tot[i])
                gL.acquire(nc.sync, h)
                for buf, src in ((khb, self.K_s), (vhb, self.V_s), (qhb, self.Q_s)):
                    self.dma(nc.sync, buf.ap()[:, par], src.ap()[s, h, :, :],
                             "atl%d" % par)
                self.wge(nc.tensor, "atl%d" % par, self.cnt["atl%d" % par])
                self.wge(nc.tensor, "dve", ones_dve)
                for jb in range(NTC // 4):
                    ib = h * (NTC // 4) + jb
                    gVP.acquire(nc.tensor, ib)
                    for j in range(4):
                        kc = jb * 4 + j
                        ins = nc.tensor.transpose(
                            vp[0].ap()[:, j * 128:(j + 1) * 128],
                            vhb.ap()[:, par, kc * 128:(kc + 1) * 128],
                            self.sb_id.ap())
                    vmm = self.inc(ins, "mm")
                    self.wge(nc.vector, "mm", vmm)
                    ins = nc.vector.tensor_copy(
                        vtm.ap()[:, par, jb * 4:(jb + 1) * 4, 0:128],
                        vp[0].ap().rearrange("p (k w) -> p k w", w=128))
                    vd = self.inc(ins, "dve")
                    gVP.release(ib, self.sem["dve"], vd)
                    emit_attv()
                vtm_ready = self.cnt["dve"]
                for qt in range(NQT):
                    ptpar = (h * NQT + qt) % 2
                    kmax = 4 * qt + 4
                    exp_last = 0
                    msk_last = 0
                    for kc0 in range(0, kmax, 2):
                        gSP.acquire(nc.tensor, sc_n)
                        for kh in range(2):
                            kc = kc0 + kh
                            ins = nc.tensor.matmul(
                                sp[sc_n % 2].ap()[:, kh * 512:(kh + 1) * 512],
                                khb.ap()[:, par, kc * 128:(kc + 1) * 128],
                                qhb.ap()[:, par, qt * 512:(qt + 1) * 512],
                                start=True, stop=True)
                        vmm = self.inc(ins, "mm")
                        self.wge(nc.scalar, "mm", vmm)
                        ins = nc.scalar.activation(
                            pt.ap()[:, ptpar, kc0:kc0 + 2, :],
                            sp[sc_n % 2].ap(), AF.Exp, scale=c.SCALE)
                        exp_last = self.inc(ins, "act")
                        gSP.release(sc_n, self.sem["act"], exp_last)
                        for kh in range(2):
                            kc = kc0 + kh
                            if kc >= 4 * qt:
                                self.wge(nc.vector, "act", exp_last)
                                ins = nc.vector.tensor_mul(pt.ap()[:, ptpar, kc, :],
                                                           pt.ap()[:, ptpar, kc, :],
                                                           masks.ap()[:, kc - 4 * qt, :])
                                msk_last = self.inc(ins, "dve")
                        sc_n += 1
                        emit_attv()
                    for qs in range(4):
                        pending.append((h, par, ptpar, qt, qs,
                                        exp_last, msk_last, vtm_ready))
                gL.release(h, self.sem["mm"], self.cnt["mm"])
            while pending:
                emit_attv()

    def outproj_pass(self, s, o_tm, bout_bc):
        nc, c = self.nc, self.cfg
        NTC = c.NTC
        NN = c.D // c.QW
        with contextlib.ExitStack() as ctx:
            woc = ctx.enter_context(nc.sbuf_tensor(self.nm("o_wo"), [128, c.NK, c.D], BF16))
            ofm = ctx.enter_context(nc.sbuf_tensor(self.nm("o_ofm"), [128, 2, c.NK, 128], BF16))
            xisb = ctx.enter_context(nc.sbuf_tensor(self.nm("o_xi"), [128, 2, c.D], F32))
            xosb = ctx.enter_context(nc.sbuf_tensor(self.nm("o_xo"), [128, 2, c.D], F32))
            sqs = ctx.enter_context(nc.sbuf_tensor(self.nm("o_sq"), [128, c.QW], F32))
            tp = [ctx.enter_context(nc.psum_tensor(self.nm("o_tp%d" % i), [128, 512], BF16))
                  for i in range(2)]
            pd = [ctx.enter_context(nc.psum_tensor(self.nm("o_pd%d" % i), [128, c.QW], F32))
                  for i in range(2)]
            self.quiesce(nc.gpsimd, nc.sync, nc.tensor, nc.scalar)
            kg = max(1, c.NK // 4)
            b_wo = [self.cnt["wo%d" % i] for i in range(4)]
            for pi, k0 in enumerate(range(0, c.NK, kg)):
                self.dma(nc.gpsimd, woc.ap()[:, k0:k0 + kg, :],
                         self.Wout.ap()
                         .rearrange("(kc p) n -> p kc n", p=128)[:, k0:k0 + kg, :],
                         "wo%d" % (pi % 4))
            gTP = RotGuard(2)
            gPD = RotGuard(2)
            gXI = RotGuard(2)
            gXO = RotGuard(2)
            wo_seen = [False] * 4
            for tc in range(NTC):
                par = tc % 2
                gXI.acquire(nc.sync, tc)
                r0 = s * c.T + tc * 128
                vxi = self.dma(nc.sync, xisb.ap()[:, par], self.x1.ap()[r0:r0 + 128, :],
                               "xl%d" % par)
                for jb in range(c.NK // 4):
                    ib = tc * (c.NK // 4) + jb
                    gTP.acquire(nc.tensor, ib)
                    for j in range(4):
                        kc = jb * 4 + j
                        ins = nc.tensor.transpose(
                            tp[ib % 2].ap()[:, j * 128:(j + 1) * 128],
                            o_tm.ap()[:, tc * c.D + kc * 128: tc * c.D + kc * 128 + 128],
                            self.sb_id.ap())
                    vmm = self.inc(ins, "mm")
                    self.wge(nc.vector, "mm", vmm)
                    ins = nc.vector.tensor_copy(
                        ofm.ap()[:, par, jb * 4:(jb + 1) * 4, :],
                        tp[ib % 2].ap().rearrange("p (k w) -> p k w", w=128))
                    vd = self.inc(ins, "dve")
                    gTP.release(ib, self.sem["dve"], vd)
                self.wge(nc.tensor, "dve", self.cnt["dve"])
                self.wge(nc.vector, "xl%d" % par, vxi)
                gXO.acquire(nc.vector, tc)
                ins = nc.vector.tensor_add(xosb.ap()[:, par], xisb.ap()[:, par],
                                           bout_bc.ap())
                self.inc(ins, "fv")
                self.wge(nc.vector, "fv", self.cnt["fv"])
                for n in range(NN):
                    ip = tc * NN + n
                    gPD.acquire(nc.tensor, ip)
                    for kc in range(c.NK):
                        pi = min(3, kc // kg)
                        if tc == 0 and not wo_seen[pi]:
                            self.wge(nc.tensor, "wo%d" % pi, b_wo[pi] + 16)
                            wo_seen[pi] = True
                        ins = nc.tensor.matmul(
                            pd[ip % 2].ap(), ofm.ap()[:, par, kc],
                            woc.ap()[:, kc, n * c.QW:(n + 1) * c.QW],
                            start=(kc == 0), stop=(kc == c.NK - 1))
                    vmm = self.inc(ins, "mm")
                    self.wge(nc.vector, "mm", vmm)
                    gt = s * (c.T // 128) + tc
                    ins = nc.vector.scalar_tensor_tensor(
                        xosb.ap()[:, par, n * c.QW:(n + 1) * c.QW],
                        pd[ip % 2].ap(), 1.0,
                        xosb.ap()[:, par, n * c.QW:(n + 1) * c.QW], ALU.mult, ALU.add,
                        accum_out=self.ln_acc.ap()[:, gt, n:n + 1])
                    vd = self.inc(ins, "dve")
                    gPD.release(ip, self.sem["dve"], vd)
                    ins = nc.vector.scalar_tensor_tensor(
                        sqs.ap(),
                        xosb.ap()[:, par, n * c.QW:(n + 1) * c.QW], 1.0,
                        xosb.ap()[:, par, n * c.QW:(n + 1) * c.QW],
                        ALU.mult, ALU.mult,
                        accum_out=self.ln_acc.ap()[:, gt, 4 + n:5 + n])
                    self.inc(ins, "fv")
                    self.stats_ready[gt] = (self.cnt["dve"], self.cnt["fv"])
                gXI.release(tc, self.sem["dve"], self.cnt["dve"])
                self.wge(nc.scalar, "dve", self.cnt["dve"])
                self.dma(nc.scalar, self.x2.ap()[r0:r0 + 128, :],
                         xosb.ap()[:, par], "stx%d" % par)
                gXO.release(tc, self.sem["stx%d" % par], self.cnt["stx%d" % par])

    def emit(self):
        nc, c = self.nc, self.cfg
        with contextlib.ExitStack() as topctx:
            self.setup_consts(topctx)
            self.ln_acc = topctx.enter_context(
                nc.sbuf_tensor(self.nm("lnacc"), [128, c.NTT, 8], F32))
            ins = nc.vector.memset(self.ln_acc.ap(), 0.0)
            self.inc(ins, "dve")
            self.stats_ready = {}
            self.w2f_prepass()
            self.ffn_up(self.x0, self.hidA, pre=True)
            self.ffn_down(self.x0, self.x1, self.hidA, collect=True)
            with contextlib.ExitStack() as actx:
                o_tm = actx.enter_context(
                    nc.sbuf_tensor(self.nm("at_otm"), [128, c.NTC * c.D], BF16))
                masks = actx.enter_context(
                    nc.sbuf_tensor(self.nm("at_msk"), [128, 4, 512], BF16))
                bout_bc = actx.enter_context(
                    nc.sbuf_tensor(self.nm("at_bo"), [128, c.D], F32))
                self.quiesce(nc.sync)
                for j in range(4):
                    self.dma(nc.sync, masks.ap()[:, j, :], self.cMask.ap()[j], "bb")
                self.dma(nc.sync, bout_bc.ap(),
                         bass.AP(self.bout.ap().tensor, 0, [[0, 128], [1, c.D]]), "bb")
                bb_done = self.cnt["bb"]
                self.wge(nc.vector, "bb", bb_done)
                self.wge(nc.tensor, "bb", bb_done)
                o_guard = None
                for s in range(c.S):
                    self.qkv_pass(s)
                    self.attn_pass(s, o_tm, masks, o_guard)
                    self.outproj_pass(s, o_tm, bout_bc)
                    o_guard = self.cnt["mm"]
            self.ffn_up(self.x2, self.hidB, pre=True)
            self.ffn_down(self.x2, self.out, self.hidB)
            for name in STORE_SEMS:
                self.wge(nc.sync, name, self.cnt[name])
        return self.nc


def build(cfg: Cfg):
    return Builder(cfg).emit()


def shard_inputs(cfg, L0_attn_x, L0_attn_W_qkv, L0_attn_b_qkv, L0_attn_W_out,
                 L0_attn_b_out, L0_ffn_W1, L0_ffn_W2, L0_ffn_W3, n_cores=N_CORES):
    x = np.ascontiguousarray(np.asarray(L0_attn_x, dtype=np.float32))
    w = {
        "W1T": np.ascontiguousarray(np.asarray(L0_ffn_W1, dtype=np.float32).T),
        "W3T": np.ascontiguousarray(np.asarray(L0_ffn_W3, dtype=np.float32).T),
        "W2T": np.ascontiguousarray(np.asarray(L0_ffn_W2, dtype=np.float32).T),
        "Wqkv": np.ascontiguousarray(np.asarray(L0_attn_W_qkv, dtype=np.float32)),
        "Wout": np.ascontiguousarray(np.asarray(L0_attn_W_out, dtype=np.float32)),
        "bqkv": np.ascontiguousarray(np.asarray(L0_attn_b_qkv, dtype=np.float32)),
        "bout": np.ascontiguousarray(np.asarray(L0_attn_b_out, dtype=np.float32)),
        "H128f": np.ascontiguousarray(_hadamard(128).astype(np.float32)),
    }
    in_maps = []
    for core in range(n_cores):
        xs = np.ascontiguousarray(
            x[cfg.S * core: cfg.S * (core + 1)].reshape(cfg.TOKS, cfg.D))
        m = {"x0": xs}
        m.update(w)
        in_maps.append(m)
    return in_maps


def kernel(L0_attn_x, L0_attn_W_qkv, L0_attn_b_qkv, L0_attn_W_out, L0_attn_b_out,
           L0_ffn_W1, L0_ffn_W2, L0_ffn_W3):
    cfg = Cfg(S=2, T=2048, D=2048, H=8192, NHEADS=16)
    nc = build(cfg)
    in_maps = shard_inputs(cfg, L0_attn_x, L0_attn_W_qkv, L0_attn_b_qkv,
                           L0_attn_W_out, L0_attn_b_out, L0_ffn_W1, L0_ffn_W2,
                           L0_ffn_W3)
    res = run_bass_kernel_spmd(nc, in_maps, core_ids=list(range(N_CORES)))
    out = np.concatenate([np.asarray(r["out"]).reshape(cfg.S, cfg.T, cfg.D)
                          for r in res.results], axis=0)
    return out


# version 18
# speedup vs baseline: 1.0542x; 1.0145x over previous
import math
import contextlib
import numpy as np
import ml_dtypes
import concourse.bass as bass
import concourse.mybir as mybir
from concourse.bass_utils import run_bass_kernel_spmd

F32 = mybir.dt.float32
F32R = mybir.dt.float32r
BF16 = mybir.dt.bfloat16
AF = mybir.ActivationFunctionType
ALU = mybir.AluOpType
AX = mybir.AxisListType

N_CORES = 8


class Cfg:
    def __init__(self, S=2, T=2048, D=2048, H=8192, NHEADS=16):
        self.S = S
        self.T = T
        self.D = D
        self.H = H
        self.NHEADS = NHEADS
        self.DH = D // NHEADS
        assert self.DH == 128
        self.NK = D // 128
        self.NH = H // 128
        self.A = H // 128
        self.TOKS = S * T
        self.NT = self.TOKS // 512
        self.NTT = self.TOKS // 128
        self.NTC = T // 128
        self.NQT = T // 512
        self.QW = min(512, D)
        self.NQ = D // self.QW
        self.SCALE = 1.0 / self.DH


def _hadamard(n):
    h = np.array([[1.0]], dtype=np.float64)
    while h.shape[0] < n:
        h = np.block([[h, h], [h, -h]])
    return h


class RotGuard:

    def __init__(self, n):
        self.n = n
        self.req = [None] * n

    def acquire(self, eng, i):
        r = self.req[i % self.n]
        if r is not None:
            eng.wait_ge(r[0], r[1])

    def release(self, i, sem, val):
        self.req[i % self.n] = (sem, val)


STORE_SEMS = (["stx0", "stx1", "stp0", "stp1", "stz0", "stz1"]
              + ["sth%d" % i for i in range(4)] + ["stq%d" % i for i in range(4)])


class Builder:
    def __init__(self, cfg: Cfg):
        self.cfg = cfg
        nc = bass.Bass("TRN2", target_bir_lowering=False)
        self.nc = nc
        c = cfg
        self.x0 = nc.dram_tensor("x0", [c.TOKS, c.D], F32, kind="ExternalInput")
        self.W1T = nc.dram_tensor("W1T", [c.D, c.H], F32, kind="ExternalInput")
        self.W3T = nc.dram_tensor("W3T", [c.D, c.H], F32, kind="ExternalInput")
        self.W2T = nc.dram_tensor("W2T", [c.H, c.D], F32R, kind="ExternalInput")
        self.H128f = nc.dram_tensor("H128f", [128, 128], F32R, kind="ExternalInput")
        self.Wqkv = nc.dram_tensor("Wqkv", [c.D, 3 * c.D], F32, kind="ExternalInput")
        self.Wout = nc.dram_tensor("Wout", [c.D, c.D], F32, kind="ExternalInput")
        self.bqkv = nc.dram_tensor("bqkv", [3 * c.D], F32, kind="ExternalInput")
        self.bout = nc.dram_tensor("bout", [c.D], F32, kind="ExternalInput")
        self.out = nc.dram_tensor("out", [c.TOKS, c.D], F32, kind="ExternalOutput")
        self.x1 = nc.dram_tensor("x1", [c.TOKS, c.D], F32, kind="Internal")
        self.x2 = nc.dram_tensor("x2", [c.TOKS, c.D], F32, kind="Internal")
        self.hidA = nc.dram_tensor("hidA", [c.NH, 128, c.TOKS], BF16, kind="Internal")
        self.hidB = nc.dram_tensor("hidB", [c.NH, 128, c.TOKS], BF16, kind="Internal")
        self.z_s = nc.dram_tensor("z_s", [128, c.A, c.D], BF16, kind="Internal")
        self.W2fT = nc.dram_tensor("W2fT", [c.H, c.D], BF16, kind="Internal")
        self.Q_s = nc.dram_tensor("Q_s", [c.S, c.NHEADS, 128, c.T], BF16, kind="Internal")
        self.K_s = nc.dram_tensor("K_s", [c.S, c.NHEADS, 128, c.T], BF16, kind="Internal")
        self.V_s = nc.dram_tensor("V_s", [c.S, c.NHEADS, 128, c.T], BF16, kind="Internal")
        h128 = _hadamard(128).astype(ml_dtypes.bfloat16)
        hA = _hadamard(c.A).astype(ml_dtypes.bfloat16)
        self.cH128 = nc.inline_tensor(np.ascontiguousarray(h128), "cH128")
        self.cHA = nc.inline_tensor(np.ascontiguousarray(hA), "cHA")
        self.cId = nc.inline_tensor(np.eye(128, dtype=ml_dtypes.bfloat16), "cId")
        self.cIdF = nc.inline_tensor(np.eye(128, dtype=np.float32), "cIdF")
        masks = np.stack([(np.arange(512)[None, :] >= (np.arange(128)[:, None] + 128 * j))
                          for j in range(4)]).astype(ml_dtypes.bfloat16)
        self.cMask = nc.inline_tensor(np.ascontiguousarray(masks), "cMask")
        self.sem = {}
        self.cnt = {}
        for name in (["mm", "mmt", "act", "dve", "fv", "fa", "mq",
                      "xl0", "xl1", "wl0", "wl1", "w2q", "hl0", "hl1",
                      "atl0", "atl1", "wo", "c0", "zl0", "zl1", "bb"]
                     + ["w2p%d" % i for i in range(8)]
                     + ["wo%d" % i for i in range(4)] + STORE_SEMS):
            self.sem[name] = nc.alloc_semaphore("s_" + name)
            self.cnt[name] = 0

    def nm(self, base):
        self._uid = getattr(self, "_uid", 0) + 1
        return "%s_%d" % (base, self._uid)

    def inc(self, ins, name, k=1):
        ins.then_inc(self.sem[name], k)
        self.cnt[name] += k
        return self.cnt[name]

    def wge(self, eng, name, val):
        if val > 0:
            eng.wait_ge(self.sem[name], val)

    def dma(self, eng, out_ap, in_ap, name):
        ins = eng.dma_start(out_ap, in_ap)
        return self.inc(ins, name, 16)

    def quiesce(self, *engines):
        self.quiesce_at(dict(self.cnt), *engines)

    def quiesce_at(self, snap, *engines):
        for eng in engines:
            for name in snap:
                self.wge(eng, name, snap[name])

    def setup_consts(self, ctx):
        nc, c = self.nc, self.cfg
        self.sb_h128r = ctx.enter_context(nc.sbuf_tensor(self.nm("sbH128r"), [128, 128], F32R))
        self.sb_hA = ctx.enter_context(nc.sbuf_tensor(self.nm("sbHA"), [c.A, c.A], BF16))
        self.sb_id = ctx.enter_context(nc.sbuf_tensor(self.nm("sbId"), [128, 128], BF16))
        self.sb_idf = ctx.enter_context(nc.sbuf_tensor(self.nm("sbIdF"), [128, 128], F32))
        self.sb_bq = ctx.enter_context(nc.sbuf_tensor(self.nm("sbBq"), [128, 3 * c.NK], F32))
        self.sb_eps = ctx.enter_context(nc.sbuf_tensor(self.nm("sbEps"), [128, 1], F32))
        nc.vector.memset(self.sb_eps.ap(), 1e-5)
        self.dma(nc.sync, self.sb_h128r.ap(), self.H128f.ap(), "c0")
        self.dma(nc.sync, self.sb_hA.ap(), self.cHA.ap(), "c0")
        self.dma(nc.sync, self.sb_id.ap(), self.cId.ap(), "c0")
        self.dma(nc.sync, self.sb_idf.ap(), self.cIdF.ap(), "c0")
        nb = 3 * c.NK
        with (nc.sbuf_tensor(self.nm("bq_t"), [nb, 128], F32) as bq_t,
              nc.psum_tensor(self.nm("bq_p"), [128, nb], F32) as bq_p):
            self.dma(nc.sync, bq_t.ap(),
                     self.bqkv.ap().rearrange("(c p) -> c p", p=128), "c0")
            self.wge(nc.tensor, "c0", self.cnt["c0"])
            ins = nc.tensor.transpose(bq_p.ap(), bq_t.ap(), self.sb_idf.ap()[0:nb, 0:nb])
            vmm = self.inc(ins, "mm")
            self.wge(nc.vector, "mm", vmm)
            ins = nc.vector.tensor_copy(self.sb_bq.ap(), bq_p.ap())
            self.inc(ins, "dve")
        self.bq_ready = self.cnt["dve"]

    def w2f_prepass(self):
        nc, c = self.nc, self.cfg
        nd = max(1, c.D // 512)
        dw = min(512, c.D)
        with (nc.sbuf_tensor(self.nm("p0_wf32"), [128, 3, c.D], F32R) as wf32,
              nc.sbuf_tensor(self.nm("p0_zc"), [128, 4, c.D], BF16) as zc,
              nc.sbuf_tensor(self.nm("p0_zr"), [c.A, 2, 2, c.D], BF16) as zr,
              nc.sbuf_tensor(self.nm("p0_wf"), [c.A, 2, 2, c.D], BF16) as wf,
              nc.sbuf_tensor(self.nm("p0_xs"), [128, 4, c.D], F32) as xs0,
              nc.sbuf_tensor(self.nm("p0_xq"), [128, c.D], F32) as xq0,
              nc.psum_tensor(self.nm("p0_ps0"), [128, nd * dw], F32) as ps0,
              nc.psum_tensor(self.nm("p0_ps1"), [128, nd * dw], F32) as ps1):
            ps = [ps0, ps1]
            self.quiesce(nc.gpsimd, nc.sync, nc.scalar)
            self.wge(nc.tensor, "c0", self.cnt["c0"])
            self.wge(nc.tensor, "dve", self.cnt["dve"])
            gW = RotGuard(3)
            gZ = RotGuard(4)
            gP = RotGuard(2)
            gX0 = RotGuard(4)
            x0ld = {}

            def pre_ln_load(tt):
                par = tt % 2
                gX0.acquire(nc.scalar, tt)
                x0ld[tt] = self.dma(nc.scalar, xs0.ap()[:, tt % 4],
                                    self.x0.ap()[tt * 128:(tt + 1) * 128, :],
                                    "xl%d" % par)

            def pre_ln_stats(tt):
                par = tt % 2
                self.wge(nc.vector, "xl%d" % par, x0ld[tt])
                ins = nc.vector.reduce_sum(self.ln_acc.ap()[:, tt, 0:1],
                                           xs0.ap()[:, tt % 4], AX.X)
                self.inc(ins, "fv")
                ins = nc.vector.scalar_tensor_tensor(
                    xq0.ap(), xs0.ap()[:, tt % 4], 1.0, xs0.ap()[:, tt % 4],
                    ALU.mult, ALU.mult,
                    accum_out=self.ln_acc.ap()[:, tt, 4:5])
                vfv = self.inc(ins, "fv")
                gX0.release(tt, self.sem["fv"], vfv)
                self.stats_ready[tt] = (self.cnt["dve"], vfv)

            for a in range(c.A):
                par = a % 2
                p4 = a % 3
                if a % 2 == 0 and a // 2 < c.NTT:
                    pre_ln_load(a // 2)
                elif a // 2 < c.NTT:
                    pre_ln_stats(a // 2)
                gW.acquire(nc.sync, a)
                vld = self.dma(nc.sync, wf32.ap()[:, p4],
                               self.W2T.ap()[a * 128:(a + 1) * 128, :], "wl%d" % par)
                self.wge(nc.tensor, "wl%d" % par, vld)
                gP.acquire(nc.tensor, a)
                for n in range(nd):
                    ins = nc.tensor.matmul(ps[a % 2].ap()[:, n * dw:(n + 1) * dw],
                                           self.sb_h128r.ap(),
                                           wf32.ap()[:, p4, n * dw:(n + 1) * dw],
                                           start=True, stop=True)
                vmm = self.inc(ins, "mm")
                gW.release(a, self.sem["mm"], vmm)
                self.wge(nc.scalar, "mm", vmm)
                gZ.acquire(nc.scalar, a)
                ins = nc.scalar.copy(zc.ap()[:, a % 4], ps[a % 2].ap())
                vact = self.inc(ins, "act")
                gP.release(a, self.sem["act"], vact)
                self.wge(nc.gpsimd, "act", vact)
                vst = self.dma(nc.gpsimd, self.z_s.ap()[:, a, :], zc.ap()[:, a % 4],
                               "stp%d" % par)
                gZ.release(a, self.sem["stp%d" % par], vst)
            scl = float(1.0 / math.sqrt(c.H))
            stA0, stA1 = self.cnt["stp0"], self.cnt["stp1"]
            gWf = [RotGuard(2), RotGuard(2)]
            gP2 = [RotGuard(2), RotGuard(2)]
            w2f_view = self.W2fT.ap().rearrange("(a p) d -> a p d", p=128)
            actA = self.cnt["act"]
            gZr = RotGuard(2)
            for bp in range(64):
                par = bp % 2
                p4 = bp % 2
                if bp == 0:
                    self.wge(nc.sync, "stp0", stA0)
                    self.wge(nc.sync, "stp1", stA1)
                    self.wge(nc.tensor, "act", actA)
                gZr.acquire(nc.sync, bp)
                vld = self.dma(nc.sync, zr.ap()[:, p4],
                               self.z_s.ap()[2 * bp:2 * bp + 2, :, :]
                               .rearrange("b a d -> a b d"), "zl%d" % par)
                self.wge(nc.tensor, "zl%d" % par, vld)
                vmmj = [0, 0]
                for j in range(2):
                    gP2[j].acquire(nc.tensor, bp)
                    for n in range(nd):
                        ins = nc.tensor.matmul(
                            ps[j].ap()[0:c.A, n * dw:(n + 1) * dw], self.sb_hA.ap(),
                            zr.ap()[:, p4, j, n * dw:(n + 1) * dw],
                            start=True, stop=True)
                    vmmj[j] = self.inc(ins, "mm")
                gZr.release(bp, self.sem["mm"], vmmj[1])
                self.wge(nc.vector, "mm", vmmj[0])
                gWf[0].acquire(nc.vector, bp)
                ins = nc.vector.tensor_scalar_mul(wf.ap()[:, par, 0],
                                                  ps[0].ap()[0:c.A, :], scl)
                vd0 = self.inc(ins, "dve")
                gP2[0].release(bp, self.sem["dve"], vd0)
                self.wge(nc.scalar, "mm", vmmj[1])
                gWf[1].acquire(nc.scalar, bp)
                ins = nc.scalar.activation(wf.ap()[:, par, 1], ps[1].ap()[0:c.A, :],
                                           AF.Copy, scale=scl)
                va1 = self.inc(ins, "act")
                gP2[1].release(bp, self.sem["act"], va1)
                self.wge(nc.gpsimd, "dve", vd0)
                self.wge(nc.gpsimd, "act", va1)
                vst = self.dma(nc.gpsimd, w2f_view[:, 2 * bp:2 * bp + 2, :],
                               wf.ap()[:, par], "stz%d" % par)
                gWf[0].release(bp, self.sem["stz%d" % par], vst)
                gWf[1].release(bp, self.sem["stz%d" % par], vst)

    def ln_state(self, ctx, x_in, tok0, tfm, W, pre=False):
        nc, c = self.nc, self.cfg
        b = self

        class LNState:

            def __init__(s):
                s.xs = ctx.enter_context(nc.sbuf_tensor(b.nm("ln_xs"), [128, 2, c.D], F32))
                s.xc = ctx.enter_context(nc.sbuf_tensor(b.nm("ln_xc"), [128, 2, c.D], F32))
                s.xq = ctx.enter_context(nc.sbuf_tensor(b.nm("ln_xq"), [128, c.D], F32))
                s.ttm = ctx.enter_context(nc.sbuf_tensor(b.nm("ln_tt"), [128, 2, c.D], BF16))
                s.st8 = ctx.enter_context(nc.sbuf_tensor(b.nm("ln_st"), [128, 2, 8], F32))
                s.tp = [ctx.enter_context(nc.psum_tensor(b.nm("ln_tp%d" % i), [128, 512], BF16))
                        for i in range(2)]
                s.gX = RotGuard(2)
                s.gT = RotGuard(2)
                s.gP = RotGuard(2)
                s.ttm_done = {}
                s.trans_act = {}
                s.last_stats = -1

            def stats_pre(s, tt):
                par = tt % 2
                gt = tok0 // 128 + tt
                s.gX.acquire(nc.sync, tt)
                r0 = tok0 + tt * 128
                vld = b.dma(nc.sync, s.xs.ap()[:, par], x_in.ap()[r0:r0 + 128, :],
                            "xl%d" % par)
                rdy = b.stats_ready[gt]
                b.wge(nc.vector, "dve", rdy[0])
                b.wge(nc.vector, "fv", rdy[1])
                ins = nc.vector.reduce_sum(s.st8.ap()[:, par, 0:1],
                                           b.ln_acc.ap()[:, gt, 0:4], AX.X)
                b.inc(ins, "fv")
                ins = nc.vector.reduce_sum(s.st8.ap()[:, par, 5:6],
                                           b.ln_acc.ap()[:, gt, 4:8], AX.X)
                b.inc(ins, "fv")
                b.wge(nc.vector, "fv", b.cnt["fv"])
                ins = nc.vector.tensor_scalar_mul(s.st8.ap()[:, par, 1:2],
                                                  s.st8.ap()[:, par, 0:1], 1.0 / c.D)
                b.inc(ins, "fv")
                b.wge(nc.vector, "fv", b.cnt["fv"])
                ins = nc.vector.tensor_mul(s.st8.ap()[:, par, 6:7],
                                           s.st8.ap()[:, par, 0:1],
                                           s.st8.ap()[:, par, 1:2])
                b.inc(ins, "fv")
                b.wge(nc.vector, "fv", b.cnt["fv"])
                ins = nc.vector.tensor_sub(s.st8.ap()[:, par, 2:3],
                                           s.st8.ap()[:, par, 5:6],
                                           s.st8.ap()[:, par, 6:7])
                vfv = b.inc(ins, "fv")
                b.wge(nc.scalar, "fv", vfv)
                ins = nc.scalar.activation(s.st8.ap()[:, par, 3:4],
                                           s.st8.ap()[:, par, 2:3], AF.Ln,
                                           bias=b.sb_eps.ap(), scale=1.0 / c.D)
                b.inc(ins, "fa")
                b.wge(nc.scalar, "fa", b.cnt["fa"])
                ins = nc.scalar.activation(s.st8.ap()[:, par, 4:5],
                                           s.st8.ap()[:, par, 3:4], AF.Exp,
                                           scale=-0.5)
                vact = b.inc(ins, "act")
                b.wge(nc.vector, "act", vact)
                b.wge(nc.vector, "xl%d" % par, vld)
                s.gT.acquire(nc.vector, tt)
                ins = nc.vector.tensor_scalar(s.ttm.ap()[:, par], s.xs.ap()[:, par],
                                              s.st8.ap()[:, par, 1:2],
                                              s.st8.ap()[:, par, 4:5],
                                              ALU.subtract, ALU.mult)
                vdve = b.inc(ins, "dve")
                s.gX.release(tt, b.sem["dve"], vdve)
                s.ttm_done[tt] = vdve

            def stats(s, tt):
                if pre:
                    return s.stats_pre(tt)
                par = tt % 2
                s.gX.acquire(nc.sync, tt)
                r0 = tok0 + tt * 128
                vld = b.dma(nc.sync, s.xs.ap()[:, par], x_in.ap()[r0:r0 + 128, :],
                            "xl%d" % par)
                b.wge(nc.vector, "xl%d" % par, vld)
                ins = nc.vector.reduce_sum(s.st8.ap()[:, par, 0:1], s.xs.ap()[:, par],
                                           AX.X)
                b.inc(ins, "fv")
                b.wge(nc.vector, "fv", b.cnt["fv"])
                ins = nc.vector.tensor_scalar_mul(s.st8.ap()[:, par, 1:2],
                                                  s.st8.ap()[:, par, 0:1], 1.0 / c.D)
                b.inc(ins, "fv")
                b.wge(nc.vector, "fv", b.cnt["fv"])
                ins = nc.vector.tensor_scalar(s.xc.ap()[:, par], s.xs.ap()[:, par],
                                              s.st8.ap()[:, par, 1:2], None,
                                              ALU.subtract)
                b.inc(ins, "fv")
                b.wge(nc.vector, "fv", b.cnt["fv"])
                ins = nc.vector.scalar_tensor_tensor(s.xq.ap(), s.xc.ap()[:, par], 1.0,
                                                     s.xc.ap()[:, par],
                                                     ALU.mult, ALU.mult,
                                                     accum_out=s.st8.ap()[:, par, 2:3])
                vdve = b.inc(ins, "dve")
                s.gX.release(tt, b.sem["dve"], vdve)
                b.wge(nc.scalar, "dve", vdve)
                ins = nc.scalar.activation(s.st8.ap()[:, par, 3:4],
                                           s.st8.ap()[:, par, 2:3], AF.Ln,
                                           bias=b.sb_eps.ap(), scale=1.0 / c.D)
                b.inc(ins, "fa")
                b.wge(nc.scalar, "fa", b.cnt["fa"])
                ins = nc.scalar.activation(s.st8.ap()[:, par, 4:5],
                                           s.st8.ap()[:, par, 3:4], AF.Exp,
                                           scale=-0.5)
                vact = b.inc(ins, "act")
                b.wge(nc.vector, "act", vact)
                s.gT.acquire(nc.vector, tt)
                ins = nc.vector.tensor_scalar(s.ttm.ap()[:, par], s.xc.ap()[:, par],
                                              s.st8.ap()[:, par, 4:5], None,
                                              ALU.mult)
                vdve = b.inc(ins, "dve")
                s.ttm_done[tt] = vdve

            def trans(s, tt):
                par = tt % 2
                b.wge(nc.tensor, "dve", s.ttm_done[tt])
                tfm3 = tfm.ap().rearrange("p (k w) -> p k w", w=W)
                for jb in range(c.NK // 4):
                    ib = tt * (c.NK // 4) + jb
                    s.gP.acquire(nc.tensor, ib)
                    for j in range(4):
                        kc = jb * 4 + j
                        ins = nc.tensor.transpose(
                            s.tp[ib % 2].ap()[:, j * 128:(j + 1) * 128],
                            s.ttm.ap()[:, par, kc * 128:(kc + 1) * 128],
                            b.sb_id.ap())
                    vmm = b.inc(ins, "mmt")
                    b.wge(nc.scalar, "mmt", vmm)
                    ins = nc.scalar.copy(
                        tfm3[:, jb * 4:(jb + 1) * 4, tt * 128:tt * 128 + 128],
                        s.tp[ib % 2].ap().rearrange("p (k w) -> p k w", w=128))
                    va = b.inc(ins, "act")
                    s.gP.release(ib, b.sem["act"], va)
                s.gT.release(tt, b.sem["mmt"], b.cnt["mmt"])
                s.trans_act[tt] = b.cnt["act"]

            def tile(s, tt):
                s.stats(tt)
                if tt > 0:
                    s.trans(tt - 1)
                s.last_stats = tt

            def flush(s):
                s.trans(s.last_stats)
        return LNState()

    def ffn_up(self, x_in, hid, pre=False):
        nc, c = self.nc, self.cfg
        NT = c.NT
        ln_per_t = 512 // 128
        snap = dict(self.cnt)
        with contextlib.ExitStack() as ctx:
            tfm = ctx.enter_context(nc.sbuf_tensor(self.nm("f_tfm"), [128, c.NK * c.TOKS], BF16))
            w1c = ctx.enter_context(nc.sbuf_tensor(self.nm("f_w1c"), [128, 2, c.NK, 128], BF16))
            w3c = ctx.enter_context(nc.sbuf_tensor(self.nm("f_w3c"), [128, 2, c.NK, 128], BF16))
            gsb = ctx.enter_context(nc.sbuf_tensor(self.nm("f_gsb"), [128, 4, 512], F32))
            hsb = ctx.enter_context(nc.sbuf_tensor(self.nm("f_hsb"), [128, 4, 512], BF16))
            self.quiesce_at(snap, nc.sync, nc.gpsimd, nc.tensor)
            lns = self.ln_state(ctx, x_in, 0, tfm, c.TOKS, pre=pre)
            pg = [ctx.enter_context(nc.psum_tensor(self.nm("f_pg%d" % i), [128, 512], F32))
                  for i in range(3)]
            pl = [ctx.enter_context(nc.psum_tensor(self.nm("f_pl%d" % i), [128, 512], F32))
                  for i in range(3)]
            b_mm = self.cnt["mm"]
            b_wl = (self.cnt["wl0"], self.cnt["wl1"])
            gW = RotGuard(2)
            gPG = RotGuard(3)
            gPL = RotGuard(3)
            gGS = RotGuard(4)
            gH = RotGuard(4)
            for ch in range(c.NH):
                bpar = ch % 2
                gW.acquire(nc.gpsimd, ch)
                for wsb, wdr in ((w1c, self.W1T), (w3c, self.W3T)):
                    self.dma(nc.gpsimd, wsb.ap()[:, bpar],
                             wdr.ap()[:, ch * 128:(ch + 1) * 128]
                             .rearrange("(kc p) m -> p kc m", p=128), "wl%d" % bpar)
                gW.release(ch, self.sem["mm"], b_mm + (ch + 1) * 2 * NT)
            act_after_t = {}

            def emit_tiles(tw):
                for tt in range(tw * ln_per_t, (tw + 1) * ln_per_t):
                    lns.tile(tt)

            emit_tiles(0)
            if NT > 1:
                emit_tiles(1)
            for ch in range(c.NH):
                bpar = ch % 2
                self.wge(nc.tensor, "wl%d" % bpar, b_wl[bpar] + 32 * (ch // 2 + 1))
                for t in range(NT):
                    gi = ch * NT + t
                    par = gi % 4
                    pp = gi % 3
                    if ch == 0:
                        self.wge(nc.tensor, "act",
                                 lns.trans_act[(t + 1) * ln_per_t - 1])
                    gPG.acquire(nc.tensor, gi)
                    for k in range(c.NK):
                        ins = nc.tensor.matmul(
                            pg[pp].ap(), w1c.ap()[:, bpar, k],
                            tfm.ap()[:, k * c.TOKS + t * 512: k * c.TOKS + t * 512 + 512],
                            start=(k == 0), stop=(k == c.NK - 1))
                    vmm_g = self.inc(ins, "mm")
                    gPL.acquire(nc.tensor, gi)
                    for k in range(c.NK):
                        ins = nc.tensor.matmul(
                            pl[pp].ap(), w3c.ap()[:, bpar, k],
                            tfm.ap()[:, k * c.TOKS + t * 512: k * c.TOKS + t * 512 + 512],
                            start=(k == 0), stop=(k == c.NK - 1))
                    vmm_l = self.inc(ins, "mm")
                    self.wge(nc.scalar, "mm", vmm_g)
                    gGS.acquire(nc.scalar, gi)
                    ins = nc.scalar.activation(gsb.ap()[:, par], pg[pp].ap(), AF.Silu)
                    vact = self.inc(ins, "act")
                    gPG.release(gi, self.sem["act"], vact)
                    self.wge(nc.vector, "mm", vmm_l)
                    self.wge(nc.vector, "act", vact)
                    gH.acquire(nc.vector, gi)
                    ins = nc.vector.tensor_tensor(hsb.ap()[:, par], pl[pp].ap(),
                                                  gsb.ap()[:, par], ALU.mult)
                    vdve = self.inc(ins, "dve")
                    gPL.release(gi, self.sem["dve"], vdve)
                    gGS.release(gi, self.sem["dve"], vdve)
                    self.wge(nc.sync, "dve", vdve)
                    vst = self.dma(nc.sync, hid.ap()[ch, :, t * 512:(t + 1) * 512],
                                   hsb.ap()[:, par], "sth%d" % par)
                    gH.release(gi, self.sem["sth%d" % par], vst)
                    if ch == 0 and t + 2 < NT:
                        emit_tiles(t + 2)
                        if t + 2 == NT - 1:
                            lns.flush()

    def ffn_down(self, x_in, x_out, hid, collect=False):
        nc, c = self.nc, self.cfg
        NQ, QW, NT = c.NQ, c.QW, c.NT
        kgrp = max(1, c.NH // 8)
        nwq = (c.NH + kgrp - 1) // kgrp
        nspd = (c.NH + kgrp - 1) // kgrp
        NG = NQ * NT * 4
        snap = dict(self.cnt)
        spans = []
        li = 0
        prev_tw = None
        for q in range(NQ):
            order = range(NT) if q % 2 == 0 else range(NT - 1, -1, -1)
            for j, tw in enumerate(order):
                if j == 0 and tw == prev_tw and q > 0:
                    spans.append((q, tw, None))
                else:
                    spans.append((q, tw, li))
                    li += 1
                prev_tw = tw
        nloads = li
        groups = []
        for si, (q, tw, _l) in enumerate(spans):
            for ts in range(4):
                groups.append((q, tw, ts, si))
        span_buf = {}
        bufp = {}
        nxt = 0
        for si, (q, tw, l) in enumerate(spans):
            if l is None:
                span_buf[si] = span_buf[si - 1]
            else:
                span_buf[si] = l % 2
        def mm_after_span(si):
            return (si + 1) * 4
        with contextlib.ExitStack() as ctx:
            w2q = ctx.enter_context(nc.sbuf_tensor(self.nm("d_w2q"), [128, c.NH, QW], BF16))
            hsp = ctx.enter_context(nc.sbuf_tensor(self.nm("d_hsp"), [128, 2, c.NH, 512], BF16))
            xisb = ctx.enter_context(nc.sbuf_tensor(self.nm("d_xi"), [128, 2, QW], F32))
            xosb = ctx.enter_context(nc.sbuf_tensor(self.nm("d_xo"), [128, 2, QW], F32))
            sqs = (ctx.enter_context(nc.sbuf_tensor(self.nm("d_sq"), [128, QW], F32))
                   if collect else None)
            pd = [ctx.enter_context(nc.psum_tensor(self.nm("d_pd%d" % i), [128, QW], F32))
                  for i in range(4)]
            b_mm = self.cnt["mm"]
            b_mq = self.cnt["mq"]
            b_dve = self.cnt["dve"]
            b_w2p = [self.cnt["w2p%d" % i] for i in range(8)]
            b_hl = (self.cnt["hl0"], self.cnt["hl1"])
            b_xl = (self.cnt["xl0"], self.cnt["xl1"])
            b_stx = (self.cnt["stx0"], self.cnt["stx1"])
            sth_tot = [self.cnt["sth%d" % i] for i in range(4)]
            self.quiesce_at(snap, nc.sync)
            for i in range(4):
                self.wge(nc.sync, "sth%d" % i, sth_tot[i])
            g2 = 0
            for si, (q, tw, l) in enumerate(spans):
                if tw == (0 if q % 2 == 0 else NT - 1) or (si > 0 and spans[si - 1][0] != q):
                    pass
                if si == 0 or spans[si - 1][0] != q:
                    for pi, k0 in enumerate(range(0, c.NH, kgrp)):
                        if q >= 1:
                            if pi < nwq - 1:
                                self.wge(nc.sync, "mq",
                                         b_mq + (q - 1) * (nwq - 1) + pi + 1)
                            else:
                                self.wge(nc.sync, "mm", b_mm + q * NT * 4)
                        self.dma(nc.sync, w2q.ap()[:, k0:k0 + kgrp, :],
                                 self.W2fT.ap()[:, q * QW:(q + 1) * QW]
                                 .rearrange("(kc p) m -> p kc m", p=128)[:, k0:k0 + kgrp, :],
                                 "w2p%d" % (pi % 8))
                if l is not None:
                    par = l % 2
                    if l >= 2:
                        prev_si = [s for s, sp in enumerate(spans)
                                   if sp[2] == l - 2][0]
                        last_use = prev_si
                        if prev_si + 1 < len(spans) and spans[prev_si + 1][2] is None:
                            last_use = prev_si + 1
                        self.wge(nc.sync, "mm", b_mm + mm_after_span(last_use))
                    for k0 in range(0, c.NH, kgrp):
                        self.dma(nc.sync, hsp.ap()[:, par, k0:k0 + kgrp, :],
                                 hid.ap()[k0:k0 + kgrp, :, tw * 512:(tw + 1) * 512]
                                 .rearrange("kc p m -> p kc m"), "hl%d" % par)
                for ts in range(4):
                    p2 = g2 % 2
                    if g2 >= 2:
                        self.wge(nc.sync, "dve", b_dve + g2 - 1)
                    r0 = tw * 512 + ts * 128
                    self.dma(nc.sync, xisb.ap()[:, p2],
                             x_in.ap()[r0:r0 + 128, q * QW:(q + 1) * QW],
                             "xl%d" % p2)
                    g2 += 1
            self.quiesce_at(snap, nc.tensor)
            g2 = 0
            hl_seen = [0, 0]
            for si, (q, tw, l) in enumerate(spans):
                first_of_q = (si == 0 or spans[si - 1][0] != q)
                if l is not None:
                    par = l % 2
                    hl_seen[par] += nspd * 16
                    self.wge(nc.tensor, "hl%d" % par, b_hl[par] + hl_seen[par])
                else:
                    par = span_buf[si]
                last_of_q = (si + 1 == len(spans) or spans[si + 1][0] != q)
                for ts in range(4):
                    p4 = g2 % 4
                    if g2 >= 4:
                        self.wge(nc.tensor, "dve", b_dve + g2 - 3)
                    for kc in range(c.NH):
                        if first_of_q and ts == 0 and kc % kgrp == 0:
                            pi = kc // kgrp
                            self.wge(nc.tensor, "w2p%d" % (pi % 8),
                                     b_w2p[pi % 8] + (q + 1) * 16)
                        ins = nc.tensor.matmul(
                            pd[p4].ap(),
                            hsp.ap()[:, par, kc, ts * 128:(ts + 1) * 128],
                            w2q.ap()[:, kc, :],
                            start=(kc == 0), stop=(kc == c.NH - 1))
                        if (last_of_q and ts == 3 and kc % kgrp == kgrp - 1
                                and kc < c.NH - 1):
                            self.inc(ins, "mq")
                    self.inc(ins, "mm")
                    g2 += 1
            def emit_sq(g2s):
                q, tw, ts, _si = groups[g2s]
                gtile = tw * 4 + ts
                p2s = g2s % 2
                ins = nc.vector.scalar_tensor_tensor(
                    sqs.ap(), xosb.ap()[:, p2s], 1.0, xosb.ap()[:, p2s],
                    ALU.mult, ALU.mult,
                    accum_out=self.ln_acc.ap()[:, gtile, 4 + q:5 + q])
                self.inc(ins, "fv")
                self.stats_ready[gtile] = (self.cnt["dve"], self.cnt["fv"])

            for g2 in range(NG):
                p2 = g2 % 2
                q, tw, ts, _si = groups[g2]
                gtile = tw * 4 + ts
                self.wge(nc.vector, "mm", b_mm + g2 + 1)
                self.wge(nc.vector, "xl%d" % p2, b_xl[p2] + (g2 // 2 + 1) * 16)
                if g2 >= 2:
                    self.wge(nc.vector, "stx%d" % p2, b_stx[p2] + 16 * (g2 // 2))
                ins = nc.vector.scalar_tensor_tensor(
                    xosb.ap()[:, p2], pd[g2 % 4].ap(), 0.5, xisb.ap()[:, p2],
                    ALU.mult, ALU.add,
                    accum_out=(self.ln_acc.ap()[:, gtile, q:q + 1]
                               if collect else None))
                self.inc(ins, "dve")
                if collect and g2 >= 1:
                    emit_sq(g2 - 1)
            if collect:
                emit_sq(NG - 1)
            for g2 in range(NG):
                p2 = g2 % 2
                q, tw, ts, _si = groups[g2]
                r0 = tw * 512 + ts * 128
                self.wge(nc.scalar, "dve", b_dve + g2 + 1)
                self.dma(nc.scalar, x_out.ap()[r0:r0 + 128, q * QW:(q + 1) * QW],
                         xosb.ap()[:, p2], "stx%d" % p2)

    def qkv_pass(self, s):
        nc, c = self.nc, self.cfg
        TQ = c.T // 512
        NM = 3 * c.NK
        ln_per_t = 512 // 128
        dsts = [self.Q_s, self.K_s, self.V_s]
        snap = dict(self.cnt)
        with contextlib.ExitStack() as ctx:
            tfm = ctx.enter_context(nc.sbuf_tensor(self.nm("q_tfm"), [128, c.NK * c.T], BF16))
            wqc = ctx.enter_context(nc.sbuf_tensor(self.nm("q_wq"), [128, 2, c.NK, 128], BF16))
            qsb = ctx.enter_context(nc.sbuf_tensor(self.nm("q_qs"), [128, 4, 512], BF16))
            self.quiesce_at(snap, nc.sync, nc.gpsimd, nc.tensor)
            lns = self.ln_state(ctx, self.x1, s * c.T, tfm, c.T, pre=True)
            pq = [ctx.enter_context(nc.psum_tensor(self.nm("q_pq%d" % i), [128, 512], F32))
                  for i in range(3)]
            b_mm = self.cnt["mm"]
            b_wl = (self.cnt["wl0"], self.cnt["wl1"])
            gW = RotGuard(2)
            gP = RotGuard(3)
            gQ = RotGuard(4)
            for m in range(NM):
                bpar = m % 2
                gW.acquire(nc.gpsimd, m)
                self.dma(nc.gpsimd, wqc.ap()[:, bpar],
                         self.Wqkv.ap()[:, m * 128:(m + 1) * 128]
                         .rearrange("(kc p) m -> p kc m", p=128), "wl%d" % bpar)
                gW.release(m, self.sem["mm"], b_mm + (m + 1) * TQ)
            def emit_tiles(tw):
                for tt in range(tw * ln_per_t, (tw + 1) * ln_per_t):
                    lns.tile(tt)

            emit_tiles(0)
            if TQ > 1:
                emit_tiles(1)
            for m in range(NM):
                bpar = m % 2
                kind, h = divmod(m, c.NK)
                tot = (NM + 1) // 2 if bpar == 0 else NM // 2
                self.wge(nc.tensor, "wl%d" % bpar, b_wl[bpar] + 16 * (m // 2 + 1))
                for tq in range(TQ):
                    gi = m * TQ + tq
                    par = gi % 4
                    pp = gi % 3
                    if m == 0:
                        self.wge(nc.tensor, "act",
                                 lns.trans_act[(tq + 1) * ln_per_t - 1])
                    gP.acquire(nc.tensor, gi)
                    for k in range(c.NK):
                        ins = nc.tensor.matmul(
                            pq[pp].ap(), wqc.ap()[:, bpar, k],
                            tfm.ap()[:, k * c.T + tq * 512: k * c.T + tq * 512 + 512],
                            start=(k == 0), stop=(k == c.NK - 1))
                    vmm = self.inc(ins, "mm")
                    self.wge(nc.scalar, "mm", vmm)
                    if gi == 0:
                        self.wge(nc.scalar, "dve", self.bq_ready)
                    gQ.acquire(nc.scalar, gi)
                    ins = nc.scalar.add(qsb.ap()[:, par], pq[pp].ap(),
                                        self.sb_bq.ap()[:, m:m + 1])
                    vact = self.inc(ins, "act")
                    gP.release(gi, self.sem["act"], vact)
                    self.wge(nc.sync, "act", vact)
                    vst = self.dma(nc.sync,
                                   dsts[kind].ap()[s, h % c.NHEADS, :, tq * 512:(tq + 1) * 512],
                                   qsb.ap()[:, par], "stq%d" % par)
                    gQ.release(gi, self.sem["stq%d" % par], vst)
                    if m == 0 and tq + 2 < TQ:
                        emit_tiles(tq + 2)
                        if tq + 2 == TQ - 1:
                            lns.flush()

    def attn_pass(self, s, o_tm, masks, o_guard):
        nc, c = self.nc, self.cfg
        NTC, NQT = c.NTC, c.NQT
        with contextlib.ExitStack() as ctx:
            khb = ctx.enter_context(nc.sbuf_tensor(self.nm("a_kh"), [128, 2, c.T], BF16))
            vhb = ctx.enter_context(nc.sbuf_tensor(self.nm("a_vh"), [128, 2, c.T], BF16))
            qhb = ctx.enter_context(nc.sbuf_tensor(self.nm("a_qh"), [128, 2, c.T], BF16))
            vtm = ctx.enter_context(nc.sbuf_tensor(self.nm("a_vtm"), [128, 2, NTC, 129], BF16))
            pt = ctx.enter_context(nc.sbuf_tensor(self.nm("a_pt"), [128, 2, NTC, 512], BF16))
            rden = ctx.enter_context(nc.sbuf_tensor(self.nm("a_rd"), [128, 3], F32))
            sp = [ctx.enter_context(nc.psum_tensor(self.nm("a_sp%d" % i), [128, 1024], F32))
                  for i in range(2)]
            op = [ctx.enter_context(nc.psum_tensor(self.nm("a_op%d" % i), [128, 129], F32))
                  for i in range(3)]
            vp = [ctx.enter_context(nc.psum_tensor(self.nm("a_vp%d" % i), [128, 512], BF16))
                  for i in range(1)]
            self.quiesce(nc.sync, nc.tensor, nc.vector)
            for par in range(2):
                ins = nc.vector.memset(vtm.ap()[:, par, :, 128:129], 1.0)
                self.inc(ins, "dve")
            ones_dve = self.cnt["dve"]
            stq_tot = [self.cnt["stq%d" % i] for i in range(4)]
            gL = RotGuard(2)
            gVP = RotGuard(1)
            gSP = RotGuard(2)
            gOP = RotGuard(3)
            sc_n = 0
            st = {"iop": 0, "first": True}
            pending = []

            def emit_attv():
                if not pending:
                    return
                hh, hpar, ptp, qt, qs, exp_w, msk_w, vtm_w = pending.pop(0)
                qg = qt * 4 + qs
                iop = st["iop"]
                st["iop"] += 1
                self.wge(nc.tensor, "act", exp_w)
                self.wge(nc.tensor, "dve", max(msk_w, vtm_w))
                gOP.acquire(nc.tensor, iop)
                for kc in range(qg + 1):
                    ins = nc.tensor.matmul(
                        op[iop % 3].ap(),
                        pt.ap()[:, ptp, kc, qs * 128:(qs + 1) * 128],
                        vtm.ap()[:, hpar, kc, :],
                        start=(kc == 0), stop=(kc == qg))
                vmm = self.inc(ins, "mm")
                self.wge(nc.vector, "mm", vmm)
                ins = nc.vector.reciprocal(rden.ap()[:, iop % 3:iop % 3 + 1],
                                           op[iop % 3].ap()[:, 128:129])
                self.inc(ins, "fv")
                self.wge(nc.vector, "fv", self.cnt["fv"])
                if st["first"] and o_guard is not None:
                    self.wge(nc.vector, "mm", o_guard)
                st["first"] = False
                ins = nc.vector.tensor_scalar(
                    o_tm.ap()[:, qg * c.D + hh * 128: qg * c.D + hh * 128 + 128],
                    op[iop % 3].ap()[:, 0:128],
                    rden.ap()[:, iop % 3:iop % 3 + 1], None, ALU.mult)
                vd = self.inc(ins, "dve")
                gOP.release(iop, self.sem["dve"], vd)

            for h in range(c.NHEADS):
                par = h % 2
                if h == 0:
                    for i in range(4):
                        self.wge(nc.sync, "stq%d" % i, stq_tot[i])
                gL.acquire(nc.sync, h)
                for buf, src in ((khb, self.K_s), (vhb, self.V_s), (qhb, self.Q_s)):
                    self.dma(nc.sync, buf.ap()[:, par], src.ap()[s, h, :, :],
                             "atl%d" % par)
                self.wge(nc.tensor, "atl%d" % par, self.cnt["atl%d" % par])
                self.wge(nc.tensor, "dve", ones_dve)
                for jb in range(NTC // 4):
                    ib = h * (NTC // 4) + jb
                    gVP.acquire(nc.tensor, ib)
                    for j in range(4):
                        kc = jb * 4 + j
                        ins = nc.tensor.transpose(
                            vp[0].ap()[:, j * 128:(j + 1) * 128],
                            vhb.ap()[:, par, kc * 128:(kc + 1) * 128],
                            self.sb_id.ap())
                    vmm = self.inc(ins, "mm")
                    self.wge(nc.vector, "mm", vmm)
                    ins = nc.vector.tensor_copy(
                        vtm.ap()[:, par, jb * 4:(jb + 1) * 4, 0:128],
                        vp[0].ap().rearrange("p (k w) -> p k w", w=128))
                    vd = self.inc(ins, "dve")
                    gVP.release(ib, self.sem["dve"], vd)
                    emit_attv()
                vtm_ready = self.cnt["dve"]
                for qt in range(NQT):
                    ptpar = (h * NQT + qt) % 2
                    kmax = 4 * qt + 4
                    exp_last = 0
                    msk_last = 0
                    for kc0 in range(0, kmax, 2):
                        gSP.acquire(nc.tensor, sc_n)
                        for kh in range(2):
                            kc = kc0 + kh
                            ins = nc.tensor.matmul(
                                sp[sc_n % 2].ap()[:, kh * 512:(kh + 1) * 512],
                                khb.ap()[:, par, kc * 128:(kc + 1) * 128],
                                qhb.ap()[:, par, qt * 512:(qt + 1) * 512],
                                start=True, stop=True)
                        vmm = self.inc(ins, "mm")
                        self.wge(nc.scalar, "mm", vmm)
                        ins = nc.scalar.activation(
                            pt.ap()[:, ptpar, kc0:kc0 + 2, :],
                            sp[sc_n % 2].ap(), AF.Exp, scale=c.SCALE)
                        exp_last = self.inc(ins, "act")
                        gSP.release(sc_n, self.sem["act"], exp_last)
                        for kh in range(2):
                            kc = kc0 + kh
                            if kc >= 4 * qt:
                                self.wge(nc.vector, "act", exp_last)
                                ins = nc.vector.tensor_mul(pt.ap()[:, ptpar, kc, :],
                                                           pt.ap()[:, ptpar, kc, :],
                                                           masks.ap()[:, kc - 4 * qt, :])
                                msk_last = self.inc(ins, "dve")
                        sc_n += 1
                        emit_attv()
                    for qs in range(4):
                        pending.append((h, par, ptpar, qt, qs,
                                        exp_last, msk_last, vtm_ready))
                gL.release(h, self.sem["mm"], self.cnt["mm"])
            while pending:
                emit_attv()

    def outproj_pass(self, s, o_tm, bout_bc):
        nc, c = self.nc, self.cfg
        NTC = c.NTC
        NN = c.D // c.QW
        with contextlib.ExitStack() as ctx:
            woc = ctx.enter_context(nc.sbuf_tensor(self.nm("o_wo"), [128, c.NK, c.D], BF16))
            ofm = ctx.enter_context(nc.sbuf_tensor(self.nm("o_ofm"), [128, 2, c.NK, 128], BF16))
            xisb = ctx.enter_context(nc.sbuf_tensor(self.nm("o_xi"), [128, 2, c.D], F32))
            xosb = ctx.enter_context(nc.sbuf_tensor(self.nm("o_xo"), [128, 2, c.D], F32))
            sqs = ctx.enter_context(nc.sbuf_tensor(self.nm("o_sq"), [128, c.QW], F32))
            tp = [ctx.enter_context(nc.psum_tensor(self.nm("o_tp%d" % i), [128, 512], BF16))
                  for i in range(2)]
            pd = [ctx.enter_context(nc.psum_tensor(self.nm("o_pd%d" % i), [128, c.QW], F32))
                  for i in range(2)]
            self.quiesce(nc.gpsimd, nc.sync, nc.tensor, nc.scalar)
            kg = max(1, c.NK // 4)
            b_wo = [self.cnt["wo%d" % i] for i in range(4)]
            for pi, k0 in enumerate(range(0, c.NK, kg)):
                self.dma(nc.gpsimd, woc.ap()[:, k0:k0 + kg, :],
                         self.Wout.ap()
                         .rearrange("(kc p) n -> p kc n", p=128)[:, k0:k0 + kg, :],
                         "wo%d" % (pi % 4))
            gTP = RotGuard(2)
            gPD = RotGuard(2)
            gXI = RotGuard(2)
            gXO = RotGuard(2)
            wo_seen = [False] * 4
            for tc in range(NTC):
                par = tc % 2
                gXI.acquire(nc.sync, tc)
                r0 = s * c.T + tc * 128
                vxi = self.dma(nc.sync, xisb.ap()[:, par], self.x1.ap()[r0:r0 + 128, :],
                               "xl%d" % par)
                for jb in range(c.NK // 4):
                    ib = tc * (c.NK // 4) + jb
                    gTP.acquire(nc.tensor, ib)
                    for j in range(4):
                        kc = jb * 4 + j
                        ins = nc.tensor.transpose(
                            tp[ib % 2].ap()[:, j * 128:(j + 1) * 128],
                            o_tm.ap()[:, tc * c.D + kc * 128: tc * c.D + kc * 128 + 128],
                            self.sb_id.ap())
                    vmm = self.inc(ins, "mm")
                    self.wge(nc.vector, "mm", vmm)
                    ins = nc.vector.tensor_copy(
                        ofm.ap()[:, par, jb * 4:(jb + 1) * 4, :],
                        tp[ib % 2].ap().rearrange("p (k w) -> p k w", w=128))
                    vd = self.inc(ins, "dve")
                    gTP.release(ib, self.sem["dve"], vd)
                self.wge(nc.tensor, "dve", self.cnt["dve"])
                self.wge(nc.vector, "xl%d" % par, vxi)
                gXO.acquire(nc.vector, tc)
                ins = nc.vector.tensor_add(xosb.ap()[:, par], xisb.ap()[:, par],
                                           bout_bc.ap())
                self.inc(ins, "fv")
                self.wge(nc.vector, "fv", self.cnt["fv"])
                for n in range(NN):
                    ip = tc * NN + n
                    gPD.acquire(nc.tensor, ip)
                    for kc in range(c.NK):
                        pi = min(3, kc // kg)
                        if tc == 0 and not wo_seen[pi]:
                            self.wge(nc.tensor, "wo%d" % pi, b_wo[pi] + 16)
                            wo_seen[pi] = True
                        ins = nc.tensor.matmul(
                            pd[ip % 2].ap(), ofm.ap()[:, par, kc],
                            woc.ap()[:, kc, n * c.QW:(n + 1) * c.QW],
                            start=(kc == 0), stop=(kc == c.NK - 1))
                    vmm = self.inc(ins, "mm")
                    self.wge(nc.vector, "mm", vmm)
                    gt = s * (c.T // 128) + tc
                    ins = nc.vector.scalar_tensor_tensor(
                        xosb.ap()[:, par, n * c.QW:(n + 1) * c.QW],
                        pd[ip % 2].ap(), 1.0,
                        xosb.ap()[:, par, n * c.QW:(n + 1) * c.QW], ALU.mult, ALU.add,
                        accum_out=self.ln_acc.ap()[:, gt, n:n + 1])
                    vd = self.inc(ins, "dve")
                    gPD.release(ip, self.sem["dve"], vd)
                    ins = nc.vector.scalar_tensor_tensor(
                        sqs.ap(),
                        xosb.ap()[:, par, n * c.QW:(n + 1) * c.QW], 1.0,
                        xosb.ap()[:, par, n * c.QW:(n + 1) * c.QW],
                        ALU.mult, ALU.mult,
                        accum_out=self.ln_acc.ap()[:, gt, 4 + n:5 + n])
                    self.inc(ins, "fv")
                    self.stats_ready[gt] = (self.cnt["dve"], self.cnt["fv"])
                gXI.release(tc, self.sem["dve"], self.cnt["dve"])
                self.wge(nc.scalar, "dve", self.cnt["dve"])
                self.dma(nc.scalar, self.x2.ap()[r0:r0 + 128, :],
                         xosb.ap()[:, par], "stx%d" % par)
                gXO.release(tc, self.sem["stx%d" % par], self.cnt["stx%d" % par])

    def emit(self):
        nc, c = self.nc, self.cfg
        with contextlib.ExitStack() as topctx:
            self.setup_consts(topctx)
            self.ln_acc = topctx.enter_context(
                nc.sbuf_tensor(self.nm("lnacc"), [128, c.NTT, 8], F32))
            ins = nc.vector.memset(self.ln_acc.ap(), 0.0)
            self.inc(ins, "dve")
            self.stats_ready = {}
            self.w2f_prepass()
            self.ffn_up(self.x0, self.hidA, pre=True)
            self.ffn_down(self.x0, self.x1, self.hidA, collect=True)
            with contextlib.ExitStack() as actx:
                o_tm = actx.enter_context(
                    nc.sbuf_tensor(self.nm("at_otm"), [128, c.NTC * c.D], BF16))
                masks = actx.enter_context(
                    nc.sbuf_tensor(self.nm("at_msk"), [128, 4, 512], BF16))
                bout_bc = actx.enter_context(
                    nc.sbuf_tensor(self.nm("at_bo"), [128, c.D], F32))
                self.quiesce(nc.sync)
                for j in range(4):
                    self.dma(nc.sync, masks.ap()[:, j, :], self.cMask.ap()[j], "bb")
                self.dma(nc.sync, bout_bc.ap(),
                         bass.AP(self.bout.ap().tensor, 0, [[0, 128], [1, c.D]]), "bb")
                bb_done = self.cnt["bb"]
                self.wge(nc.vector, "bb", bb_done)
                self.wge(nc.tensor, "bb", bb_done)
                o_guard = None
                for s in range(c.S):
                    self.qkv_pass(s)
                    self.attn_pass(s, o_tm, masks, o_guard)
                    self.outproj_pass(s, o_tm, bout_bc)
                    o_guard = self.cnt["mm"]
            self.ffn_up(self.x2, self.hidB, pre=True)
            self.ffn_down(self.x2, self.out, self.hidB)
            for name in STORE_SEMS:
                self.wge(nc.sync, name, self.cnt[name])
        return self.nc


def build(cfg: Cfg):
    return Builder(cfg).emit()


def shard_inputs(cfg, L0_attn_x, L0_attn_W_qkv, L0_attn_b_qkv, L0_attn_W_out,
                 L0_attn_b_out, L0_ffn_W1, L0_ffn_W2, L0_ffn_W3, n_cores=N_CORES):
    x = np.ascontiguousarray(np.asarray(L0_attn_x, dtype=np.float32))
    w = {
        "W1T": np.ascontiguousarray(np.asarray(L0_ffn_W1, dtype=np.float32).T),
        "W3T": np.ascontiguousarray(np.asarray(L0_ffn_W3, dtype=np.float32).T),
        "W2T": np.ascontiguousarray(np.asarray(L0_ffn_W2, dtype=np.float32).T),
        "Wqkv": np.ascontiguousarray(np.asarray(L0_attn_W_qkv, dtype=np.float32)),
        "Wout": np.ascontiguousarray(np.asarray(L0_attn_W_out, dtype=np.float32)),
        "bqkv": np.ascontiguousarray(np.asarray(L0_attn_b_qkv, dtype=np.float32)),
        "bout": np.ascontiguousarray(np.asarray(L0_attn_b_out, dtype=np.float32)),
        "H128f": np.ascontiguousarray(_hadamard(128).astype(np.float32)),
    }
    in_maps = []
    for core in range(n_cores):
        xs = np.ascontiguousarray(
            x[cfg.S * core: cfg.S * (core + 1)].reshape(cfg.TOKS, cfg.D))
        m = {"x0": xs}
        m.update(w)
        in_maps.append(m)
    return in_maps


def kernel(L0_attn_x, L0_attn_W_qkv, L0_attn_b_qkv, L0_attn_W_out, L0_attn_b_out,
           L0_ffn_W1, L0_ffn_W2, L0_ffn_W3):
    cfg = Cfg(S=2, T=2048, D=2048, H=8192, NHEADS=16)
    nc = build(cfg)
    in_maps = shard_inputs(cfg, L0_attn_x, L0_attn_W_qkv, L0_attn_b_qkv,
                           L0_attn_W_out, L0_attn_b_out, L0_ffn_W1, L0_ffn_W2,
                           L0_ffn_W3)
    res = run_bass_kernel_spmd(nc, in_maps, core_ids=list(range(N_CORES)))
    out = np.concatenate([np.asarray(r["out"]).reshape(cfg.S, cfg.T, cfg.D)
                          for r in res.results], axis=0)
    return out


# version 19 (final)
# speedup vs baseline: 1.0673x; 1.0124x over previous
import math
import contextlib
import numpy as np
import ml_dtypes
import concourse.bass as bass
import concourse.mybir as mybir
from concourse.bass_utils import run_bass_kernel_spmd

F32 = mybir.dt.float32
F32R = mybir.dt.float32r
BF16 = mybir.dt.bfloat16
AF = mybir.ActivationFunctionType
ALU = mybir.AluOpType
AX = mybir.AxisListType

N_CORES = 8


class Cfg:
    def __init__(self, S=2, T=2048, D=2048, H=8192, NHEADS=16):
        self.S = S
        self.T = T
        self.D = D
        self.H = H
        self.NHEADS = NHEADS
        self.DH = D // NHEADS
        assert self.DH == 128
        self.NK = D // 128
        self.NH = H // 128
        self.A = H // 128
        self.TOKS = S * T
        self.NT = self.TOKS // 512
        self.NTT = self.TOKS // 128
        self.NTC = T // 128
        self.NQT = T // 512
        self.QW = min(512, D)
        self.NQ = D // self.QW
        self.SCALE = 1.0 / self.DH


def _hadamard(n):
    h = np.array([[1.0]], dtype=np.float64)
    while h.shape[0] < n:
        h = np.block([[h, h], [h, -h]])
    return h


class RotGuard:

    def __init__(self, n):
        self.n = n
        self.req = [None] * n

    def acquire(self, eng, i):
        r = self.req[i % self.n]
        if r is not None:
            eng.wait_ge(r[0], r[1])

    def release(self, i, sem, val):
        self.req[i % self.n] = (sem, val)


STORE_SEMS = (["stx0", "stx1", "stp0", "stp1", "stp2", "stp3", "stz0", "stz1"]
              + ["sth%d" % i for i in range(4)] + ["stq%d" % i for i in range(4)])


class Builder:
    def __init__(self, cfg: Cfg):
        self.cfg = cfg
        nc = bass.Bass("TRN2", target_bir_lowering=False)
        self.nc = nc
        c = cfg
        self.x0 = nc.dram_tensor("x0", [c.TOKS, c.D], F32, kind="ExternalInput")
        self.W1T = nc.dram_tensor("W1T", [c.D, c.H], F32, kind="ExternalInput")
        self.W3T = nc.dram_tensor("W3T", [c.D, c.H], F32, kind="ExternalInput")
        self.W2T = nc.dram_tensor("W2T", [c.H, c.D], F32R, kind="ExternalInput")
        self.H128f = nc.dram_tensor("H128f", [128, 128], F32R, kind="ExternalInput")
        self.Wqkv = nc.dram_tensor("Wqkv", [c.D, 3 * c.D], F32, kind="ExternalInput")
        self.Wout = nc.dram_tensor("Wout", [c.D, c.D], F32, kind="ExternalInput")
        self.bqkv = nc.dram_tensor("bqkv", [3 * c.D], F32, kind="ExternalInput")
        self.bout = nc.dram_tensor("bout", [c.D], F32, kind="ExternalInput")
        self.out = nc.dram_tensor("out", [c.TOKS, c.D], F32, kind="ExternalOutput")
        self.x1 = nc.dram_tensor("x1", [c.TOKS, c.D], F32, kind="Internal")
        self.x2 = nc.dram_tensor("x2", [c.TOKS, c.D], F32, kind="Internal")
        self.hidA = nc.dram_tensor("hidA", [c.NH, 128, c.TOKS], BF16, kind="Internal")
        self.hidB = nc.dram_tensor("hidB", [c.NH, 128, c.TOKS], BF16, kind="Internal")
        self.z_s = nc.dram_tensor("z_s", [128, c.A, c.D], BF16, kind="Internal")
        self.W2fT = nc.dram_tensor("W2fT", [c.H, c.D], BF16, kind="Internal")
        self.Q_s = nc.dram_tensor("Q_s", [c.S, c.NHEADS, 128, c.T], BF16, kind="Internal")
        self.K_s = nc.dram_tensor("K_s", [c.S, c.NHEADS, 128, c.T], BF16, kind="Internal")
        self.V_s = nc.dram_tensor("V_s", [c.S, c.NHEADS, 128, c.T], BF16, kind="Internal")
        h128 = _hadamard(128).astype(ml_dtypes.bfloat16)
        hA = _hadamard(c.A).astype(ml_dtypes.bfloat16)
        self.cH128 = nc.inline_tensor(np.ascontiguousarray(h128), "cH128")
        self.cHA = nc.inline_tensor(np.ascontiguousarray(hA), "cHA")
        self.cId = nc.inline_tensor(np.eye(128, dtype=ml_dtypes.bfloat16), "cId")
        self.cIdF = nc.inline_tensor(np.eye(128, dtype=np.float32), "cIdF")
        masks = np.stack([(np.arange(512)[None, :] >= (np.arange(128)[:, None] + 128 * j))
                          for j in range(4)]).astype(ml_dtypes.bfloat16)
        self.cMask = nc.inline_tensor(np.ascontiguousarray(masks), "cMask")
        self.sem = {}
        self.cnt = {}
        for name in (["mm", "mmt", "act", "dve", "fv", "fa", "mq",
                      "xl0", "xl1", "wl0", "wl1", "wl2", "w2q", "hl0", "hl1",
                      "px0", "px1", "px2", "px3",
                      "atl0", "atl1", "wo", "c0", "zl0", "zl1", "bb"]
                     + ["w2p%d" % i for i in range(8)]
                     + ["wo%d" % i for i in range(4)] + STORE_SEMS):
            self.sem[name] = nc.alloc_semaphore("s_" + name)
            self.cnt[name] = 0

    def nm(self, base):
        self._uid = getattr(self, "_uid", 0) + 1
        return "%s_%d" % (base, self._uid)

    def inc(self, ins, name, k=1):
        ins.then_inc(self.sem[name], k)
        self.cnt[name] += k
        return self.cnt[name]

    def wge(self, eng, name, val):
        if val > 0:
            eng.wait_ge(self.sem[name], val)

    def dma(self, eng, out_ap, in_ap, name):
        ins = eng.dma_start(out_ap, in_ap)
        return self.inc(ins, name, 16)

    def quiesce(self, *engines):
        self.quiesce_at(dict(self.cnt), *engines)

    def quiesce_at(self, snap, *engines):
        for eng in engines:
            for name in snap:
                self.wge(eng, name, snap[name])

    def setup_consts(self, ctx):
        nc, c = self.nc, self.cfg
        self.sb_h128r = ctx.enter_context(nc.sbuf_tensor(self.nm("sbH128r"), [128, 128], F32R))
        self.sb_hA = ctx.enter_context(nc.sbuf_tensor(self.nm("sbHA"), [c.A, c.A], BF16))
        self.sb_id = ctx.enter_context(nc.sbuf_tensor(self.nm("sbId"), [128, 128], BF16))
        self.sb_idf = ctx.enter_context(nc.sbuf_tensor(self.nm("sbIdF"), [128, 128], F32))
        self.sb_bq = ctx.enter_context(nc.sbuf_tensor(self.nm("sbBq"), [128, 3 * c.NK], F32))
        self.sb_eps = ctx.enter_context(nc.sbuf_tensor(self.nm("sbEps"), [128, 1], F32))
        nc.vector.memset(self.sb_eps.ap(), 1e-5)
        self.dma(nc.sync, self.sb_h128r.ap(), self.H128f.ap(), "c0")
        self.dma(nc.sync, self.sb_hA.ap(), self.cHA.ap(), "c0")
        self.dma(nc.sync, self.sb_id.ap(), self.cId.ap(), "c0")
        self.dma(nc.sync, self.sb_idf.ap(), self.cIdF.ap(), "c0")
        nb = 3 * c.NK
        with (nc.sbuf_tensor(self.nm("bq_t"), [nb, 128], F32) as bq_t,
              nc.psum_tensor(self.nm("bq_p"), [128, nb], F32) as bq_p):
            self.dma(nc.sync, bq_t.ap(),
                     self.bqkv.ap().rearrange("(c p) -> c p", p=128), "c0")
            self.wge(nc.tensor, "c0", self.cnt["c0"])
            ins = nc.tensor.transpose(bq_p.ap(), bq_t.ap(), self.sb_idf.ap()[0:nb, 0:nb])
            vmm = self.inc(ins, "mm")
            self.wge(nc.vector, "mm", vmm)
            ins = nc.vector.tensor_copy(self.sb_bq.ap(), bq_p.ap())
            self.inc(ins, "dve")
        self.bq_ready = self.cnt["dve"]

    def w2f_prepass(self):
        nc, c = self.nc, self.cfg
        nd = max(1, c.D // 512)
        dw = min(512, c.D)
        with (nc.sbuf_tensor(self.nm("p0_wf32"), [128, 3, c.D], F32R) as wf32,
              nc.sbuf_tensor(self.nm("p0_zc"), [128, 4, c.D], BF16) as zc,
              nc.sbuf_tensor(self.nm("p0_zr"), [c.A, 2, 2, c.D], BF16) as zr,
              nc.sbuf_tensor(self.nm("p0_wf"), [c.A, 2, 2, c.D], BF16) as wf,
              nc.sbuf_tensor(self.nm("p0_xs"), [128, 4, c.D], F32) as xs0,
              nc.sbuf_tensor(self.nm("p0_xq"), [128, c.D], F32) as xq0,
              nc.psum_tensor(self.nm("p0_ps0"), [128, nd * dw], F32) as ps0,
              nc.psum_tensor(self.nm("p0_ps1"), [128, nd * dw], F32) as ps1):
            ps = [ps0, ps1]
            self.quiesce(nc.gpsimd, nc.sync, nc.scalar)
            self.wge(nc.tensor, "c0", self.cnt["c0"])
            self.wge(nc.tensor, "dve", self.cnt["dve"])
            gW = RotGuard(3)
            gZ = RotGuard(4)
            gP = RotGuard(2)
            gX0 = RotGuard(4)
            x0ld = {}

            def pre_ln_load(tt):
                gX0.acquire(nc.scalar, tt)
                x0ld[tt] = self.dma(nc.scalar, xs0.ap()[:, tt % 4],
                                    self.x0.ap()[tt * 128:(tt + 1) * 128, :],
                                    "px%d" % (tt % 4))

            def pre_ln_stats(tt):
                self.wge(nc.vector, "px%d" % (tt % 4), x0ld[tt])
                ins = nc.vector.reduce_sum(self.ln_acc.ap()[:, tt, 0:1],
                                           xs0.ap()[:, tt % 4], AX.X)
                self.inc(ins, "fv")
                ins = nc.vector.scalar_tensor_tensor(
                    xq0.ap(), xs0.ap()[:, tt % 4], 1.0, xs0.ap()[:, tt % 4],
                    ALU.mult, ALU.mult,
                    accum_out=self.ln_acc.ap()[:, tt, 4:5])
                vfv = self.inc(ins, "fv")
                gX0.release(tt, self.sem["fv"], vfv)
                self.stats_ready[tt] = (self.cnt["dve"], vfv)

            for a in range(c.A):
                par = a % 2
                p4 = a % 3
                if a % 2 == 0 and a // 2 < c.NTT:
                    pre_ln_load(a // 2)
                elif a // 2 < c.NTT:
                    pre_ln_stats(a // 2)
                gW.acquire(nc.sync, a)
                vld = self.dma(nc.sync, wf32.ap()[:, p4],
                               self.W2T.ap()[a * 128:(a + 1) * 128, :], "wl%d" % p4)
                self.wge(nc.tensor, "wl%d" % p4, vld)
                gP.acquire(nc.tensor, a)
                for n in range(nd):
                    ins = nc.tensor.matmul(ps[a % 2].ap()[:, n * dw:(n + 1) * dw],
                                           self.sb_h128r.ap(),
                                           wf32.ap()[:, p4, n * dw:(n + 1) * dw],
                                           start=True, stop=True)
                vmm = self.inc(ins, "mm")
                gW.release(a, self.sem["mm"], vmm)
                self.wge(nc.scalar, "mm", vmm)
                gZ.acquire(nc.scalar, a)
                ins = nc.scalar.copy(zc.ap()[:, a % 4], ps[a % 2].ap())
                vact = self.inc(ins, "act")
                gP.release(a, self.sem["act"], vact)
                self.wge(nc.gpsimd, "act", vact)
                vst = self.dma(nc.gpsimd, self.z_s.ap()[:, a, :], zc.ap()[:, a % 4],
                               "stp%d" % (a % 4))
                gZ.release(a, self.sem["stp%d" % (a % 4)], vst)
            scl = float(1.0 / math.sqrt(c.H))
            stA = [self.cnt["stp%d" % i] for i in range(4)]
            gWf = [RotGuard(2), RotGuard(2)]
            gP2 = [RotGuard(2), RotGuard(2)]
            w2f_view = self.W2fT.ap().rearrange("(a p) d -> a p d", p=128)
            actA = self.cnt["act"]
            gZr = RotGuard(2)
            for bp in range(64):
                par = bp % 2
                p4 = bp % 2
                if bp == 0:
                    for i in range(4):
                        self.wge(nc.sync, "stp%d" % i, stA[i])
                    self.wge(nc.tensor, "act", actA)
                gZr.acquire(nc.sync, bp)
                vld = self.dma(nc.sync, zr.ap()[:, p4],
                               self.z_s.ap()[2 * bp:2 * bp + 2, :, :]
                               .rearrange("b a d -> a b d"), "zl%d" % par)
                self.wge(nc.tensor, "zl%d" % par, vld)
                vmmj = [0, 0]
                for j in range(2):
                    gP2[j].acquire(nc.tensor, bp)
                    for n in range(nd):
                        ins = nc.tensor.matmul(
                            ps[j].ap()[0:c.A, n * dw:(n + 1) * dw], self.sb_hA.ap(),
                            zr.ap()[:, p4, j, n * dw:(n + 1) * dw],
                            start=True, stop=True)
                    vmmj[j] = self.inc(ins, "mm")
                gZr.release(bp, self.sem["mm"], vmmj[1])
                self.wge(nc.vector, "mm", vmmj[0])
                gWf[0].acquire(nc.vector, bp)
                ins = nc.vector.tensor_scalar_mul(wf.ap()[:, par, 0],
                                                  ps[0].ap()[0:c.A, :], scl)
                vd0 = self.inc(ins, "dve")
                gP2[0].release(bp, self.sem["dve"], vd0)
                self.wge(nc.scalar, "mm", vmmj[1])
                gWf[1].acquire(nc.scalar, bp)
                ins = nc.scalar.activation(wf.ap()[:, par, 1], ps[1].ap()[0:c.A, :],
                                           AF.Copy, scale=scl)
                va1 = self.inc(ins, "act")
                gP2[1].release(bp, self.sem["act"], va1)
                self.wge(nc.gpsimd, "dve", vd0)
                self.wge(nc.gpsimd, "act", va1)
                vst = self.dma(nc.gpsimd, w2f_view[:, 2 * bp:2 * bp + 2, :],
                               wf.ap()[:, par], "stz%d" % par)
                gWf[0].release(bp, self.sem["stz%d" % par], vst)
                gWf[1].release(bp, self.sem["stz%d" % par], vst)

    def ln_state(self, ctx, x_in, tok0, tfm, W, pre=False):
        nc, c = self.nc, self.cfg
        b = self

        class LNState:

            def __init__(s):
                s.xs = ctx.enter_context(nc.sbuf_tensor(b.nm("ln_xs"), [128, 2, c.D], F32))
                s.xc = ctx.enter_context(nc.sbuf_tensor(b.nm("ln_xc"), [128, 2, c.D], F32))
                s.xq = ctx.enter_context(nc.sbuf_tensor(b.nm("ln_xq"), [128, c.D], F32))
                s.ttm = ctx.enter_context(nc.sbuf_tensor(b.nm("ln_tt"), [128, 2, c.D], BF16))
                s.st8 = ctx.enter_context(nc.sbuf_tensor(b.nm("ln_st"), [128, 2, 8], F32))
                s.tp = [ctx.enter_context(nc.psum_tensor(b.nm("ln_tp%d" % i), [128, 512], BF16))
                        for i in range(2)]
                s.gX = RotGuard(2)
                s.gT = RotGuard(2)
                s.gP = RotGuard(2)
                s.ttm_done = {}
                s.trans_act = {}
                s.last_stats = -1

            def stats_pre(s, tt):
                par = tt % 2
                gt = tok0 // 128 + tt
                s.gX.acquire(nc.sync, tt)
                r0 = tok0 + tt * 128
                vld = b.dma(nc.sync, s.xs.ap()[:, par], x_in.ap()[r0:r0 + 128, :],
                            "xl%d" % par)
                rdy = b.stats_ready[gt]
                b.wge(nc.vector, "dve", rdy[0])
                b.wge(nc.vector, "fv", rdy[1])
                ins = nc.vector.reduce_sum(s.st8.ap()[:, par, 0:1],
                                           b.ln_acc.ap()[:, gt, 0:4], AX.X)
                b.inc(ins, "fv")
                ins = nc.vector.reduce_sum(s.st8.ap()[:, par, 5:6],
                                           b.ln_acc.ap()[:, gt, 4:8], AX.X)
                b.inc(ins, "fv")
                b.wge(nc.vector, "fv", b.cnt["fv"])
                ins = nc.vector.tensor_scalar_mul(s.st8.ap()[:, par, 1:2],
                                                  s.st8.ap()[:, par, 0:1], 1.0 / c.D)
                b.inc(ins, "fv")
                b.wge(nc.vector, "fv", b.cnt["fv"])
                ins = nc.vector.tensor_mul(s.st8.ap()[:, par, 6:7],
                                           s.st8.ap()[:, par, 0:1],
                                           s.st8.ap()[:, par, 1:2])
                b.inc(ins, "fv")
                b.wge(nc.vector, "fv", b.cnt["fv"])
                ins = nc.vector.tensor_sub(s.st8.ap()[:, par, 2:3],
                                           s.st8.ap()[:, par, 5:6],
                                           s.st8.ap()[:, par, 6:7])
                vfv = b.inc(ins, "fv")
                b.wge(nc.scalar, "fv", vfv)
                ins = nc.scalar.activation(s.st8.ap()[:, par, 3:4],
                                           s.st8.ap()[:, par, 2:3], AF.Ln,
                                           bias=b.sb_eps.ap(), scale=1.0 / c.D)
                b.inc(ins, "fa")
                b.wge(nc.scalar, "fa", b.cnt["fa"])
                ins = nc.scalar.activation(s.st8.ap()[:, par, 4:5],
                                           s.st8.ap()[:, par, 3:4], AF.Exp,
                                           scale=-0.5)
                vact = b.inc(ins, "act")
                b.wge(nc.vector, "act", vact)
                b.wge(nc.vector, "xl%d" % par, vld)
                s.gT.acquire(nc.vector, tt)
                ins = nc.vector.tensor_scalar(s.ttm.ap()[:, par], s.xs.ap()[:, par],
                                              s.st8.ap()[:, par, 1:2],
                                              s.st8.ap()[:, par, 4:5],
                                              ALU.subtract, ALU.mult)
                vdve = b.inc(ins, "dve")
                s.gX.release(tt, b.sem["dve"], vdve)
                s.ttm_done[tt] = vdve

            def stats(s, tt):
                if pre:
                    return s.stats_pre(tt)
                par = tt % 2
                s.gX.acquire(nc.sync, tt)
                r0 = tok0 + tt * 128
                vld = b.dma(nc.sync, s.xs.ap()[:, par], x_in.ap()[r0:r0 + 128, :],
                            "xl%d" % par)
                b.wge(nc.vector, "xl%d" % par, vld)
                ins = nc.vector.reduce_sum(s.st8.ap()[:, par, 0:1], s.xs.ap()[:, par],
                                           AX.X)
                b.inc(ins, "fv")
                b.wge(nc.vector, "fv", b.cnt["fv"])
                ins = nc.vector.tensor_scalar_mul(s.st8.ap()[:, par, 1:2],
                                                  s.st8.ap()[:, par, 0:1], 1.0 / c.D)
                b.inc(ins, "fv")
                b.wge(nc.vector, "fv", b.cnt["fv"])
                ins = nc.vector.tensor_scalar(s.xc.ap()[:, par], s.xs.ap()[:, par],
                                              s.st8.ap()[:, par, 1:2], None,
                                              ALU.subtract)
                b.inc(ins, "fv")
                b.wge(nc.vector, "fv", b.cnt["fv"])
                ins = nc.vector.scalar_tensor_tensor(s.xq.ap(), s.xc.ap()[:, par], 1.0,
                                                     s.xc.ap()[:, par],
                                                     ALU.mult, ALU.mult,
                                                     accum_out=s.st8.ap()[:, par, 2:3])
                vdve = b.inc(ins, "dve")
                s.gX.release(tt, b.sem["dve"], vdve)
                b.wge(nc.scalar, "dve", vdve)
                ins = nc.scalar.activation(s.st8.ap()[:, par, 3:4],
                                           s.st8.ap()[:, par, 2:3], AF.Ln,
                                           bias=b.sb_eps.ap(), scale=1.0 / c.D)
                b.inc(ins, "fa")
                b.wge(nc.scalar, "fa", b.cnt["fa"])
                ins = nc.scalar.activation(s.st8.ap()[:, par, 4:5],
                                           s.st8.ap()[:, par, 3:4], AF.Exp,
                                           scale=-0.5)
                vact = b.inc(ins, "act")
                b.wge(nc.vector, "act", vact)
                s.gT.acquire(nc.vector, tt)
                ins = nc.vector.tensor_scalar(s.ttm.ap()[:, par], s.xc.ap()[:, par],
                                              s.st8.ap()[:, par, 4:5], None,
                                              ALU.mult)
                vdve = b.inc(ins, "dve")
                s.ttm_done[tt] = vdve

            def trans(s, tt):
                par = tt % 2
                b.wge(nc.tensor, "dve", s.ttm_done[tt])
                tfm3 = tfm.ap().rearrange("p (k w) -> p k w", w=W)
                for jb in range(c.NK // 4):
                    ib = tt * (c.NK // 4) + jb
                    s.gP.acquire(nc.tensor, ib)
                    for j in range(4):
                        kc = jb * 4 + j
                        ins = nc.tensor.transpose(
                            s.tp[ib % 2].ap()[:, j * 128:(j + 1) * 128],
                            s.ttm.ap()[:, par, kc * 128:(kc + 1) * 128],
                            b.sb_id.ap())
                    vmm = b.inc(ins, "mmt")
                    b.wge(nc.scalar, "mmt", vmm)
                    ins = nc.scalar.copy(
                        tfm3[:, jb * 4:(jb + 1) * 4, tt * 128:tt * 128 + 128],
                        s.tp[ib % 2].ap().rearrange("p (k w) -> p k w", w=128))
                    va = b.inc(ins, "act")
                    s.gP.release(ib, b.sem["act"], va)
                s.gT.release(tt, b.sem["mmt"], b.cnt["mmt"])
                s.trans_act[tt] = b.cnt["act"]

            def tile(s, tt):
                s.stats(tt)
                if tt > 0:
                    s.trans(tt - 1)
                s.last_stats = tt

            def flush(s):
                s.trans(s.last_stats)
        return LNState()

    def ffn_up(self, x_in, hid, pre=False):
        nc, c = self.nc, self.cfg
        NT = c.NT
        ln_per_t = 512 // 128
        snap = dict(self.cnt)
        with contextlib.ExitStack() as ctx:
            tfm = ctx.enter_context(nc.sbuf_tensor(self.nm("f_tfm"), [128, c.NK * c.TOKS], BF16))
            w1c = ctx.enter_context(nc.sbuf_tensor(self.nm("f_w1c"), [128, 2, c.NK, 128], BF16))
            w3c = ctx.enter_context(nc.sbuf_tensor(self.nm("f_w3c"), [128, 2, c.NK, 128], BF16))
            gsb = ctx.enter_context(nc.sbuf_tensor(self.nm("f_gsb"), [128, 4, 512], F32))
            hsb = ctx.enter_context(nc.sbuf_tensor(self.nm("f_hsb"), [128, 4, 512], BF16))
            self.quiesce_at(snap, nc.sync, nc.gpsimd, nc.tensor)
            lns = self.ln_state(ctx, x_in, 0, tfm, c.TOKS, pre=pre)
            pg = [ctx.enter_context(nc.psum_tensor(self.nm("f_pg%d" % i), [128, 512], F32))
                  for i in range(3)]
            pl = [ctx.enter_context(nc.psum_tensor(self.nm("f_pl%d" % i), [128, 512], F32))
                  for i in range(3)]
            b_mm = self.cnt["mm"]
            b_wl = (self.cnt["wl0"], self.cnt["wl1"])
            gW = RotGuard(2)
            gPG = RotGuard(3)
            gPL = RotGuard(3)
            gGS = RotGuard(4)
            gH = RotGuard(4)
            for ch in range(c.NH):
                bpar = ch % 2
                gW.acquire(nc.gpsimd, ch)
                for wsb, wdr in ((w1c, self.W1T), (w3c, self.W3T)):
                    self.dma(nc.gpsimd, wsb.ap()[:, bpar],
                             wdr.ap()[:, ch * 128:(ch + 1) * 128]
                             .rearrange("(kc p) m -> p kc m", p=128), "wl%d" % bpar)
                gW.release(ch, self.sem["mm"], b_mm + (ch + 1) * 2 * NT)
            act_after_t = {}

            def emit_tiles(tw):
                for tt in range(tw * ln_per_t, (tw + 1) * ln_per_t):
                    lns.tile(tt)

            emit_tiles(0)
            if NT > 1:
                emit_tiles(1)
            for ch in range(c.NH):
                bpar = ch % 2
                self.wge(nc.tensor, "wl%d" % bpar, b_wl[bpar] + 32 * (ch // 2 + 1))
                for t in range(NT):
                    gi = ch * NT + t
                    par = gi % 4
                    pp = gi % 3
                    if ch == 0:
                        self.wge(nc.tensor, "act",
                                 lns.trans_act[(t + 1) * ln_per_t - 1])
                    gPG.acquire(nc.tensor, gi)
                    for k in range(c.NK):
                        ins = nc.tensor.matmul(
                            pg[pp].ap(), w1c.ap()[:, bpar, k],
                            tfm.ap()[:, k * c.TOKS + t * 512: k * c.TOKS + t * 512 + 512],
                            start=(k == 0), stop=(k == c.NK - 1))
                    vmm_g = self.inc(ins, "mm")
                    gPL.acquire(nc.tensor, gi)
                    for k in range(c.NK):
                        ins = nc.tensor.matmul(
                            pl[pp].ap(), w3c.ap()[:, bpar, k],
                            tfm.ap()[:, k * c.TOKS + t * 512: k * c.TOKS + t * 512 + 512],
                            start=(k == 0), stop=(k == c.NK - 1))
                    vmm_l = self.inc(ins, "mm")
                    self.wge(nc.scalar, "mm", vmm_g)
                    gGS.acquire(nc.scalar, gi)
                    ins = nc.scalar.activation(gsb.ap()[:, par], pg[pp].ap(), AF.Silu)
                    vact = self.inc(ins, "act")
                    gPG.release(gi, self.sem["act"], vact)
                    self.wge(nc.vector, "mm", vmm_l)
                    self.wge(nc.vector, "act", vact)
                    gH.acquire(nc.vector, gi)
                    ins = nc.vector.tensor_tensor(hsb.ap()[:, par], pl[pp].ap(),
                                                  gsb.ap()[:, par], ALU.mult)
                    vdve = self.inc(ins, "dve")
                    gPL.release(gi, self.sem["dve"], vdve)
                    gGS.release(gi, self.sem["dve"], vdve)
                    self.wge(nc.sync, "dve", vdve)
                    vst = self.dma(nc.sync, hid.ap()[ch, :, t * 512:(t + 1) * 512],
                                   hsb.ap()[:, par], "sth%d" % par)
                    gH.release(gi, self.sem["sth%d" % par], vst)
                    if ch == 0 and t + 2 < NT:
                        emit_tiles(t + 2)
                        if t + 2 == NT - 1:
                            lns.flush()

    def ffn_down(self, x_in, x_out, hid, collect=False):
        nc, c = self.nc, self.cfg
        NQ, QW, NT = c.NQ, c.QW, c.NT
        kgrp = max(1, c.NH // 8)
        nwq = (c.NH + kgrp - 1) // kgrp
        nspd = (c.NH + kgrp - 1) // kgrp
        NG = NQ * NT * 4
        snap = dict(self.cnt)
        spans = []
        li = 0
        prev_tw = None
        for q in range(NQ):
            order = range(NT) if q % 2 == 0 else range(NT - 1, -1, -1)
            for j, tw in enumerate(order):
                if j == 0 and tw == prev_tw and q > 0:
                    spans.append((q, tw, None))
                else:
                    spans.append((q, tw, li))
                    li += 1
                prev_tw = tw
        nloads = li
        groups = []
        for si, (q, tw, _l) in enumerate(spans):
            for ts in range(4):
                groups.append((q, tw, ts, si))
        span_buf = {}
        bufp = {}
        nxt = 0
        for si, (q, tw, l) in enumerate(spans):
            if l is None:
                span_buf[si] = span_buf[si - 1]
            else:
                span_buf[si] = l % 2
        def mm_after_span(si):
            return (si + 1) * 4
        with contextlib.ExitStack() as ctx:
            w2q = ctx.enter_context(nc.sbuf_tensor(self.nm("d_w2q"), [128, c.NH, QW], BF16))
            hsp = ctx.enter_context(nc.sbuf_tensor(self.nm("d_hsp"), [128, 2, c.NH, 512], BF16))
            xisb = ctx.enter_context(nc.sbuf_tensor(self.nm("d_xi"), [128, 2, QW], F32))
            xosb = ctx.enter_context(nc.sbuf_tensor(self.nm("d_xo"), [128, 2, QW], F32))
            sqs = (ctx.enter_context(nc.sbuf_tensor(self.nm("d_sq"), [128, QW], F32))
                   if collect else None)
            pd = [ctx.enter_context(nc.psum_tensor(self.nm("d_pd%d" % i), [128, QW], F32))
                  for i in range(4)]
            b_mm = self.cnt["mm"]
            b_mq = self.cnt["mq"]
            b_dve = self.cnt["dve"]
            b_w2p = [self.cnt["w2p%d" % i] for i in range(8)]
            b_hl = (self.cnt["hl0"], self.cnt["hl1"])
            b_xl = (self.cnt["xl0"], self.cnt["xl1"])
            b_stx = (self.cnt["stx0"], self.cnt["stx1"])
            sth_tot = [self.cnt["sth%d" % i] for i in range(4)]
            self.quiesce_at(snap, nc.sync)
            for i in range(4):
                self.wge(nc.sync, "sth%d" % i, sth_tot[i])
            g2 = 0
            for si, (q, tw, l) in enumerate(spans):
                if tw == (0 if q % 2 == 0 else NT - 1) or (si > 0 and spans[si - 1][0] != q):
                    pass
                if si == 0 or spans[si - 1][0] != q:
                    for pi, k0 in enumerate(range(0, c.NH, kgrp)):
                        if q >= 1:
                            if pi < nwq - 1:
                                self.wge(nc.sync, "mq",
                                         b_mq + (q - 1) * (nwq - 1) + pi + 1)
                            else:
                                self.wge(nc.sync, "mm", b_mm + q * NT * 4)
                        self.dma(nc.sync, w2q.ap()[:, k0:k0 + kgrp, :],
                                 self.W2fT.ap()[:, q * QW:(q + 1) * QW]
                                 .rearrange("(kc p) m -> p kc m", p=128)[:, k0:k0 + kgrp, :],
                                 "w2p%d" % (pi % 8))
                if l is not None:
                    par = l % 2
                    if l >= 2:
                        prev_si = [s for s, sp in enumerate(spans)
                                   if sp[2] == l - 2][0]
                        last_use = prev_si
                        if prev_si + 1 < len(spans) and spans[prev_si + 1][2] is None:
                            last_use = prev_si + 1
                        self.wge(nc.sync, "mm", b_mm + mm_after_span(last_use))
                    for k0 in range(0, c.NH, kgrp):
                        self.dma(nc.sync, hsp.ap()[:, par, k0:k0 + kgrp, :],
                                 hid.ap()[k0:k0 + kgrp, :, tw * 512:(tw + 1) * 512]
                                 .rearrange("kc p m -> p kc m"), "hl%d" % par)
                for ts in range(4):
                    p2 = g2 % 2
                    if g2 >= 2:
                        self.wge(nc.sync, "dve", b_dve + g2 - 1)
                    r0 = tw * 512 + ts * 128
                    self.dma(nc.sync, xisb.ap()[:, p2],
                             x_in.ap()[r0:r0 + 128, q * QW:(q + 1) * QW],
                             "xl%d" % p2)
                    g2 += 1
            self.quiesce_at(snap, nc.tensor)
            g2 = 0
            hl_seen = [0, 0]
            for si, (q, tw, l) in enumerate(spans):
                first_of_q = (si == 0 or spans[si - 1][0] != q)
                if l is not None:
                    par = l % 2
                    hl_seen[par] += nspd * 16
                    self.wge(nc.tensor, "hl%d" % par, b_hl[par] + hl_seen[par])
                else:
                    par = span_buf[si]
                last_of_q = (si + 1 == len(spans) or spans[si + 1][0] != q)
                for ts in range(4):
                    p4 = g2 % 4
                    if g2 >= 4:
                        self.wge(nc.tensor, "dve", b_dve + g2 - 3)
                    for kc in range(c.NH):
                        if first_of_q and ts == 0 and kc % kgrp == 0:
                            pi = kc // kgrp
                            self.wge(nc.tensor, "w2p%d" % (pi % 8),
                                     b_w2p[pi % 8] + (q + 1) * 16)
                        ins = nc.tensor.matmul(
                            pd[p4].ap(),
                            hsp.ap()[:, par, kc, ts * 128:(ts + 1) * 128],
                            w2q.ap()[:, kc, :],
                            start=(kc == 0), stop=(kc == c.NH - 1))
                        if (last_of_q and ts == 3 and kc % kgrp == kgrp - 1
                                and kc < c.NH - 1):
                            self.inc(ins, "mq")
                    self.inc(ins, "mm")
                    g2 += 1
            def emit_sq(g2s):
                q, tw, ts, _si = groups[g2s]
                gtile = tw * 4 + ts
                p2s = g2s % 2
                ins = nc.vector.scalar_tensor_tensor(
                    sqs.ap(), xosb.ap()[:, p2s], 1.0, xosb.ap()[:, p2s],
                    ALU.mult, ALU.mult,
                    accum_out=self.ln_acc.ap()[:, gtile, 4 + q:5 + q])
                self.inc(ins, "fv")
                self.stats_ready[gtile] = (self.cnt["dve"], self.cnt["fv"])

            for g2 in range(NG):
                p2 = g2 % 2
                q, tw, ts, _si = groups[g2]
                gtile = tw * 4 + ts
                self.wge(nc.vector, "mm", b_mm + g2 + 1)
                self.wge(nc.vector, "xl%d" % p2, b_xl[p2] + (g2 // 2 + 1) * 16)
                if g2 >= 2:
                    self.wge(nc.vector, "stx%d" % p2, b_stx[p2] + 16 * (g2 // 2))
                ins = nc.vector.scalar_tensor_tensor(
                    xosb.ap()[:, p2], pd[g2 % 4].ap(), 0.5, xisb.ap()[:, p2],
                    ALU.mult, ALU.add,
                    accum_out=(self.ln_acc.ap()[:, gtile, q:q + 1]
                               if collect else None))
                self.inc(ins, "dve")
                if collect and g2 >= 1:
                    emit_sq(g2 - 1)
            if collect:
                emit_sq(NG - 1)
            for g2 in range(NG):
                p2 = g2 % 2
                q, tw, ts, _si = groups[g2]
                r0 = tw * 512 + ts * 128
                self.wge(nc.scalar, "dve", b_dve + g2 + 1)
                self.dma(nc.scalar, x_out.ap()[r0:r0 + 128, q * QW:(q + 1) * QW],
                         xosb.ap()[:, p2], "stx%d" % p2)

    def qkv_pass(self, s):
        nc, c = self.nc, self.cfg
        TQ = c.T // 512
        NM = 3 * c.NK
        ln_per_t = 512 // 128
        dsts = [self.Q_s, self.K_s, self.V_s]
        snap = dict(self.cnt)
        with contextlib.ExitStack() as ctx:
            tfm = ctx.enter_context(nc.sbuf_tensor(self.nm("q_tfm"), [128, c.NK * c.T], BF16))
            wqc = ctx.enter_context(nc.sbuf_tensor(self.nm("q_wq"), [128, 2, c.NK, 128], BF16))
            qsb = ctx.enter_context(nc.sbuf_tensor(self.nm("q_qs"), [128, 4, 512], BF16))
            self.quiesce_at(snap, nc.sync, nc.gpsimd, nc.tensor)
            lns = self.ln_state(ctx, self.x1, s * c.T, tfm, c.T, pre=True)
            pq = [ctx.enter_context(nc.psum_tensor(self.nm("q_pq%d" % i), [128, 512], F32))
                  for i in range(3)]
            b_mm = self.cnt["mm"]
            b_wl = (self.cnt["wl0"], self.cnt["wl1"])
            gW = RotGuard(2)
            gP = RotGuard(3)
            gQ = RotGuard(4)
            for m in range(NM):
                bpar = m % 2
                gW.acquire(nc.gpsimd, m)
                self.dma(nc.gpsimd, wqc.ap()[:, bpar],
                         self.Wqkv.ap()[:, m * 128:(m + 1) * 128]
                         .rearrange("(kc p) m -> p kc m", p=128), "wl%d" % bpar)
                gW.release(m, self.sem["mm"], b_mm + (m + 1) * TQ)
            def emit_tiles(tw):
                for tt in range(tw * ln_per_t, (tw + 1) * ln_per_t):
                    lns.tile(tt)

            emit_tiles(0)
            if TQ > 1:
                emit_tiles(1)
            for m in range(NM):
                bpar = m % 2
                kind, h = divmod(m, c.NK)
                tot = (NM + 1) // 2 if bpar == 0 else NM // 2
                self.wge(nc.tensor, "wl%d" % bpar, b_wl[bpar] + 16 * (m // 2 + 1))
                for tq in range(TQ):
                    gi = m * TQ + tq
                    par = gi % 4
                    pp = gi % 3
                    if m == 0:
                        self.wge(nc.tensor, "act",
                                 lns.trans_act[(tq + 1) * ln_per_t - 1])
                    gP.acquire(nc.tensor, gi)
                    for k in range(c.NK):
                        ins = nc.tensor.matmul(
                            pq[pp].ap(), wqc.ap()[:, bpar, k],
                            tfm.ap()[:, k * c.T + tq * 512: k * c.T + tq * 512 + 512],
                            start=(k == 0), stop=(k == c.NK - 1))
                    vmm = self.inc(ins, "mm")
                    self.wge(nc.scalar, "mm", vmm)
                    if gi == 0:
                        self.wge(nc.scalar, "dve", self.bq_ready)
                    gQ.acquire(nc.scalar, gi)
                    ins = nc.scalar.add(qsb.ap()[:, par], pq[pp].ap(),
                                        self.sb_bq.ap()[:, m:m + 1])
                    vact = self.inc(ins, "act")
                    gP.release(gi, self.sem["act"], vact)
                    self.wge(nc.sync, "act", vact)
                    vst = self.dma(nc.sync,
                                   dsts[kind].ap()[s, h % c.NHEADS, :, tq * 512:(tq + 1) * 512],
                                   qsb.ap()[:, par], "stq%d" % par)
                    gQ.release(gi, self.sem["stq%d" % par], vst)
                    if m == 0 and tq + 2 < TQ:
                        emit_tiles(tq + 2)
                        if tq + 2 == TQ - 1:
                            lns.flush()

    def attn_pass(self, s, o_tm, masks, o_guard):
        nc, c = self.nc, self.cfg
        NTC, NQT = c.NTC, c.NQT
        with contextlib.ExitStack() as ctx:
            khb = ctx.enter_context(nc.sbuf_tensor(self.nm("a_kh"), [128, 2, c.T], BF16))
            vhb = ctx.enter_context(nc.sbuf_tensor(self.nm("a_vh"), [128, 2, c.T], BF16))
            qhb = ctx.enter_context(nc.sbuf_tensor(self.nm("a_qh"), [128, 2, c.T], BF16))
            vtm = ctx.enter_context(nc.sbuf_tensor(self.nm("a_vtm"), [128, 2, NTC, 129], BF16))
            pt = ctx.enter_context(nc.sbuf_tensor(self.nm("a_pt"), [128, 2, NTC, 512], BF16))
            rden = ctx.enter_context(nc.sbuf_tensor(self.nm("a_rd"), [128, 3], F32))
            sp = [ctx.enter_context(nc.psum_tensor(self.nm("a_sp%d" % i), [128, 1024], F32))
                  for i in range(2)]
            op = [ctx.enter_context(nc.psum_tensor(self.nm("a_op%d" % i), [128, 129], F32))
                  for i in range(3)]
            vp = [ctx.enter_context(nc.psum_tensor(self.nm("a_vp%d" % i), [128, 512], BF16))
                  for i in range(1)]
            self.quiesce(nc.sync, nc.tensor, nc.vector)
            for par in range(2):
                ins = nc.vector.memset(vtm.ap()[:, par, :, 128:129], 1.0)
                self.inc(ins, "dve")
            ones_dve = self.cnt["dve"]
            stq_tot = [self.cnt["stq%d" % i] for i in range(4)]
            gL = RotGuard(2)
            gVP = RotGuard(1)
            gSP = RotGuard(2)
            gOP = RotGuard(3)
            sc_n = 0
            st = {"iop": 0, "first": True}
            pending = []

            def emit_attv():
                if not pending:
                    return
                hh, hpar, ptp, qt, qs, exp_w, msk_w, vtm_w = pending.pop(0)
                qg = qt * 4 + qs
                iop = st["iop"]
                st["iop"] += 1
                self.wge(nc.tensor, "act", exp_w)
                self.wge(nc.tensor, "dve", max(msk_w, vtm_w))
                gOP.acquire(nc.tensor, iop)
                for kc in range(qg + 1):
                    ins = nc.tensor.matmul(
                        op[iop % 3].ap(),
                        pt.ap()[:, ptp, kc, qs * 128:(qs + 1) * 128],
                        vtm.ap()[:, hpar, kc, :],
                        start=(kc == 0), stop=(kc == qg))
                vmm = self.inc(ins, "mm")
                self.wge(nc.vector, "mm", vmm)
                ins = nc.vector.reciprocal(rden.ap()[:, iop % 3:iop % 3 + 1],
                                           op[iop % 3].ap()[:, 128:129])
                self.inc(ins, "fv")
                self.wge(nc.vector, "fv", self.cnt["fv"])
                if st["first"] and o_guard is not None:
                    self.wge(nc.vector, "mm", o_guard)
                st["first"] = False
                ins = nc.vector.tensor_scalar(
                    o_tm.ap()[:, qg * c.D + hh * 128: qg * c.D + hh * 128 + 128],
                    op[iop % 3].ap()[:, 0:128],
                    rden.ap()[:, iop % 3:iop % 3 + 1], None, ALU.mult)
                vd = self.inc(ins, "dve")
                gOP.release(iop, self.sem["dve"], vd)

            for h in range(c.NHEADS):
                par = h % 2
                if h == 0:
                    for i in range(4):
                        self.wge(nc.sync, "stq%d" % i, stq_tot[i])
                gL.acquire(nc.sync, h)
                for buf, src in ((khb, self.K_s), (vhb, self.V_s), (qhb, self.Q_s)):
                    self.dma(nc.sync, buf.ap()[:, par], src.ap()[s, h, :, :],
                             "atl%d" % par)
                self.wge(nc.tensor, "atl%d" % par, self.cnt["atl%d" % par])
                self.wge(nc.tensor, "dve", ones_dve)
                for jb in range(NTC // 4):
                    ib = h * (NTC // 4) + jb
                    gVP.acquire(nc.tensor, ib)
                    for j in range(4):
                        kc = jb * 4 + j
                        ins = nc.tensor.transpose(
                            vp[0].ap()[:, j * 128:(j + 1) * 128],
                            vhb.ap()[:, par, kc * 128:(kc + 1) * 128],
                            self.sb_id.ap())
                    vmm = self.inc(ins, "mm")
                    self.wge(nc.vector, "mm", vmm)
                    ins = nc.vector.tensor_copy(
                        vtm.ap()[:, par, jb * 4:(jb + 1) * 4, 0:128],
                        vp[0].ap().rearrange("p (k w) -> p k w", w=128))
                    vd = self.inc(ins, "dve")
                    gVP.release(ib, self.sem["dve"], vd)
                    emit_attv()
                vtm_ready = self.cnt["dve"]
                for qt in range(NQT):
                    ptpar = (h * NQT + qt) % 2
                    kmax = 4 * qt + 4
                    exp_last = 0
                    msk_last = 0
                    for kc0 in range(0, kmax, 2):
                        gSP.acquire(nc.tensor, sc_n)
                        for kh in range(2):
                            kc = kc0 + kh
                            ins = nc.tensor.matmul(
                                sp[sc_n % 2].ap()[:, kh * 512:(kh + 1) * 512],
                                khb.ap()[:, par, kc * 128:(kc + 1) * 128],
                                qhb.ap()[:, par, qt * 512:(qt + 1) * 512],
                                start=True, stop=True)
                        vmm = self.inc(ins, "mm")
                        self.wge(nc.scalar, "mm", vmm)
                        ins = nc.scalar.activation(
                            pt.ap()[:, ptpar, kc0:kc0 + 2, :],
                            sp[sc_n % 2].ap(), AF.Exp, scale=c.SCALE)
                        exp_last = self.inc(ins, "act")
                        gSP.release(sc_n, self.sem["act"], exp_last)
                        for kh in range(2):
                            kc = kc0 + kh
                            if kc >= 4 * qt:
                                self.wge(nc.vector, "act", exp_last)
                                ins = nc.vector.tensor_mul(pt.ap()[:, ptpar, kc, :],
                                                           pt.ap()[:, ptpar, kc, :],
                                                           masks.ap()[:, kc - 4 * qt, :])
                                msk_last = self.inc(ins, "dve")
                        sc_n += 1
                        emit_attv()
                    for qs in range(4):
                        pending.append((h, par, ptpar, qt, qs,
                                        exp_last, msk_last, vtm_ready))
                gL.release(h, self.sem["mm"], self.cnt["mm"])
            while pending:
                emit_attv()

    def outproj_pass(self, s, o_tm, bout_bc):
        nc, c = self.nc, self.cfg
        NTC = c.NTC
        NN = c.D // c.QW
        with contextlib.ExitStack() as ctx:
            woc = ctx.enter_context(nc.sbuf_tensor(self.nm("o_wo"), [128, c.NK, c.D], BF16))
            ofm = ctx.enter_context(nc.sbuf_tensor(self.nm("o_ofm"), [128, 2, c.NK, 128], BF16))
            xisb = ctx.enter_context(nc.sbuf_tensor(self.nm("o_xi"), [128, 2, c.D], F32))
            xosb = ctx.enter_context(nc.sbuf_tensor(self.nm("o_xo"), [128, 2, c.D], F32))
            sqs = ctx.enter_context(nc.sbuf_tensor(self.nm("o_sq"), [128, c.QW], F32))
            tp = [ctx.enter_context(nc.psum_tensor(self.nm("o_tp%d" % i), [128, 512], BF16))
                  for i in range(2)]
            pd = [ctx.enter_context(nc.psum_tensor(self.nm("o_pd%d" % i), [128, c.QW], F32))
                  for i in range(2)]
            self.quiesce(nc.gpsimd, nc.sync, nc.tensor, nc.scalar)
            kg = max(1, c.NK // 4)
            b_wo = [self.cnt["wo%d" % i] for i in range(4)]
            for pi, k0 in enumerate(range(0, c.NK, kg)):
                self.dma(nc.gpsimd, woc.ap()[:, k0:k0 + kg, :],
                         self.Wout.ap()
                         .rearrange("(kc p) n -> p kc n", p=128)[:, k0:k0 + kg, :],
                         "wo%d" % (pi % 4))
            gTP = RotGuard(2)
            gPD = RotGuard(2)
            gXI = RotGuard(2)
            gXO = RotGuard(2)
            wo_seen = [False] * 4
            for tc in range(NTC):
                par = tc % 2
                gXI.acquire(nc.sync, tc)
                r0 = s * c.T + tc * 128
                vxi = self.dma(nc.sync, xisb.ap()[:, par], self.x1.ap()[r0:r0 + 128, :],
                               "xl%d" % par)
                for jb in range(c.NK // 4):
                    ib = tc * (c.NK // 4) + jb
                    gTP.acquire(nc.tensor, ib)
                    for j in range(4):
                        kc = jb * 4 + j
                        ins = nc.tensor.transpose(
                            tp[ib % 2].ap()[:, j * 128:(j + 1) * 128],
                            o_tm.ap()[:, tc * c.D + kc * 128: tc * c.D + kc * 128 + 128],
                            self.sb_id.ap())
                    vmm = self.inc(ins, "mm")
                    self.wge(nc.vector, "mm", vmm)
                    ins = nc.vector.tensor_copy(
                        ofm.ap()[:, par, jb * 4:(jb + 1) * 4, :],
                        tp[ib % 2].ap().rearrange("p (k w) -> p k w", w=128))
                    vd = self.inc(ins, "dve")
                    gTP.release(ib, self.sem["dve"], vd)
                self.wge(nc.tensor, "dve", self.cnt["dve"])
                self.wge(nc.vector, "xl%d" % par, vxi)
                gXO.acquire(nc.vector, tc)
                ins = nc.vector.tensor_add(xosb.ap()[:, par], xisb.ap()[:, par],
                                           bout_bc.ap())
                self.inc(ins, "fv")
                self.wge(nc.vector, "fv", self.cnt["fv"])
                for n in range(NN):
                    ip = tc * NN + n
                    gPD.acquire(nc.tensor, ip)
                    for kc in range(c.NK):
                        pi = min(3, kc // kg)
                        if tc == 0 and not wo_seen[pi]:
                            self.wge(nc.tensor, "wo%d" % pi, b_wo[pi] + 16)
                            wo_seen[pi] = True
                        ins = nc.tensor.matmul(
                            pd[ip % 2].ap(), ofm.ap()[:, par, kc],
                            woc.ap()[:, kc, n * c.QW:(n + 1) * c.QW],
                            start=(kc == 0), stop=(kc == c.NK - 1))
                    vmm = self.inc(ins, "mm")
                    self.wge(nc.vector, "mm", vmm)
                    gt = s * (c.T // 128) + tc
                    ins = nc.vector.scalar_tensor_tensor(
                        xosb.ap()[:, par, n * c.QW:(n + 1) * c.QW],
                        pd[ip % 2].ap(), 1.0,
                        xosb.ap()[:, par, n * c.QW:(n + 1) * c.QW], ALU.mult, ALU.add,
                        accum_out=self.ln_acc.ap()[:, gt, n:n + 1])
                    vd = self.inc(ins, "dve")
                    gPD.release(ip, self.sem["dve"], vd)
                    ins = nc.vector.scalar_tensor_tensor(
                        sqs.ap(),
                        xosb.ap()[:, par, n * c.QW:(n + 1) * c.QW], 1.0,
                        xosb.ap()[:, par, n * c.QW:(n + 1) * c.QW],
                        ALU.mult, ALU.mult,
                        accum_out=self.ln_acc.ap()[:, gt, 4 + n:5 + n])
                    self.inc(ins, "fv")
                    self.stats_ready[gt] = (self.cnt["dve"], self.cnt["fv"])
                gXI.release(tc, self.sem["dve"], self.cnt["dve"])
                self.wge(nc.scalar, "dve", self.cnt["dve"])
                self.dma(nc.scalar, self.x2.ap()[r0:r0 + 128, :],
                         xosb.ap()[:, par], "stx%d" % par)
                gXO.release(tc, self.sem["stx%d" % par], self.cnt["stx%d" % par])

    def emit(self):
        nc, c = self.nc, self.cfg
        with contextlib.ExitStack() as topctx:
            self.setup_consts(topctx)
            self.ln_acc = topctx.enter_context(
                nc.sbuf_tensor(self.nm("lnacc"), [128, c.NTT, 8], F32))
            ins = nc.vector.memset(self.ln_acc.ap(), 0.0)
            self.inc(ins, "dve")
            self.stats_ready = {}
            self.w2f_prepass()
            self.ffn_up(self.x0, self.hidA, pre=True)
            self.ffn_down(self.x0, self.x1, self.hidA, collect=True)
            with contextlib.ExitStack() as actx:
                o_tm = actx.enter_context(
                    nc.sbuf_tensor(self.nm("at_otm"), [128, c.NTC * c.D], BF16))
                masks = actx.enter_context(
                    nc.sbuf_tensor(self.nm("at_msk"), [128, 4, 512], BF16))
                bout_bc = actx.enter_context(
                    nc.sbuf_tensor(self.nm("at_bo"), [128, c.D], F32))
                self.quiesce(nc.sync)
                for j in range(4):
                    self.dma(nc.sync, masks.ap()[:, j, :], self.cMask.ap()[j], "bb")
                self.dma(nc.sync, bout_bc.ap(),
                         bass.AP(self.bout.ap().tensor, 0, [[0, 128], [1, c.D]]), "bb")
                bb_done = self.cnt["bb"]
                self.wge(nc.vector, "bb", bb_done)
                self.wge(nc.tensor, "bb", bb_done)
                o_guard = None
                for s in range(c.S):
                    self.qkv_pass(s)
                    self.attn_pass(s, o_tm, masks, o_guard)
                    self.outproj_pass(s, o_tm, bout_bc)
                    o_guard = self.cnt["mm"]
            self.ffn_up(self.x2, self.hidB, pre=True)
            self.ffn_down(self.x2, self.out, self.hidB)
            for name in STORE_SEMS:
                self.wge(nc.sync, name, self.cnt[name])
        return self.nc


def build(cfg: Cfg):
    return Builder(cfg).emit()


def shard_inputs(cfg, L0_attn_x, L0_attn_W_qkv, L0_attn_b_qkv, L0_attn_W_out,
                 L0_attn_b_out, L0_ffn_W1, L0_ffn_W2, L0_ffn_W3, n_cores=N_CORES):
    x = np.ascontiguousarray(np.asarray(L0_attn_x, dtype=np.float32))
    w = {
        "W1T": np.ascontiguousarray(np.asarray(L0_ffn_W1, dtype=np.float32).T),
        "W3T": np.ascontiguousarray(np.asarray(L0_ffn_W3, dtype=np.float32).T),
        "W2T": np.ascontiguousarray(np.asarray(L0_ffn_W2, dtype=np.float32).T),
        "Wqkv": np.ascontiguousarray(np.asarray(L0_attn_W_qkv, dtype=np.float32)),
        "Wout": np.ascontiguousarray(np.asarray(L0_attn_W_out, dtype=np.float32)),
        "bqkv": np.ascontiguousarray(np.asarray(L0_attn_b_qkv, dtype=np.float32)),
        "bout": np.ascontiguousarray(np.asarray(L0_attn_b_out, dtype=np.float32)),
        "H128f": np.ascontiguousarray(_hadamard(128).astype(np.float32)),
    }
    in_maps = []
    for core in range(n_cores):
        xs = np.ascontiguousarray(
            x[cfg.S * core: cfg.S * (core + 1)].reshape(cfg.TOKS, cfg.D))
        m = {"x0": xs}
        m.update(w)
        in_maps.append(m)
    return in_maps


def kernel(L0_attn_x, L0_attn_W_qkv, L0_attn_b_qkv, L0_attn_W_out, L0_attn_b_out,
           L0_ffn_W1, L0_ffn_W2, L0_ffn_W3):
    cfg = Cfg(S=2, T=2048, D=2048, H=8192, NHEADS=16)
    nc = build(cfg)
    in_maps = shard_inputs(cfg, L0_attn_x, L0_attn_W_qkv, L0_attn_b_qkv,
                           L0_attn_W_out, L0_attn_b_out, L0_ffn_W1, L0_ffn_W2,
                           L0_ffn_W3)
    res = run_bass_kernel_spmd(nc, in_maps, core_ids=list(range(N_CORES)))
    out = np.concatenate([np.asarray(r["out"]).reshape(cfg.S, cfg.T, cfg.D)
                          for r in res.results], axis=0)
    return out
